# version 14
# baseline (speedup 1.0000x reference)
import numpy as np
import ml_dtypes
import contextlib
import concourse.bass as bass
import concourse.mybir as mybir
from concourse.bass_utils import run_bass_kernel_spmd

F32 = mybir.dt.float32
BF16 = mybir.dt.bfloat16
AF = mybir.ActivationFunctionType
ALU = mybir.AluOpType
AX = mybir.AxisListType

T, D = 2048, 1024
E = 2048
N = 16
MC = 512
NMC = T // MC
TC = 256
NSC = T // TC
KD = E // 128
KDM = D // 128

TRUNCATE = None
DEBUG_SCRATCH = False
MARKS = {}

ENG_SEM = {"scalar": "s_act", "vector": "s_dve", "tensor": "s_pe", "gpsimd": "s_gp"}


DMA_ROT = {"d_x": 1, "d_w": 3, "d_scr": 3, "d_brep": 4, "d_wr": 2, "d_out": 1}


def dma_sem_names():
    out = []
    for base, n in DMA_ROT.items():
        if n == 1:
            out.append(base)
        else:
            out.extend(f"{base}{i}" for i in range(n))
    return out


class Prog:

    def __init__(self):
        self.ops = {k: [] for k in ("sync", "scalar", "vector", "tensor", "gpsimd")}
        self.cnt = {}
        self.gidx = 0
        self.rrc = {}

    def add(self, eng, fn, waits=(), inc=None, inc_k=1):
        self.gidx += 1
        new = None
        if inc is not None:
            self.cnt[inc] = self.cnt.get(inc, 0) + inc_k
            new = self.cnt[inc]
        if TRUNCATE is not None and self.gidx > TRUNCATE:
            return new
        own = ENG_SEM.get(eng)
        for (sem, val) in waits:
            if val and val > 0 and sem != own:
                self.ops[eng].append(("wait", sem, val))
        self.ops[eng].append(("op", fn, inc, new, inc_k))
        return new

    def dma(self, eng, out, in_, waits=(), inc=None, slow=False):
        def fn(e, out=out, in_=in_, slow=slow):
            if slow:
                return e.dma_start(out=out, in_=in_, allow_slow_non_contiguous=True)
            return e.dma_start(out=out, in_=in_)
        waits = list(waits)
        n = DMA_ROT.get(inc, 1)
        if n > 1:
            idx = self.rrc.get(inc, 0)
            self.rrc[inc] = (idx + 1) % n
            sem = f"{inc}{idx}"
        else:
            sem = inc
        prev = self.c(sem)
        if prev:
            waits.append((sem, prev))
        val = self.add(eng, fn, waits=waits, inc=sem, inc_k=16)
        return (sem, val)

    def c(self, sem):
        return self.cnt.get(sem, 0)

    def cs(self, base):
        n = DMA_ROT.get(base, 1)
        names = [base] if n == 1 else [f"{base}{i}" for i in range(n)]
        return [(s, self.cnt[s]) for s in names if self.cnt.get(s, 0) > 0]

    def mark(self, name):
        MARKS[name] = self.gidx

    def emit(self, block, sems):
        waited = {}
        for eng, items in self.ops.items():
            for it in items:
                if it[0] == "wait":
                    waited.setdefault(it[1], set()).add(it[2])
        rank = {}
        for sem, vals in waited.items():
            if not sem.startswith("d_"):
                rank[sem] = {v: i + 1 for i, v in enumerate(sorted(vals))}

        def mk(eng_name):
            def body(engobj):
                for item in self.ops[eng_name]:
                    if item[0] == "wait":
                        _, sem, val = item
                        if sem in rank:
                            val = rank[sem][val]
                        engobj.wait_ge(sems[sem], val)
                    else:
                        _, fn, inc, val, inc_k = item
                        instr = fn(engobj)
                        if inc is not None:
                            if inc.startswith("d_"):
                                instr.then_inc(sems[inc], inc_k)
                            elif val in rank.get(inc, ()):
                                instr.then_inc(sems[inc], inc_k)
            return body
        block.sync(mk("sync"))
        block.scalar(mk("scalar"))
        block.vector(mk("vector"))
        block.tensor(mk("tensor"))
        block.gpsimd(mk("gpsimd"))


def build_nc():
    nmc = T // MC
    nsc = T // TC
    nc = bass.Bass("TRN2", target_bir_lowering=False, detect_race_conditions=False)

    x_h = nc.dram_tensor("x", [T, D], F32, kind="ExternalInput").ap()
    winT_h = nc.dram_tensor("winT", [D, 2 * E], F32, kind="ExternalInput").ap()
    wdtT_h = nc.dram_tensor("wdtT", [E, E], F32, kind="ExternalInput").ap()
    wbcT_h = nc.dram_tensor("wbcT", [E, 2 * N], F32, kind="ExternalInput").ap()
    woutT_h = nc.dram_tensor("woutT", [E, D], F32, kind="ExternalInput").ap()
    wsT_h = nc.dram_tensor("wsT", [D, D], F32, kind="ExternalInput").ap()
    weT_h = nc.dram_tensor("weT", [D, D], F32, kind="ExternalInput").ap()
    alog_h = nc.dram_tensor("alog", [E, N], F32, kind="ExternalInput").ap()
    bdt_h = nc.dram_tensor("bdt", [E], F32, kind="ExternalInput").ap()
    dvec_h = nc.dram_tensor("dvec", [E], F32, kind="ExternalInput").ap()
    bias_h = nc.dram_tensor("bias", [D], F32, kind="ExternalInput").ap()
    gate_h = nc.dram_tensor("gate", [1, 1], F32, kind="ExternalInput").ap()
    consts_h = nc.dram_tensor("consts", [128, 8], F32, kind="ExternalInput").ap()
    identf_h = nc.dram_tensor("identf", [128, 128], F32, kind="ExternalInput").ap()
    identb_h = nc.dram_tensor("identb", [128, 128], BF16, kind="ExternalInput").ap()

    out_h = nc.dram_tensor("out", [T, D], F32, kind="ExternalOutput").ap()

    knd = "ExternalOutput" if DEBUG_SCRATCH else "Internal"
    xs_sc = nc.dram_tensor("xs_sc", [E, T], F32, kind=knd).ap()
    dt_sc = nc.dram_tensor("dt_sc", [E, T], F32, kind=knd).ap()
    sz_sc = nc.dram_tensor("sz_sc", [E, T], BF16, kind=knd).ap()
    bc_sc = nc.dram_tensor("bc_sc", [2 * N, T], F32, kind=knd).ap()
    sgate_sc = nc.dram_tensor("sgate_sc", [1, 1], F32, kind="Internal").ap()
    lnsc = nc.dram_tensor("lnsc", [128, 8], F32, kind="Internal").ap()
    lnsc2 = nc.dram_tensor("lnsc2", [128, 4], F32, kind="Internal").ap()
    winb_sc = nc.dram_tensor("winb_sc", [2 * KD, 128, KDM * 128], BF16, kind="Internal").ap()
    wdtb_sc = nc.dram_tensor("wdtb_sc", [KD, 128, KD * 128], BF16, kind="Internal").ap()
    woutm_sc = nc.dram_tensor("woutm_sc", [KDM, 128, KD * 128], BF16, kind=knd).ap()
    wcm_sc = nc.dram_tensor("wcm_sc", [KDM, 128, KDM * 128], BF16, kind=knd).ap()

    nc.reset()

    p = Prog()
    sem_names = dma_sem_names() + ["s_act", "s_dve", "s_pe", "s_gp"]

    stack = contextlib.ExitStack()
    sbuf = lambda name, shape, dt: stack.enter_context(nc.sbuf_tensor(name, shape, dt)).ap()

    with stack:
        a_neg = sbuf("a_neg", [128, KD * N], F32)
        bdt_sb = sbuf("bdt_sb", [128, KD], F32)
        d_sb = sbuf("d_sb", [128, KD], F32)
        bias_sb = sbuf("bias_sb", [128, KDM], F32)
        ident_f = sbuf("ident_f", [128, 128], F32)
        ident_b = sbuf("ident_b", [128, 128], BF16)
        g_sb = sbuf("g_sb", [1, 4], F32)
        csts = sbuf("cst", [128, 8], F32)
        ln_tmp = sbuf("ln_tmp", [128, 2], F32)
        ln_rb = sbuf("ln_rb", [128, 8], F32)
        ln_ra = sbuf("ln_ra", [128, 16], F32)
        ln_rb2 = sbuf("ln_rb2", [128, 4], F32)
        ln_ra2 = sbuf("ln_ra2", [128, 8], F32)
        sbc_sb = sbuf("sbc_sb", [128, 1], F32)

        wbc_bf = sbuf("wbc_bf", [128, KD * 2 * N], BF16)
        wstage = sbuf("wstage", [128, KD * 128], F32)
        wbf = [sbuf(f"wbf{i}", [128, KD * 128], BF16) for i in range(2)]
        wout_m = [sbuf(f"woutm{i}", [128, KD * 128], BF16) for i in range(2)]
        wcm_m = [sbuf(f"wcmm{i}", [128, KDM * 128], BF16) for i in range(2)]

        x_sb = sbuf("x_sb", [128, D], F32)
        xc_sb = sbuf("xc_sb", [128, D], F32)
        xhat = sbuf("xhat", [128, D], F32)
        xhatT = sbuf("xhatT", [128, KDM * MC], BF16)
        xsbf = sbuf("xsbf", [128, KD * MC], BF16)
        xs_f = [sbuf(f"xs_f{i}", [128, MC], F32) for i in range(2)]
        dt_f = [sbuf(f"dt_f{i}", [128, MC], F32) for i in range(2)]
        szb = [sbuf(f"szb{i}", [128, MC], BF16) for i in range(2)]
        bc_f = sbuf("bc_f", [32, MC], F32)
        ln_r = sbuf("ln_r", [128, 16], F32)
        ln_r2 = sbuf("ln_r2", [128, 8], F32)

        brep = sbuf("brep", [128, N * TC], F32)
        crep = sbuf("crep", [128, N * TC], F32)
        dt_t = [sbuf(f"dt_t{i}", [128, TC], F32) for i in range(2)]
        xs_t = [sbuf(f"xs_t{i}", [128, TC], F32) for i in range(2)]
        sz_t = [sbuf(f"sz_t{i}", [128, TC], BF16) for i in range(2)]
        dtx_t0 = sbuf("dtx0", [128, TC], F32)
        ab_g = [sbuf(f"ab{i}", [128, 4 * TC], F32) for i in range(2)]
        bb_g2 = sbuf("bb_g2", [128, 4 * TC], F32)
        h_slab = sbuf("h_slab", [128, N * TC], F32)
        z4 = [sbuf(f"z4{i}", [128, 4 * TC], BF16) for i in range(4)]
        y_tot = sbuf("y_tot", [128, TC], F32)
        yg = [sbuf(f"yg{i}", [128, KD * TC], BF16) for i in range(2)]
        state = sbuf("state", [128, KD * N], F32)

        at_f = [sbuf(f"at_f{i}", [128, TC], F32) for i in range(2)]
        x1_sb = sbuf("x1_sb", [128, 2 * D], F32)
        xr_sb = [sbuf(f"xr_sb{i}", [128, D], F32) for i in range(2)]
        xh2 = sbuf("xh2", [128, D], F32)
        xh2d = sbuf("xh2d", [128, D], F32)
        xh2T = sbuf("xh2T", [128, KDM * TC], BF16)
        ff_f = [sbuf(f"ff_f{i}", [128, TC], F32) for i in range(2)]
        x2_sb = sbuf("x2_sb", [128, D], F32)

        bb_d = wstage[:, 0:4 * TC]
        dtx_t = [dtx_t0, wstage[:, 4 * TC:5 * TC]]

        banks = [stack.enter_context(nc.psum_tensor(f"ps{i}", [128, 512], F32)).ap()
                 for i in range(8)]

        sems = {}
        for s in sem_names:
            sems[s] = stack.enter_context(nc.semaphore(s))

        last_use = {}
        def lu(key):
            return list(last_use.get(key, []))

        p.add("vector", lambda e, o=state[:]: e.memset(o, 0.0), inc="s_dve")

        for apx in (ab_g[0][:], ab_g[1][:], xhat[:], xc_sb[:], xh2[:], xh2d[:],
                    xhatT[:], xh2T[:], xs_f[0][:], xs_f[1][:], dt_f[0][:],
                    dt_f[1][:], szb[0][:], szb[1][:], bc_f[:], at_f[0][:],
                    at_f[1][:], ff_f[0][:], ff_f[1][:], a_neg[:], ln_tmp[:],
                    ln_ra[:], ln_ra2[:], g_sb[0:1, 1:4]):
            p.add("scalar", lambda e, o=apx: e.memzero(o), inc=None)
        for apx in (wbf[0], wbf[1], wbc_bf, h_slab, x1_sb, x2_sb, y_tot,
                    ln_r, ln_r2, z4[0], z4[1], z4[2], z4[3]):
            p.add("vector", lambda e, o=apx[:]: e.memset(o, 0.0), inc=None)
        for apx in (yg[0], yg[1], xsbf, dtx_t0, bb_g2):
            p.add("gpsimd", lambda e, o=apx[:]: e.memset(o, 0.0), inc=None)
        cms = None
        for i in range(8):
            cms = p.add("vector", lambda e, o=banks[i][:]: e.memset(o, 0.0),
                        inc="s_dve" if i == 7 else None)
        for i in range(8):
            last_use[("bank", i)] = [("s_dve", cms)]

        p.dma("sync", ident_f[:], identf_h, inc="d_w")
        p.dma("sync", ident_b[:], identb_h, inc="d_w")
        p.dma("sync", csts[:], consts_h, inc="d_w")
        p.dma("sync", bdt_sb[:], bdt_h.rearrange("(k p) -> p k", p=128), inc="d_w", slow=True)
        p.dma("sync", d_sb[:], dvec_h.rearrange("(k p) -> p k", p=128), inc="d_w", slow=True)
        p.dma("sync", bias_sb[:], bias_h.rearrange("(k p) -> p k", p=128), inc="d_w", slow=True)
        p.dma("sync", g_sb[0:1, 0:1], gate_h, inc="d_w")
        c_w0 = p.cs("d_w")
        ca0 = p.dma("sync", wstage[:, 0:KD * N].rearrange("p (k n) -> p k n", k=KD),
                    alog_h.rearrange("(k p) n -> p k n", p=128), inc="d_w")
        ca1 = p.add("scalar", lambda e, o=a_neg[:], i=wstage[:, 0:KD * N]:
                    e.activation(out=o, in_=i, func=AF.Exp),
                    waits=[ca0], inc="s_act")
        last_use["wstage"] = [("s_act", ca1)]
        p.add("vector", lambda e, o=a_neg[:], i=a_neg[:]: e.tensor_scalar_mul(o, i, -1.0),
              waits=[("s_act", ca1)], inc="s_dve")
        cga = p.add("scalar", lambda e, o=g_sb[0:1, 1:2], i=g_sb[0:1, 0:1]:
                    e.activation(out=o, in_=i, func=AF.Sigmoid),
                    waits=c_w0, inc="s_act")
        csg = p.dma("sync", sgate_sc, g_sb[0:1, 1:2], waits=[("s_act", cga)], inc="d_wr")
        c_sbc = p.dma("sync", sbc_sb[:], sgate_sc.broadcast_to([128, 1]),
                      waits=[csg], inc="d_w")

        for k in range(KDM):
            c1 = p.dma("sync", wstage[:, 0:D], wsT_h[k * 128:(k + 1) * 128, :],
                       waits=lu("wstage"), inc="d_w")
            cw = p.dma("sync", wstage[:, D:2 * D], weT_h[k * 128:(k + 1) * 128, :], inc="d_w")
            cv = p.add("vector", lambda e, o=wbf[k % 2][:, 0:D], i0=wstage[:, 0:D],
                       i1=wstage[:, D:2 * D]: e.tensor_tensor(out=o, in0=i0, in1=i1, op=ALU.add),
                       waits=[c1, cw] + lu(("wbf", k % 2)), inc="s_dve")
            last_use["wstage"] = [("s_dve", cv)]
            cd = p.dma("sync", wcm_sc[:, :, k * 128:(k + 1) * 128].rearrange("m p j -> p m j"),
                       wbf[k % 2][:, 0:D].rearrange("p (m j) -> p m j", m=KDM),
                       waits=[("s_dve", cv)], inc="d_wr", slow=True)
            last_use[("wbf", k % 2)] = [cd]
        cw = p.dma("sync", wstage[:, 0:KD * 32].rearrange("p (k m) -> p k m", k=KD),
                   wbcT_h.rearrange("(k p) m -> p k m", p=128),
                   waits=lu("wstage"), inc="d_w")
        cv = p.add("vector", lambda e, o=wbc_bf[:], i=wstage[:, 0:KD * 32]: e.tensor_copy(o, i),
                   waits=[cw], inc="s_dve")
        last_use["wstage"] = [("s_dve", cv)]
        for k in range(KD):
            cw = p.dma("sync", wstage[:, 0:D], woutT_h[k * 128:(k + 1) * 128, :],
                       waits=lu("wstage"), inc="d_w")
            cv = p.add("vector", lambda e, o=wbf[k % 2][:, 0:D], i=wstage[:, 0:D], s=sbc_sb[:, 0:1]:
                       e.tensor_scalar(out=o, in0=i, scalar1=s, scalar2=None, op0=ALU.mult),
                       waits=[cw, c_sbc] + lu(("wbf", k % 2)), inc="s_dve")
            last_use["wstage"] = [("s_dve", cv)]
            cd = p.dma("sync", woutm_sc[:, :, k * 128:(k + 1) * 128].rearrange("m p j -> p m j"),
                       wbf[k % 2][:, 0:D].rearrange("p (m j) -> p m j", m=KDM),
                       waits=[("s_dve", cv)], inc="d_wr", slow=True)
            last_use[("wbf", k % 2)] = [cd]
        c_wm_w = p.cs("d_wr")
        c_wcast_w = [0]

        c_xs_w = {}
        c_dt_w = {}
        c_sz_w = {}
        c_bc_w = {}
        c_yg = {}

        def stage1_units(cp):
            units = []
            t0 = cp * MC
            ctx = {}

            def lnA(tt, t0=t0):
                def f():
                    r0 = t0 + tt * 128
                    sqb = xc_sb if tt % 2 == 0 else xhat
                    sqk = "xc_sb" if tt % 2 == 0 else "xhat"
                    cx = p.dma("sync", x_sb[:], x_h[r0:r0 + 128, :],
                               waits=lu("x_sb"), inc="d_x")
                    p.add("vector", lambda e, o=ln_r[:, 2 * tt:2 * tt + 1], i=x_sb[:]:
                          e.reduce_sum(o, i, axis=AX.X),
                          waits=[cx] + (lu("lnr1") if tt == 0 else []),
                          inc="s_dve")
                    c_sq = p.add("scalar", lambda e, o=sqb[:], i=x_sb[:]:
                                 e.activation(out=o, in_=i, func=AF.Square),
                                 waits=[cx] + lu(sqk), inc="s_act")
                    cr2 = p.add("vector", lambda e, o=ln_r[:, 2 * tt + 1:2 * tt + 2], i=sqb[:]:
                                e.reduce_sum(o, i, axis=AX.X),
                                waits=[("s_act", c_sq)], inc="s_dve")
                    ctx["c_sums"] = cr2
                    last_use[sqk] = [("s_dve", cr2)]
                    last_use["x_sb"] = [("s_dve", cr2), ("s_act", c_sq)]
                return f

            def lnRT():
                def f():
                    cd1 = p.dma("sync", lnsc, ln_r[:, 0:8], waits=[("s_dve", ctx["c_sums"])],
                                inc="d_wr")
                    cd2 = p.dma("sync", ln_rb[:], lnsc,
                                waits=[cd1] + lu("ln_rb"), inc="d_w")
                    ctx["c_lnrt"] = cd2
                    last_use["lnr1"] = [cd1]
                return f

            def lnB(tt):
                def f():
                    if True:
                        w0 = [ctx["c_lnrt"]] if tt == 0 else []
                        p.add("scalar", lambda e, o=ln_ra[:, 4 * tt:4 * tt + 1],
                              i=ln_rb[:, 2 * tt:2 * tt + 1], s=csts[:, 5:6]:
                              e.activation(out=o, in_=i, func=AF.Identity, scale=s),
                              waits=w0, inc=None)
                        p.add("scalar", lambda e, o=ln_tmp[:, 0:1], i=ln_ra[:, 4 * tt:4 * tt + 1]:
                              e.activation(out=o, in_=i, func=AF.Square), inc=None)
                        p.add("scalar", lambda e, o=ln_tmp[:, 1:2], i=ln_tmp[:, 0:1],
                              s=csts[:, 3:4], bi=csts[:, 0:1]:
                              e.activation(out=o, in_=i, func=AF.Identity, scale=s, bias=bi), inc=None)
                        p.add("scalar", lambda e, o=ln_tmp[:, 0:1], i=ln_rb[:, 2 * tt + 1:2 * tt + 2],
                              s=csts[:, 2:3], bi=ln_tmp[:, 1:2]:
                              e.activation(out=o, in_=i, func=AF.Ln, scale=s, bias=bi), inc=None)
                        p.add("scalar", lambda e, o=ln_ra[:, 4 * tt + 2:4 * tt + 3], i=ln_tmp[:, 0:1],
                              s=csts[:, 1:2]:
                              e.activation(out=o, in_=i, func=AF.Exp, scale=s), inc=None)
                        ca = p.add("scalar", lambda e, o=ln_ra[:, 4 * tt + 3:4 * tt + 4],
                                   i=ln_ra[:, 4 * tt + 2:4 * tt + 3], s=ln_ra[:, 4 * tt:4 * tt + 1]:
                                   e.activation(out=o, in_=i, func=AF.Identity, scale=s),
                                   inc="s_act" if tt == 3 else None)
                    if tt == 3:
                        last_use["ln_rb"] = [("s_act", ca)]
                        last_use["xc_sb"] = [("s_act", ca)]
                        last_use["x_sb"] = [("s_act", ca)]
                return f

            def lnC(tt, t0=t0):
                def f():
                    r0 = t0 + tt * 128
                    cx = p.dma("sync", x_sb[:], x_h[r0:r0 + 128, :],
                               waits=lu("x_sb"), inc="d_x")
                    c_ph = p.add("scalar", lambda e, o=xhat[:], i=x_sb[:],
                                 s=ln_ra[:, 4 * tt + 2:4 * tt + 3],
                                 bi=ln_ra[:, 4 * tt + 3:4 * tt + 4]:
                                 e.activation(out=o, in_=i, func=AF.Identity, scale=s, bias=bi),
                                 waits=[cx] + lu("xhat"),
                                 inc="s_act")
                    last_use["x_sb"] = [("s_act", c_ph)]
                    xh_done = []
                    for b in range(2):
                        bk = banks[2 + b]
                        for q in range(4):
                            dk = b * 4 + q
                            cpe = p.add("tensor", lambda e, o=bk[:, q * 128:(q + 1) * 128],
                                        i=xhat[:, dk * 128:(dk + 1) * 128], idn=ident_f[:]:
                                        e.transpose(o, i, idn),
                                        waits=([("s_act", c_ph)] + lu(("bank", 2 + b))) if q == 0 else [],
                                        inc="s_pe" if q == 3 else None)
                        xh_done.append(("s_pe", cpe))
                        ca2 = p.add("scalar", lambda e,
                                    o=xhatT[:].rearrange("p (dk t) -> p dk t", dk=KDM)[:, b * 4:(b + 1) * 4, tt * 128:(tt + 1) * 128],
                                    i=bk[:]:
                                    e.activation(out=o, in_=i, func=AF.Copy),
                                    waits=[("s_pe", cpe)], inc="s_act")
                        last_use[("bank", 2 + b)] = [("s_act", ca2)]
                    last_use["xhat"] = xh_done
                    if tt == 3:
                        ctx["c_xhatT"] = p.c("s_act")
                return f

            def xz(eo, t0=t0, cp=cp):
                def f():
                    wb = wbf[eo % 2]
                    if cp == 0:
                        cw = p.dma("sync", wstage[:, 0:KDM * 128].rearrange("p (k m) -> p k m", k=KDM),
                                   winT_h.rearrange("(k p) m -> p k m", p=128)[:, :, eo * 128:(eo + 1) * 128],
                                   waits=lu("wstage"), inc="d_w")
                        cv = p.add("vector", lambda e, o=wb[:, 0:KDM * 128], i=wstage[:, 0:KDM * 128]:
                                   e.tensor_copy(o, i),
                                   waits=[cw] + lu(("wbf", eo % 2)), inc="s_dve")
                        last_use["wstage"] = [("s_dve", cv)]
                        p.dma("sync", winb_sc[eo], wb[:, 0:KDM * 128], waits=[("s_dve", cv)], inc="d_wr")
                        rdy = [("s_dve", cv)]
                    else:
                        cw = p.dma("sync", wb[:, 0:KDM * 128], winb_sc[eo],
                                   waits=c_wcast_w[0] + lu(("wbf", eo % 2)), inc="d_w")
                        rdy = [cw]
                    bi = eo % 2
                    bank = banks[bi]
                    cpe = None
                    for k in range(KDM):
                        w0 = (rdy + [("s_act", ctx["c_xhatT"])] + lu(("bank", bi))) if k == 0 else []
                        cpe = p.add("tensor", lambda e, o=bank[:], w=wb[:, k * 128:(k + 1) * 128],
                                    r=xhatT[:, k * MC:(k + 1) * MC], s=(k == 0), st=(k == KDM - 1):
                                    e.matmul(o, w, r, start=s, stop=st),
                                    waits=w0, inc="s_pe" if k == KDM - 1 else None)
                    last_use[("wbf", eo % 2)] = [("s_pe", cpe)]
                    if eo < KD:
                        fb = xs_f[eo % 2]
                        ca = p.add("scalar", lambda e, o=fb[:], i=bank[:]:
                                   e.activation(out=o, in_=i, func=AF.Copy),
                                   waits=[("s_pe", cpe)] + lu(("xs_f", eo % 2)), inc="s_act")
                        cv = p.add("gpsimd", lambda e, o=xsbf[:, eo * MC:(eo + 1) * MC], i=fb[:]:
                                   e.tensor_copy(o, i),
                                   waits=[("s_act", ca)] + (lu("xsbf") if eo == 0 else []), inc="s_gp")
                        cd = p.dma("sync", xs_sc[eo * 128:(eo + 1) * 128, t0:t0 + MC], fb[:],
                                   waits=[("s_act", ca)], inc="d_wr")
                        c_xs_w[(eo, cp)] = cd
                        last_use[("xs_f", eo % 2)] = [("s_gp", cv), cd]
                        last_use[("bank", bi)] = [("s_act", ca)]
                        if eo == KD - 1:
                            ctx["c_xsbf_part"] = cv
                    else:
                        zb = szb[eo % 2]
                        ca = p.add("scalar", lambda e, o=zb[:], i=bank[:]:
                                   e.activation(out=o, in_=i, func=AF.Silu),
                                   waits=[("s_pe", cpe)] + lu(("szb", eo % 2)), inc="s_act")
                        cd = p.dma("sync", sz_sc[(eo - KD) * 128:(eo - KD + 1) * 128, t0:t0 + MC], zb[:],
                                   waits=[("s_act", ca)], inc="d_wr")
                        c_sz_w[(eo - KD, cp)] = cd
                        last_use[("szb", eo % 2)] = [cd]
                        last_use[("bank", bi)] = [("s_act", ca)]
                        if eo == 2 * KD - 1:
                            ctx["c_xsbf"] = ctx["c_xsbf_part"]
                return f

            def dtm(eo, t0=t0, cp=cp):
                def f():
                    wb = wbf[eo % 2]
                    if cp == 0:
                        cw = p.dma("sync", wstage[:].rearrange("p (k m) -> p k m", k=KD),
                                   wdtT_h.rearrange("(k p) m -> p k m", p=128)[:, :, eo * 128:(eo + 1) * 128],
                                   waits=lu("wstage"), inc="d_w")
                        cv = p.add("vector", lambda e, o=wb[:], i=wstage[:]: e.tensor_copy(o, i),
                                   waits=[cw] + lu(("wbf", eo % 2)), inc="s_dve")
                        last_use["wstage"] = [("s_dve", cv)]
                        p.dma("sync", wdtb_sc[eo], wb[:], waits=[("s_dve", cv)], inc="d_wr")
                        rdy = [("s_dve", cv)]
                    else:
                        cw = p.dma("sync", wb[:], wdtb_sc[eo],
                                   waits=c_wcast_w[0] + lu(("wbf", eo % 2)), inc="d_w")
                        rdy = [cw]
                    bi = eo % 2
                    bank = banks[bi]
                    cpe = None
                    for k in range(KD):
                        w0 = (rdy + [("s_gp", ctx["c_xsbf"])] + lu(("bank", bi))) if k == 0 else []
                        cpe = p.add("tensor", lambda e, o=bank[:], w=wb[:, k * 128:(k + 1) * 128],
                                    r=xsbf[:, k * MC:(k + 1) * MC], s=(k == 0), st=(k == KD - 1):
                                    e.matmul(o, w, r, start=s, stop=st),
                                    waits=w0, inc="s_pe" if k == KD - 1 else None)
                    last_use[("wbf", eo % 2)] = [("s_pe", cpe)]
                    fb = dt_f[eo % 2]
                    p.add("scalar", lambda e, o=xc_sb[:, 0:MC], i=bank[:], bi2=bdt_sb[:, eo:eo + 1]:
                          e.activation(out=o, in_=i, func=AF.Exp, bias=bi2),
                          waits=[("s_pe", cpe)] + lu("xc_sb"), inc="s_act")
                    ca = p.add("scalar", lambda e, o=fb[:], i=xc_sb[:, 0:MC], bi2=csts[:, 4:5]:
                               e.activation(out=o, in_=i, func=AF.Ln, bias=bi2),
                               waits=lu(("dt_f", eo % 2)), inc="s_act")
                    last_use["xc_sb"] = [("s_act", ca)]
                    cd = p.dma("sync", dt_sc[eo * 128:(eo + 1) * 128, t0:t0 + MC], fb[:],
                               waits=[("s_act", ca)], inc="d_wr")
                    c_dt_w[(eo, cp)] = cd
                    last_use[("dt_f", eo % 2)] = [cd]
                    last_use[("bank", bi)] = [("s_act", ca)]
                    if cp == 0 and eo == KD - 1:
                        c_wcast_w[0] = p.cs("d_wr")
                return f

            def bcm(t0=t0, cp=cp):
                def f():
                    bank = banks[0]
                    cpe = None
                    for k in range(KD):
                        w0 = lu(("bank", 0)) if k == 0 else []
                        cpe = p.add("tensor", lambda e, o=bank[0:32, :], w=wbc_bf[:, k * 32:(k + 1) * 32],
                                    r=xsbf[:, k * MC:(k + 1) * MC], s=(k == 0), st=(k == KD - 1):
                                    e.matmul(o, w, r, start=s, stop=st),
                                    waits=w0, inc="s_pe" if k == KD - 1 else None)
                    last_use["xsbf"] = [("s_pe", cpe)]
                    ca = p.add("scalar", lambda e, o=bc_f[:], i=bank[0:32, :]:
                               e.activation(out=o, in_=i, func=AF.Copy),
                               waits=[("s_pe", cpe)] + lu("bc_f"), inc="s_act")
                    cd = p.dma("sync", bc_sc[:, t0:t0 + MC], bc_f[:], waits=[("s_act", ca)], inc="d_wr")
                    last_use["bc_f"] = [cd]
                    last_use[("bank", 0)] = [("s_act", ca)]
                    c_bc_w[cp] = cd
                return f

            for tt in range(4):
                units.append(lnA(tt))
            units.append(lnRT())
            for tt in range(4):
                units.append(lnB(tt))
            for tt in range(4):
                units.append(lnC(tt))
            for eo in range(2 * KD):
                units.append(xz(eo))
            for eo in range(KD):
                units.append(dtm(eo))
            units.append(bcm())
            return units

        def post_units(sc):
            units = []
            s0 = sc * TC
            ctx = {}

            def attn_m(m, sc=sc, s0=s0):
                def f():
                    wm = wout_m[m % 2]
                    cw = p.dma("sync", wm[:], woutm_sc[m],
                               waits=c_wm_w + lu(("wout_m", m % 2)), inc="d_w")
                    bi = 4 + (m % 2)
                    pa = banks[bi][:, 0:TC]
                    cpm = None
                    for k in range(KD):
                        w0 = ([cw, ("s_gp", c_yg[sc])] + lu(("bank", bi))) if k == 0 else []
                        cpm = p.add("tensor", lambda e, o=pa, w=wm[:, k * 128:(k + 1) * 128],
                                    r=yg[sc % 2][:, k * TC:(k + 1) * TC], s=(k == 0), st=(k == KD - 1):
                                    e.matmul(o, w, r, start=s, stop=st),
                                    waits=w0, inc="s_pe" if k == KD - 1 else None)
                    last_use[("wout_m", m % 2)] = [("s_pe", cpm)]
                    ca = p.add("scalar", lambda e, o=at_f[m % 2][:], i=pa:
                               e.activation(out=o, in_=i, func=AF.Copy),
                               waits=[("s_pe", cpm)] + lu(("at_f", m % 2)), inc="s_act")
                    last_use[("bank", bi)] = [("s_act", ca)]
                    ctx[("at_evac", m)] = ca
                    if m == KDM - 1:
                        last_use[("yg", sc % 2)] = [("s_pe", cpm)]
                return f

            def xr_load(sc=sc, s0=s0):
                def f():
                    ctx["cx0"] = p.dma("sync", xr_sb[0][:], x_h[s0:s0 + 128, :],
                                       waits=lu(("xr", 0)), inc="d_x")
                    ctx["cx1"] = p.dma("sync", xr_sb[1][:], x_h[s0 + 128:s0 + 256, :],
                                       waits=lu(("xr", 1)), inc="d_x")
                return f

            def x1_pair(pr, sc=sc):
                def f():
                    tb = 2 + pr % 2
                    cpe = None
                    for j in range(2):
                        m = 2 * pr + j
                        for tt in range(2):
                            q = j * 2 + tt
                            cpe = p.add("tensor", lambda e, o=banks[tb][:, q * 128:(q + 1) * 128],
                                        i=at_f[m % 2][:, tt * 128:(tt + 1) * 128], idn=ident_f[:]:
                                        e.transpose(o, i, idn),
                                        waits=([("s_act", ctx[("at_evac", 2 * pr + 1)])] +
                                               lu(("bank", tb))) if q == 0 else [],
                                        inc="s_pe" if (j == 1 and tt == 1) else None)
                    cv = None
                    for j in range(2):
                        m = 2 * pr + j
                        for tt in range(2):
                            q = j * 2 + tt
                            cv = p.add("vector",
                                       lambda e, o=x1_sb[:, tt * D + m * 128:tt * D + (m + 1) * 128],
                                       i0=banks[tb][:, q * 128:(q + 1) * 128],
                                       i1=xr_sb[tt][:, m * 128:(m + 1) * 128]:
                                       e.tensor_tensor(out=o, in0=i0, in1=i1, op=ALU.add),
                                       waits=(([("s_pe", cpe), ctx["cx0"], ctx["cx1"]] +
                                               (lu("x1_sb") if pr == 0 else [])) if q == 0 else []),
                                       inc="s_dve" if (j == 1 and tt == 1) else None)
                    last_use[("bank", tb)] = [("s_dve", cv)]
                    last_use[("at_f", 0)] = [("s_pe", cpe)]
                    last_use[("at_f", 1)] = [("s_pe", cpe)]
                    ctx.setdefault("x1_done", []).append(cv)
                    if pr == 3:
                        ctx["c_x1"] = max(ctx["x1_done"])
                        last_use[("xr", 0)] = [("s_dve", ctx["c_x1"])]
                        last_use[("xr", 1)] = [("s_dve", ctx["c_x1"])]
                return f

            def ln2A():
                def f():
                    for tt in range(2):
                        sqb = xh2d if tt == 0 else xh2
                        sqk = "xh2d" if tt == 0 else "xh2"
                        p.add("vector", lambda e, o=ln_r2[:, 2 * tt:2 * tt + 1],
                              i=x1_sb[:, tt * D:(tt + 1) * D]:
                              e.reduce_sum(o, i, axis=AX.X),
                              waits=([("s_dve", ctx["c_x1"])] if tt == 0 else []) +
                                    (lu("lnr2") if tt == 0 else []),
                              inc="s_dve")
                        csq2 = p.add("scalar", lambda e, o=sqb[:], i=x1_sb[:, tt * D:(tt + 1) * D]:
                                     e.activation(out=o, in_=i, func=AF.Square),
                                     waits=([("s_dve", ctx["c_x1"])] if tt == 0 else []) + lu(sqk),
                                     inc="s_act")
                        cr2 = p.add("vector", lambda e, o=ln_r2[:, 2 * tt + 1:2 * tt + 2], i=sqb[:]:
                                    e.reduce_sum(o, i, axis=AX.X),
                                    waits=[("s_act", csq2)], inc="s_dve")
                        ctx["c_sums2"] = cr2
                        last_use[sqk] = [("s_dve", cr2)]
                return f

            def ln2RT():
                def f():
                    cd1 = p.dma("sync", lnsc2, ln_r2[:, 0:4], waits=[("s_dve", ctx["c_sums2"])],
                                inc="d_wr")
                    cd2 = p.dma("sync", ln_rb2[:], lnsc2,
                                waits=[cd1] + lu("ln_rb2"), inc="d_w")
                    ctx["c_lnrt2"] = cd2
                    last_use["lnr2"] = [cd1]
                return f

            def ln2BC(tt):
                def f():
                    if True:
                        w0 = [ctx["c_lnrt2"]] if tt == 0 else []
                        p.add("scalar", lambda e, o=ln_ra2[:, 4 * tt:4 * tt + 1],
                              i=ln_rb2[:, 2 * tt:2 * tt + 1], s=csts[:, 5:6]:
                              e.activation(out=o, in_=i, func=AF.Identity, scale=s),
                              waits=w0, inc=None)
                        p.add("scalar", lambda e, o=ln_tmp[:, 0:1], i=ln_ra2[:, 4 * tt:4 * tt + 1]:
                              e.activation(out=o, in_=i, func=AF.Square), inc=None)
                        p.add("scalar", lambda e, o=ln_tmp[:, 1:2], i=ln_tmp[:, 0:1],
                              s=csts[:, 3:4], bi=csts[:, 0:1]:
                              e.activation(out=o, in_=i, func=AF.Identity, scale=s, bias=bi), inc=None)
                        p.add("scalar", lambda e, o=ln_tmp[:, 0:1], i=ln_rb2[:, 2 * tt + 1:2 * tt + 2],
                              s=csts[:, 2:3], bi=ln_tmp[:, 1:2]:
                              e.activation(out=o, in_=i, func=AF.Ln, scale=s, bias=bi), inc=None)
                        p.add("scalar", lambda e, o=ln_ra2[:, 4 * tt + 2:4 * tt + 3], i=ln_tmp[:, 0:1],
                              s=csts[:, 1:2]:
                              e.activation(out=o, in_=i, func=AF.Exp, scale=s), inc=None)
                        ca = p.add("scalar", lambda e, o=ln_ra2[:, 4 * tt + 3:4 * tt + 4],
                                   i=ln_ra2[:, 4 * tt + 2:4 * tt + 3], s=ln_ra2[:, 4 * tt:4 * tt + 1]:
                                   e.activation(out=o, in_=i, func=AF.Identity, scale=s),
                                   inc="s_act" if tt == 1 else None)
                    if tt == 1:
                        last_use["ln_rb2"] = [("s_act", ca)]
                        last_use["xh2d"] = [("s_act", ca)]
                return f

            def ln2T(tt):
                def f():
                    c_ph = p.add("scalar", lambda e, o=xh2[:], i=x1_sb[:, tt * D:(tt + 1) * D],
                                 s=ln_ra2[:, 4 * tt + 2:4 * tt + 3],
                                 bi=ln_ra2[:, 4 * tt + 3:4 * tt + 4]:
                                 e.activation(out=o, in_=i, func=AF.Identity, scale=s, bias=bi),
                                 waits=lu("xh2"), inc="s_act")
                    tr2 = []
                    for b in range(2):
                        bk = banks[2 + b]
                        for q in range(4):
                            dk = b * 4 + q
                            cpe = p.add("tensor", lambda e, o=bk[:, q * 128:(q + 1) * 128],
                                        i=xh2[:, dk * 128:(dk + 1) * 128], idn=ident_f[:]:
                                        e.transpose(o, i, idn),
                                        waits=([("s_act", c_ph)] + lu(("bank", 2 + b))) if q == 0 else [],
                                        inc="s_pe" if q == 3 else None)
                        tr2.append(("s_pe", cpe))
                        ca2 = p.add("scalar", lambda e,
                                    o=xh2T[:].rearrange("p (dk t) -> p dk t", dk=KDM)[:, b * 4:(b + 1) * 4, tt * 128:(tt + 1) * 128],
                                    i=bk[:]:
                                    e.activation(out=o, in_=i, func=AF.Copy),
                                    waits=[("s_pe", cpe)], inc="s_act")
                        last_use[("bank", 2 + b)] = [("s_act", ca2)]
                    last_use["xh2"] = tr2
                    if tt == 1:
                        ctx["c_xh2T"] = p.c("s_act")
                return f

            def ffn_m(m):
                def f():
                    wm = wcm_m[m % 2]
                    cw = p.dma("sync", wm[:], wcm_sc[m],
                               waits=c_wm_w + lu(("wcm_m", m % 2)), inc="d_w")
                    bi = 4 + (m % 2)
                    pf = banks[bi][:, 0:TC]
                    cpm = None
                    for k in range(KDM):
                        w0 = ([cw, ("s_act", ctx["c_xh2T"])] + lu(("bank", bi))) if k == 0 else []
                        cpm = p.add("tensor", lambda e, o=pf, w=wm[:, k * 128:(k + 1) * 128],
                                    r=xh2T[:, k * TC:(k + 1) * TC], s=(k == 0), st=(k == KDM - 1):
                                    e.matmul(o, w, r, start=s, stop=st),
                                    waits=w0, inc="s_pe" if k == KDM - 1 else None)
                    last_use[("wcm_m", m % 2)] = [("s_pe", cpm)]
                    ca = p.add("scalar", lambda e, o=ff_f[m % 2][:], i=pf, bi2=bias_sb[:, m:m + 1]:
                               e.activation(out=o, in_=i, func=AF.Identity, bias=bi2),
                               waits=[("s_pe", cpm)] + lu(("ff_f", m % 2)), inc="s_act")
                    last_use[("bank", bi)] = [("s_act", ca)]
                    ctx[("ff_evac", m)] = ca
                return f

            def x2_pair(pr, sc=sc, s0=s0):
                def f():
                    tb = 2 + pr % 2
                    cpe = None
                    for j in range(2):
                        m = 2 * pr + j
                        for tt in range(2):
                            q = j * 2 + tt
                            cpe = p.add("tensor", lambda e, o=banks[tb][:, q * 128:(q + 1) * 128],
                                        i=ff_f[m % 2][:, tt * 128:(tt + 1) * 128], idn=ident_f[:]:
                                        e.transpose(o, i, idn),
                                        waits=([("s_act", ctx[("ff_evac", 2 * pr + 1)])] +
                                               lu(("bank", tb))) if q == 0 else [],
                                        inc="s_pe" if (j == 1 and tt == 1) else None)
                    cv = None
                    for j in range(2):
                        m = 2 * pr + j
                        for tt in range(2):
                            q = j * 2 + tt
                            dst = (x2_sb if tt == 0 else xh2d)[:, m * 128:(m + 1) * 128]
                            cv = p.add("vector",
                                       lambda e, o=dst, i0=banks[tb][:, q * 128:(q + 1) * 128],
                                       i1=x1_sb[:, tt * D + m * 128:tt * D + (m + 1) * 128]:
                                       e.tensor_tensor(out=o, in0=i0, in1=i1, op=ALU.add),
                                       waits=(([("s_pe", cpe)] +
                                               (lu("x2_sb") + lu("xh2d") if pr == 0 else []))
                                              if q == 0 else []),
                                       inc="s_dve" if (j == 1 and tt == 1) else None)
                    last_use[("bank", tb)] = [("s_dve", cv)]
                    last_use[("ff_f", 0)] = [("s_pe", cpe)]
                    last_use[("ff_f", 1)] = [("s_pe", cpe)]
                    ctx.setdefault("x2_done", []).append(cv)
                    if pr == 3:
                        cdone = max(ctx["x2_done"])
                        for tt in range(2):
                            buf = x2_sb if tt == 0 else xh2d
                            p.dma("sync", out_h[s0 + tt * 128:s0 + (tt + 1) * 128, :], buf[:],
                                  waits=[("s_dve", cdone)], inc="d_out")
                        last_use["x2_sb"] = [("d_out", p.c("d_out"))]
                        last_use["xh2d"] = [("d_out", p.c("d_out"))]
                        last_use["x1_sb"] = [("s_dve", cdone)]
                return f

            units.append(xr_load())
            for m in range(KDM):
                units.append(attn_m(m))
                if m % 2 == 1:
                    units.append(x1_pair(m // 2))
            units.append(ln2A())
            units.append(ln2RT())
            for tt in range(2):
                units.append(ln2BC(tt))
            for tt in range(2):
                units.append(ln2T(tt))
            for m in range(KDM):
                units.append(ffn_m(m))
                if m % 2 == 1:
                    units.append(x2_pair(m // 2))
            return units

        def scan_chunk(sc, bg):
            cp = sc // 2
            s0 = sc * TC
            HN = N // 2
            cbp = []
            ccp = []
            for h in range(2):
                cbp.append(p.dma("gpsimd",
                                 brep[:, h * HN * TC:(h + 1) * HN * TC].rearrange("p (n t) -> p n t", n=HN),
                                 bc_sc[h * HN:(h + 1) * HN, s0:s0 + TC][None, :, :].broadcast_to([128, HN, TC]),
                                 waits=[c_bc_w[cp]] + lu("brep"), inc="d_brep"))
            for h in range(2):
                ccp.append(p.dma("gpsimd",
                                 crep[:, h * HN * TC:(h + 1) * HN * TC].rearrange("p (n t) -> p n t", n=HN),
                                 bc_sc[N + h * HN:N + (h + 1) * HN, s0:s0 + TC][None, :, :].broadcast_to([128, HN, TC]),
                                 waits=lu("crep"), inc="d_brep"))

            pend = [None]
            loaded = {}

            def load_k(k):
                kb = k % 2
                cdt = p.dma("gpsimd", dt_t[kb][:], dt_sc[k * 128:(k + 1) * 128, s0:s0 + TC],
                            waits=[c_dt_w[(k, cp)]] + lu(("dt_t", kb)), inc="d_scr")
                cxs = p.dma("gpsimd", xs_t[kb][:], xs_sc[k * 128:(k + 1) * 128, s0:s0 + TC],
                            waits=[c_xs_w[(k, cp)]] + lu(("xs_t", kb)), inc="d_scr")
                csz = p.dma("gpsimd", sz_t[kb][:], sz_sc[k * 128:(k + 1) * 128, s0:s0 + TC],
                            waits=[c_sz_w[(k, cp)]] + lu(("sz_t", kb)), inc="d_scr")
                loaded[k] = (cdt, cxs, csz)

            def emit_yevac(rec):
                kk, kkb, cstop, csz_ = rec
                ybank = banks[6] if kk % 2 == 0 else banks[7]
                ykey = ("bank", 6) if kk % 2 == 0 else ("bank", 7)
                cyt = p.add("vector", lambda e, o=y_tot[:], i0=xs_t[kkb][:],
                            s=d_sb[:, kk:kk + 1], i1=ybank[:, 0:TC]:
                            e.scalar_tensor_tensor(out=o, in0=i0, scalar=s, in1=i1,
                                                   op0=ALU.mult, op1=ALU.add),
                            waits=[("s_pe", cstop), csz_],
                            inc="s_dve")
                last_use[ykey] = [("s_dve", cyt)]
                last_use[("xs_t", kkb)] = [("s_dve", cyt)]
                cyg = p.add("gpsimd", lambda e, o=yg[sc % 2][:, kk * TC:(kk + 1) * TC],
                            i0=y_tot[:], i1=sz_t[kkb][:]:
                            e.tensor_tensor(out=o, in0=i0, in1=i1, op=ALU.mult),
                            waits=[("s_dve", cyt)] +
                                  (lu(("yg", sc % 2)) if kk == 0 else []), inc="s_gp")
                last_use[("sz_t", kkb)] = [("s_gp", cyg)]

            load_k(0)
            load_k(1)
            for k in range(KD):
                kb = k % 2
                cdt, cxs, csz = loaded.pop(k)
                cdx = p.add("gpsimd", lambda e, o=dtx_t[kb][:], i0=dt_t[kb][:], i1=xs_t[kb][:]:
                            e.tensor_tensor(out=o, in0=i0, in1=i1, op=ALU.mult),
                            waits=[cdt, cxs] + lu(("dtx", kb)), inc="s_gp")
                ybank = banks[6] if k % 2 == 0 else banks[7]
                ykey = ("bank", 6) if k % 2 == 0 else ("bank", 7)
                cz = None
                cpe_g = None
                cbb_last = None
                cbb_of = {}

                def emit_bb(g):
                    bbuf = bb_g2 if g % 2 else bb_d
                    bkey = ("bbg", g % 2)
                    cbb = p.add("gpsimd", lambda e, o=bbuf[:],
                                i0=dtx_t[kb][:, None, :].broadcast_to([128, 4, TC]),
                                i1=brep[:].rearrange("p (n t) -> p n t", n=N)[:, g * 4:(g + 1) * 4, :]:
                                e.tensor_tensor(out=o.rearrange("p (n t) -> p n t", n=4), in0=i0, in1=i1, op=ALU.mult),
                                waits=(([cbp[g // 2]] if k <= 1 else []) +
                                       lu(bkey)),
                                inc="s_gp")
                    cbb_of[g] = (cbb, bbuf, bkey)
                    return cbb

                emit_bb(0)
                for g in range(4):
                    abuf = ab_g[g % 2]
                    caa = None
                    for j in range(4):
                        n = g * 4 + j
                        w0 = []
                        if j == 0:
                            w0 += lu(("ab", g % 2))
                            if g == 0:
                                w0 += [cdt]
                        caa = p.add("scalar", lambda e, o=abuf[:, j * TC:(j + 1) * TC],
                                    i=dt_t[kb][:], s=a_neg[:, k * N + n:k * N + n + 1]:
                                    e.activation(out=o, in_=i, func=AF.Exp, scale=s),
                                    waits=w0, inc="s_act" if j == 3 else None)
                    cbb, bbuf, bkey = cbb_of[g]
                    cbb_last = cbb
                    csn = None
                    for j in range(4):
                        n = g * 4 + j
                        w0 = []
                        if j == 0:
                            w0 = [("s_act", caa), ("s_gp", cbb)] + lu(("h", g))
                        csn = p.add("vector", lambda e, o=h_slab[:, n * TC:(n + 1) * TC],
                                    a=abuf[:, j * TC:(j + 1) * TC], b2=bbuf[:, j * TC:(j + 1) * TC],
                                    ini=state[:, k * N + n:k * N + n + 1]:
                                    e.tensor_tensor_scan(out=o, data0=a, data1=b2, initial=ini,
                                                         op0=ALU.mult, op1=ALU.add),
                                    waits=w0, inc="s_dve" if j == 3 else None)
                    last_use[("ab", g % 2)] = [("s_dve", csn)]
                    last_use[bkey] = [("s_dve", csn)]
                    if g == 0 and pend[0] is not None:
                        emit_yevac(pend[0])
                        pend[0] = None
                        if k + 1 < KD:
                            load_k(k + 1)
                    if g < 3:
                        emit_bb(g + 1)
                    zi = g % 4
                    zb = z4[zi]
                    cz = p.add("vector", lambda e, o=zb[:],
                               i0=h_slab[:, (g * 4) * TC:(g * 4 + 4) * TC],
                               i1=crep[:, (g * 4) * TC:(g * 4 + 4) * TC]:
                               e.tensor_tensor(out=o, in0=i0, in1=i1, op=ALU.mult),
                               waits=(([ccp[g // 2]] if k == 0 else []) +
                                      lu(("z4", zi))),
                               inc="s_dve")
                    last_use[("h", g)] = [("s_dve", cz)]
                    for j in range(4):
                        nn = g * 4 + j
                        w0 = [("s_dve", cz)] if j == 0 else []
                        if nn == 0:
                            w0 += lu(ykey)
                        cpe_g = p.add("tensor", lambda e, o=ybank[:, 0:TC], idn=ident_b[:],
                                      r=zb[:, j * TC:(j + 1) * TC],
                                      s=(nn == 0), st=(nn == N - 1):
                                      e.matmul(o, idn, r, start=s, stop=st),
                                      waits=w0, inc="s_pe" if j == 3 else None)
                    last_use[("z4", zi)] = [("s_pe", cpe_g)]
                    rem = (KD - k) * 4 - g - 1
                    if bg and rem > 0:
                        nbg = (len(bg) + rem - 1) // rem
                        for _ in range(min(nbg, len(bg))):
                            bg.pop(0)()
                cst = p.add("vector", lambda e, o=state[:, k * N:(k + 1) * N],
                            i=h_slab[:].rearrange("p (n t) -> p n t", n=N)[:, :, TC - 1]:
                            e.tensor_copy(o, i), inc="s_dve")
                last_use[("dt_t", kb)] = [("s_act", caa)]
                last_use[("dtx", kb)] = [("s_gp", cbb_last)]
                pend[0] = (k, kb, cpe_g, csz)
            emit_yevac(pend[0])
            pend[0] = None
            c_yg[sc] = p.c("s_gp")
            last_use["brep"] = [("s_dve", p.c("s_dve"))]
            last_use["crep"] = [("s_gp", p.c("s_gp"))]

        for u in stage1_units(0):
            u()
        p.mark("stage1_0_done")
        bg = []
        s1u = []
        for sc in range(nsc):
            cp = sc // 2
            if sc > 0:
                bg += post_units(sc - 1)
            if sc % 2 == 0 and cp + 1 < nmc:
                s1u = stage1_units(cp + 1)
            if s1u:
                take = (len(s1u) + 1) // 2 if sc % 2 == 0 else len(s1u)
                bg += s1u[:take]
                s1u = s1u[take:]
            scan_chunk(sc, bg)
            p.mark(f"scan_{sc}_done")
        for u in bg:
            u()
        for u in post_units(nsc - 1):
            u()
        p.mark("tail_done")

        if TRUNCATE is None:
            p.add("sync", lambda e: e.nop(), waits=[("d_out", p.c("d_out"))])

        with nc.Block() as block:
            p.emit(block, sems)

    return nc


_NC_CACHE = None


def kernel(L0_attn_x, L0_attn_W_in, L0_attn_A_log, L0_attn_W_B, L0_attn_W_C,
           L0_attn_W_dt, L0_attn_b_dt, L0_attn_D, L0_attn_output_gate,
           L0_attn_W_out, L0_ffn_cond, L0_ffn_mask, L0_ffn_W_shared,
           L0_ffn_W_expert, L0_ffn_W_gate, L0_ffn_bias):
    global _NC_CACHE
    if _NC_CACHE is None:
        _NC_CACHE = build_nc()
    nc = _NC_CACHE

    f32 = np.float32
    cont = np.ascontiguousarray
    winT = cont(L0_attn_W_in.T.astype(f32, copy=False))
    wdtT = cont(L0_attn_W_dt.T.astype(f32, copy=False))
    wbcT = cont(np.concatenate([L0_attn_W_B, L0_attn_W_C], axis=0).T.astype(f32, copy=False))
    woutT = cont(L0_attn_W_out.T.astype(f32, copy=False))
    wsT = cont(L0_ffn_W_shared.T.astype(f32, copy=False))
    weT = cont(L0_ffn_W_expert.T.astype(f32, copy=False))

    common = dict(winT=winT, wdtT=wdtT, wbcT=wbcT, woutT=woutT, wsT=wsT, weT=weT,
                  alog=cont(L0_attn_A_log.astype(f32, copy=False)),
                  bdt=cont(L0_attn_b_dt.astype(f32, copy=False)),
                  dvec=cont(L0_attn_D.astype(f32, copy=False)),
                  bias=cont(L0_ffn_bias.astype(f32, copy=False)),
                  gate=cont(L0_attn_output_gate.astype(f32, copy=False)).reshape(1, 1),
                  consts=np.tile(np.array([1e-5, -0.5, 1.0 / D, -1.0, 1.0, -1.0 / D, 0.0, 0.0],
                                          dtype=f32), (128, 1)),
                  identf=np.eye(128, dtype=f32),
                  identb=np.eye(128, dtype=ml_dtypes.bfloat16))
    in_maps = []
    for i in range(8):
        m = dict(common)
        m["x"] = cont(L0_attn_x[i].astype(f32, copy=False))
        in_maps.append(m)

    res = run_bass_kernel_spmd(nc, in_maps, core_ids=list(range(8)))
    return np.stack([res.results[i]["out"] for i in range(8)], axis=0)


# version 15
# speedup vs baseline: 1.0178x; 1.0178x over previous
import numpy as np
import ml_dtypes
import contextlib
import concourse.bass as bass
import concourse.mybir as mybir
from concourse.bass_utils import run_bass_kernel_spmd

F32 = mybir.dt.float32
BF16 = mybir.dt.bfloat16
AF = mybir.ActivationFunctionType
ALU = mybir.AluOpType
AX = mybir.AxisListType

T, D = 2048, 1024
E = 2048
N = 16
MC = 512
NMC = T // MC
TC = 256
NSC = T // TC
KD = E // 128
KDM = D // 128

TRUNCATE = None
DEBUG_SCRATCH = False
MARKS = {}

ENG_SEM = {"scalar": "s_act", "vector": "s_dve", "tensor": "s_pe", "gpsimd": "s_gp"}


DMA_ROT = {"d_x": 1, "d_w": 3, "d_scr": 3, "d_brep": 4, "d_wr": 2, "d_out": 1}


def dma_sem_names():
    out = []
    for base, n in DMA_ROT.items():
        if n == 1:
            out.append(base)
        else:
            out.extend(f"{base}{i}" for i in range(n))
    return out


class Prog:

    def __init__(self):
        self.ops = {k: [] for k in ("sync", "scalar", "vector", "tensor", "gpsimd")}
        self.cnt = {}
        self.gidx = 0
        self.rrc = {}

    def add(self, eng, fn, waits=(), inc=None, inc_k=1):
        self.gidx += 1
        new = None
        if inc is not None:
            self.cnt[inc] = self.cnt.get(inc, 0) + inc_k
            new = self.cnt[inc]
        if TRUNCATE is not None and self.gidx > TRUNCATE:
            return new
        own = ENG_SEM.get(eng)
        for (sem, val) in waits:
            if val and val > 0 and sem != own:
                self.ops[eng].append(("wait", sem, val))
        self.ops[eng].append(("op", fn, inc, new, inc_k))
        return new

    def dma(self, eng, out, in_, waits=(), inc=None, slow=False):
        def fn(e, out=out, in_=in_, slow=slow):
            if slow:
                return e.dma_start(out=out, in_=in_, allow_slow_non_contiguous=True)
            return e.dma_start(out=out, in_=in_)
        waits = list(waits)
        n = DMA_ROT.get(inc, 1)
        if n > 1:
            idx = self.rrc.get(inc, 0)
            self.rrc[inc] = (idx + 1) % n
            sem = f"{inc}{idx}"
        else:
            sem = inc
        prev = self.c(sem)
        if prev:
            waits.append((sem, prev))
        val = self.add(eng, fn, waits=waits, inc=sem, inc_k=16)
        return (sem, val)

    def c(self, sem):
        return self.cnt.get(sem, 0)

    def cs(self, base):
        n = DMA_ROT.get(base, 1)
        names = [base] if n == 1 else [f"{base}{i}" for i in range(n)]
        return [(s, self.cnt[s]) for s in names if self.cnt.get(s, 0) > 0]

    def mark(self, name):
        MARKS[name] = self.gidx

    def emit(self, block, sems):
        waited = {}
        for eng, items in self.ops.items():
            for it in items:
                if it[0] == "wait":
                    waited.setdefault(it[1], set()).add(it[2])
        rank = {}
        for sem, vals in waited.items():
            if not sem.startswith("d_"):
                rank[sem] = {v: i + 1 for i, v in enumerate(sorted(vals))}

        def mk(eng_name):
            def body(engobj):
                for item in self.ops[eng_name]:
                    if item[0] == "wait":
                        _, sem, val = item
                        if sem in rank:
                            val = rank[sem][val]
                        engobj.wait_ge(sems[sem], val)
                    else:
                        _, fn, inc, val, inc_k = item
                        instr = fn(engobj)
                        if inc is not None:
                            if inc.startswith("d_"):
                                instr.then_inc(sems[inc], inc_k)
                            elif val in rank.get(inc, ()):
                                instr.then_inc(sems[inc], inc_k)
            return body
        block.sync(mk("sync"))
        block.scalar(mk("scalar"))
        block.vector(mk("vector"))
        block.tensor(mk("tensor"))
        block.gpsimd(mk("gpsimd"))


def build_nc():
    nmc = T // MC
    nsc = T // TC
    nc = bass.Bass("TRN2", target_bir_lowering=False, detect_race_conditions=False)

    x_h = nc.dram_tensor("x", [T, D], F32, kind="ExternalInput").ap()
    winT_h = nc.dram_tensor("winT", [D, 2 * E], F32, kind="ExternalInput").ap()
    wdtT_h = nc.dram_tensor("wdtT", [E, E], F32, kind="ExternalInput").ap()
    wbcT_h = nc.dram_tensor("wbcT", [E, 2 * N], F32, kind="ExternalInput").ap()
    woutT_h = nc.dram_tensor("woutT", [E, D], F32, kind="ExternalInput").ap()
    wsT_h = nc.dram_tensor("wsT", [D, D], F32, kind="ExternalInput").ap()
    weT_h = nc.dram_tensor("weT", [D, D], F32, kind="ExternalInput").ap()
    alog_h = nc.dram_tensor("alog", [E, N], F32, kind="ExternalInput").ap()
    bdt_h = nc.dram_tensor("bdt", [E], F32, kind="ExternalInput").ap()
    dvec_h = nc.dram_tensor("dvec", [E], F32, kind="ExternalInput").ap()
    bias_h = nc.dram_tensor("bias", [D], F32, kind="ExternalInput").ap()
    gate_h = nc.dram_tensor("gate", [1, 1], F32, kind="ExternalInput").ap()
    consts_h = nc.dram_tensor("consts", [128, 8], F32, kind="ExternalInput").ap()
    identf_h = nc.dram_tensor("identf", [128, 128], F32, kind="ExternalInput").ap()
    identb_h = nc.dram_tensor("identb", [128, 128], BF16, kind="ExternalInput").ap()

    out_h = nc.dram_tensor("out", [T, D], F32, kind="ExternalOutput").ap()

    knd = "ExternalOutput" if DEBUG_SCRATCH else "Internal"
    xs_sc = nc.dram_tensor("xs_sc", [E, T], F32, kind=knd).ap()
    dt_sc = nc.dram_tensor("dt_sc", [E, T], F32, kind=knd).ap()
    sz_sc = nc.dram_tensor("sz_sc", [E, T], BF16, kind=knd).ap()
    bc_sc = nc.dram_tensor("bc_sc", [2 * N, T], F32, kind=knd).ap()
    sgate_sc = nc.dram_tensor("sgate_sc", [1, 1], F32, kind="Internal").ap()
    lnsc = nc.dram_tensor("lnsc", [128, 8], F32, kind="Internal").ap()
    lnsc2 = nc.dram_tensor("lnsc2", [128, 4], F32, kind="Internal").ap()
    winb_sc = nc.dram_tensor("winb_sc", [2 * KD, 128, KDM * 128], BF16, kind="Internal").ap()
    wdtb_sc = nc.dram_tensor("wdtb_sc", [KD, 128, KD * 128], BF16, kind="Internal").ap()
    woutm_sc = nc.dram_tensor("woutm_sc", [KDM, 128, KD * 128], BF16, kind=knd).ap()
    wcm_sc = nc.dram_tensor("wcm_sc", [KDM, 128, KDM * 128], BF16, kind=knd).ap()

    nc.reset()

    p = Prog()
    sem_names = dma_sem_names() + ["s_act", "s_dve", "s_pe", "s_gp"]

    stack = contextlib.ExitStack()
    sbuf = lambda name, shape, dt: stack.enter_context(nc.sbuf_tensor(name, shape, dt)).ap()

    with stack:
        a_neg = sbuf("a_neg", [128, KD * N], F32)
        bdt_sb = sbuf("bdt_sb", [128, KD], F32)
        d_sb = sbuf("d_sb", [128, KD], F32)
        bias_sb = sbuf("bias_sb", [128, KDM], F32)
        ident_f = sbuf("ident_f", [128, 128], F32)
        ident_b = sbuf("ident_b", [128, 128], BF16)
        g_sb = sbuf("g_sb", [1, 4], F32)
        csts = sbuf("cst", [128, 8], F32)
        ln_tmp = sbuf("ln_tmp", [128, 2], F32)
        ln_rb = sbuf("ln_rb", [128, 8], F32)
        ln_ra = sbuf("ln_ra", [128, 16], F32)
        ln_rb2 = sbuf("ln_rb2", [128, 4], F32)
        ln_ra2 = sbuf("ln_ra2", [128, 8], F32)
        sbc_sb = sbuf("sbc_sb", [128, 1], F32)

        wbc_bf = sbuf("wbc_bf", [128, KD * 2 * N], BF16)
        wstage = sbuf("wstage", [128, KD * 128], F32)
        wbf = [sbuf(f"wbf{i}", [128, KD * 128], BF16) for i in range(2)]
        wout_m = [sbuf(f"woutm{i}", [128, KD * 128], BF16) for i in range(2)]
        wcm_m = [sbuf(f"wcmm{i}", [128, KDM * 128], BF16) for i in range(2)]

        x_sb = sbuf("x_sb", [128, D], F32)
        xc_sb = sbuf("xc_sb", [128, D], F32)
        xhat = sbuf("xhat", [128, D], F32)
        xhatT = sbuf("xhatT", [128, KDM * MC], BF16)
        xsbf = sbuf("xsbf", [128, KD * MC], BF16)
        xs_f = [sbuf(f"xs_f{i}", [128, MC], F32) for i in range(2)]
        dt_f = [sbuf(f"dt_f{i}", [128, MC], F32) for i in range(2)]
        szb = [sbuf(f"szb{i}", [128, MC], BF16) for i in range(2)]
        bc_f = sbuf("bc_f", [32, MC], F32)
        ln_r = sbuf("ln_r", [128, 16], F32)
        ln_r2 = sbuf("ln_r2", [128, 8], F32)

        brep = sbuf("brep", [128, N * TC], F32)
        crep = sbuf("crep", [128, N * TC], F32)
        dt_t = [sbuf(f"dt_t{i}", [128, TC], F32) for i in range(2)]
        xs_t = [sbuf(f"xs_t{i}", [128, TC], F32) for i in range(2)]
        sz_t = [sbuf(f"sz_t{i}", [128, TC], BF16) for i in range(2)]
        dtx_t0 = sbuf("dtx0", [128, TC], F32)
        dtx_t1 = sbuf("dtx1", [128, TC], F32)
        ab_g = [sbuf(f"ab{i}", [128, 4 * TC], F32) for i in range(2)]
        bb_g2 = sbuf("bb_g2", [128, 4 * TC], F32)
        h_slab = sbuf("h_slab", [128, N * TC], F32)
        z4 = [sbuf(f"z4{i}", [128, 4 * TC], BF16) for i in range(3)]
        y_tot = sbuf("y_tot", [128, TC], F32)
        yg = [sbuf(f"yg{i}", [128, KD * TC], BF16) for i in range(2)]
        state = sbuf("state", [128, KD * N], F32)

        at_f = [sbuf(f"at_f{i}", [128, TC], F32) for i in range(2)]
        x1_sb = sbuf("x1_sb", [128, 2 * D], F32)
        xr_sb = [sbuf(f"xr_sb{i}", [128, D], F32) for i in range(2)]
        xh2 = sbuf("xh2", [128, D], F32)
        xh2d = sbuf("xh2d", [128, D], F32)
        xh2T = sbuf("xh2T", [128, KDM * TC], BF16)
        ff_f = [sbuf(f"ff_f{i}", [128, TC], F32) for i in range(2)]
        x2_sb = sbuf("x2_sb", [128, D], F32)

        bb_d = wstage[:, 0:4 * TC]
        bb_d2 = wstage[:, 4 * TC:8 * TC]
        dtx_t = [dtx_t0, dtx_t1]

        banks = [stack.enter_context(nc.psum_tensor(f"ps{i}", [128, 512], F32)).ap()
                 for i in range(8)]

        sems = {}
        for s in sem_names:
            sems[s] = stack.enter_context(nc.semaphore(s))

        last_use = {}
        def lu(key):
            return list(last_use.get(key, []))

        p.add("vector", lambda e, o=state[:]: e.memset(o, 0.0), inc="s_dve")

        for apx in (ab_g[0][:], ab_g[1][:], xhat[:], xc_sb[:], xh2[:], xh2d[:],
                    xhatT[:], xh2T[:], xs_f[0][:], xs_f[1][:], dt_f[0][:],
                    dt_f[1][:], szb[0][:], szb[1][:], bc_f[:], at_f[0][:],
                    at_f[1][:], ff_f[0][:], ff_f[1][:], a_neg[:], ln_tmp[:],
                    ln_ra[:], ln_ra2[:], g_sb[0:1, 1:4]):
            p.add("scalar", lambda e, o=apx: e.memzero(o), inc=None)
        for apx in (wbf[0], wbf[1], wbc_bf, h_slab, x1_sb, x2_sb, y_tot,
                    ln_r, ln_r2, z4[0], z4[1], z4[2], yg[0], yg[1]):
            p.add("vector", lambda e, o=apx[:]: e.memset(o, 0.0), inc=None)
        for apx in (dtx_t0, dtx_t1, bb_g2):
            p.add("gpsimd", lambda e, o=apx[:]: e.memset(o, 0.0), inc=None)
        p.add("scalar", lambda e, o=xsbf[:]: e.memzero(o), inc=None)
        cms = None
        for i in range(8):
            cms = p.add("vector", lambda e, o=banks[i][:]: e.memset(o, 0.0),
                        inc="s_dve" if i == 7 else None)
        for i in range(8):
            last_use[("bank", i)] = [("s_dve", cms)]

        p.dma("sync", ident_f[:], identf_h, inc="d_w")
        p.dma("sync", ident_b[:], identb_h, inc="d_w")
        p.dma("sync", csts[:], consts_h, inc="d_w")
        p.dma("sync", bdt_sb[:], bdt_h.rearrange("(k p) -> p k", p=128), inc="d_w", slow=True)
        p.dma("sync", d_sb[:], dvec_h.rearrange("(k p) -> p k", p=128), inc="d_w", slow=True)
        p.dma("sync", bias_sb[:], bias_h.rearrange("(k p) -> p k", p=128), inc="d_w", slow=True)
        p.dma("sync", g_sb[0:1, 0:1], gate_h, inc="d_w")
        c_w0 = p.cs("d_w")
        ca0 = p.dma("sync", wstage[:, 0:KD * N].rearrange("p (k n) -> p k n", k=KD),
                    alog_h.rearrange("(k p) n -> p k n", p=128), inc="d_w")
        ca1 = p.add("scalar", lambda e, o=a_neg[:], i=wstage[:, 0:KD * N]:
                    e.activation(out=o, in_=i, func=AF.Exp),
                    waits=[ca0], inc="s_act")
        last_use["wstage"] = [("s_act", ca1)]
        p.add("vector", lambda e, o=a_neg[:], i=a_neg[:]: e.tensor_scalar_mul(o, i, -1.0),
              waits=[("s_act", ca1)], inc="s_dve")
        cga = p.add("scalar", lambda e, o=g_sb[0:1, 1:2], i=g_sb[0:1, 0:1]:
                    e.activation(out=o, in_=i, func=AF.Sigmoid),
                    waits=c_w0, inc="s_act")
        csg = p.dma("sync", sgate_sc, g_sb[0:1, 1:2], waits=[("s_act", cga)], inc="d_wr")
        c_sbc = p.dma("sync", sbc_sb[:], sgate_sc.broadcast_to([128, 1]),
                      waits=[csg], inc="d_w")

        for k in range(KDM):
            c1 = p.dma("sync", wstage[:, 0:D], wsT_h[k * 128:(k + 1) * 128, :],
                       waits=lu("wstage"), inc="d_w")
            cw = p.dma("sync", wstage[:, D:2 * D], weT_h[k * 128:(k + 1) * 128, :], inc="d_w")
            cv = p.add("vector", lambda e, o=wbf[k % 2][:, 0:D], i0=wstage[:, 0:D],
                       i1=wstage[:, D:2 * D]: e.tensor_tensor(out=o, in0=i0, in1=i1, op=ALU.add),
                       waits=[c1, cw] + lu(("wbf", k % 2)), inc="s_dve")
            last_use["wstage"] = [("s_dve", cv)]
            cd = p.dma("sync", wcm_sc[:, :, k * 128:(k + 1) * 128].rearrange("m p j -> p m j"),
                       wbf[k % 2][:, 0:D].rearrange("p (m j) -> p m j", m=KDM),
                       waits=[("s_dve", cv)], inc="d_wr", slow=True)
            last_use[("wbf", k % 2)] = [cd]
        cw = p.dma("sync", wstage[:, 0:KD * 32].rearrange("p (k m) -> p k m", k=KD),
                   wbcT_h.rearrange("(k p) m -> p k m", p=128),
                   waits=lu("wstage"), inc="d_w")
        cv = p.add("vector", lambda e, o=wbc_bf[:], i=wstage[:, 0:KD * 32]: e.tensor_copy(o, i),
                   waits=[cw], inc="s_dve")
        last_use["wstage"] = [("s_dve", cv)]
        for k in range(KD):
            cw = p.dma("sync", wstage[:, 0:D], woutT_h[k * 128:(k + 1) * 128, :],
                       waits=lu("wstage"), inc="d_w")
            cv = p.add("vector", lambda e, o=wbf[k % 2][:, 0:D], i=wstage[:, 0:D], s=sbc_sb[:, 0:1]:
                       e.tensor_scalar(out=o, in0=i, scalar1=s, scalar2=None, op0=ALU.mult),
                       waits=[cw, c_sbc] + lu(("wbf", k % 2)), inc="s_dve")
            last_use["wstage"] = [("s_dve", cv)]
            cd = p.dma("sync", woutm_sc[:, :, k * 128:(k + 1) * 128].rearrange("m p j -> p m j"),
                       wbf[k % 2][:, 0:D].rearrange("p (m j) -> p m j", m=KDM),
                       waits=[("s_dve", cv)], inc="d_wr", slow=True)
            last_use[("wbf", k % 2)] = [cd]
        c_wm_w = p.cs("d_wr")
        c_wcast_w = [0]

        c_xs_w = {}
        c_dt_w = {}
        c_sz_w = {}
        c_bc_w = {}
        c_yg = {}

        def stage1_units(cp):
            units = []
            t0 = cp * MC
            ctx = {}

            def lnA(tt, t0=t0):
                def f():
                    r0 = t0 + tt * 128
                    sqb = xc_sb if tt % 2 == 0 else xhat
                    sqk = "xc_sb" if tt % 2 == 0 else "xhat"
                    cx = p.dma("sync", x_sb[:], x_h[r0:r0 + 128, :],
                               waits=lu("x_sb"), inc="d_x")
                    p.add("vector", lambda e, o=ln_r[:, 2 * tt:2 * tt + 1], i=x_sb[:]:
                          e.reduce_sum(o, i, axis=AX.X),
                          waits=[cx] + (lu("lnr1") if tt == 0 else []),
                          inc="s_dve")
                    c_sq = p.add("scalar", lambda e, o=sqb[:], i=x_sb[:]:
                                 e.activation(out=o, in_=i, func=AF.Square),
                                 waits=[cx] + lu(sqk), inc="s_act")
                    cr2 = p.add("vector", lambda e, o=ln_r[:, 2 * tt + 1:2 * tt + 2], i=sqb[:]:
                                e.reduce_sum(o, i, axis=AX.X),
                                waits=[("s_act", c_sq)], inc="s_dve")
                    ctx["c_sums"] = cr2
                    last_use[sqk] = [("s_dve", cr2)]
                    last_use["x_sb"] = [("s_dve", cr2), ("s_act", c_sq)]
                return f

            def lnRT():
                def f():
                    cd1 = p.dma("sync", lnsc, ln_r[:, 0:8], waits=[("s_dve", ctx["c_sums"])],
                                inc="d_wr")
                    cd2 = p.dma("sync", ln_rb[:], lnsc,
                                waits=[cd1] + lu("ln_rb"), inc="d_w")
                    ctx["c_lnrt"] = cd2
                    last_use["lnr1"] = [cd1]
                return f

            def lnB(tt):
                def f():
                    if True:
                        w0 = [ctx["c_lnrt"]] if tt == 0 else []
                        p.add("scalar", lambda e, o=ln_ra[:, 4 * tt:4 * tt + 1],
                              i=ln_rb[:, 2 * tt:2 * tt + 1], s=csts[:, 5:6]:
                              e.activation(out=o, in_=i, func=AF.Identity, scale=s),
                              waits=w0, inc=None)
                        p.add("scalar", lambda e, o=ln_tmp[:, 0:1], i=ln_ra[:, 4 * tt:4 * tt + 1]:
                              e.activation(out=o, in_=i, func=AF.Square), inc=None)
                        p.add("scalar", lambda e, o=ln_tmp[:, 1:2], i=ln_tmp[:, 0:1],
                              s=csts[:, 3:4], bi=csts[:, 0:1]:
                              e.activation(out=o, in_=i, func=AF.Identity, scale=s, bias=bi), inc=None)
                        p.add("scalar", lambda e, o=ln_tmp[:, 0:1], i=ln_rb[:, 2 * tt + 1:2 * tt + 2],
                              s=csts[:, 2:3], bi=ln_tmp[:, 1:2]:
                              e.activation(out=o, in_=i, func=AF.Ln, scale=s, bias=bi), inc=None)
                        p.add("scalar", lambda e, o=ln_ra[:, 4 * tt + 2:4 * tt + 3], i=ln_tmp[:, 0:1],
                              s=csts[:, 1:2]:
                              e.activation(out=o, in_=i, func=AF.Exp, scale=s), inc=None)
                        ca = p.add("scalar", lambda e, o=ln_ra[:, 4 * tt + 3:4 * tt + 4],
                                   i=ln_ra[:, 4 * tt + 2:4 * tt + 3], s=ln_ra[:, 4 * tt:4 * tt + 1]:
                                   e.activation(out=o, in_=i, func=AF.Identity, scale=s),
                                   inc="s_act" if tt == 3 else None)
                    if tt == 3:
                        last_use["ln_rb"] = [("s_act", ca)]
                        last_use["xc_sb"] = [("s_act", ca)]
                        last_use["x_sb"] = [("s_act", ca)]
                return f

            def lnC(tt, t0=t0):
                def f():
                    r0 = t0 + tt * 128
                    cx = p.dma("sync", x_sb[:], x_h[r0:r0 + 128, :],
                               waits=lu("x_sb"), inc="d_x")
                    c_ph = p.add("scalar", lambda e, o=xhat[:], i=x_sb[:],
                                 s=ln_ra[:, 4 * tt + 2:4 * tt + 3],
                                 bi=ln_ra[:, 4 * tt + 3:4 * tt + 4]:
                                 e.activation(out=o, in_=i, func=AF.Identity, scale=s, bias=bi),
                                 waits=[cx] + lu("xhat"),
                                 inc="s_act")
                    last_use["x_sb"] = [("s_act", c_ph)]
                    xh_done = []
                    for b in range(2):
                        bk = banks[2 + b]
                        for q in range(4):
                            dk = b * 4 + q
                            cpe = p.add("tensor", lambda e, o=bk[:, q * 128:(q + 1) * 128],
                                        i=xhat[:, dk * 128:(dk + 1) * 128], idn=ident_f[:]:
                                        e.transpose(o, i, idn),
                                        waits=([("s_act", c_ph)] + lu(("bank", 2 + b))) if q == 0 else [],
                                        inc="s_pe" if q == 3 else None)
                        xh_done.append(("s_pe", cpe))
                        ca2 = p.add("scalar", lambda e,
                                    o=xhatT[:].rearrange("p (dk t) -> p dk t", dk=KDM)[:, b * 4:(b + 1) * 4, tt * 128:(tt + 1) * 128],
                                    i=bk[:]:
                                    e.activation(out=o, in_=i, func=AF.Copy),
                                    waits=[("s_pe", cpe)], inc="s_act")
                        last_use[("bank", 2 + b)] = [("s_act", ca2)]
                    last_use["xhat"] = xh_done
                    if tt == 3:
                        ctx["c_xhatT"] = p.c("s_act")
                return f

            def xz(eo, t0=t0, cp=cp):
                def f():
                    wb = wbf[eo % 2]
                    if cp == 0:
                        cw = p.dma("sync", wstage[:, 0:KDM * 128].rearrange("p (k m) -> p k m", k=KDM),
                                   winT_h.rearrange("(k p) m -> p k m", p=128)[:, :, eo * 128:(eo + 1) * 128],
                                   waits=lu("wstage"), inc="d_w")
                        cv = p.add("vector", lambda e, o=wb[:, 0:KDM * 128], i=wstage[:, 0:KDM * 128]:
                                   e.tensor_copy(o, i),
                                   waits=[cw] + lu(("wbf", eo % 2)), inc="s_dve")
                        last_use["wstage"] = [("s_dve", cv)]
                        p.dma("sync", winb_sc[eo], wb[:, 0:KDM * 128], waits=[("s_dve", cv)], inc="d_wr")
                        rdy = [("s_dve", cv)]
                    else:
                        cw = p.dma("sync", wb[:, 0:KDM * 128], winb_sc[eo],
                                   waits=c_wcast_w[0] + lu(("wbf", eo % 2)), inc="d_w")
                        rdy = [cw]
                    bi = eo % 2
                    bank = banks[bi]
                    cpe = None
                    for k in range(KDM):
                        w0 = (rdy + [("s_act", ctx["c_xhatT"])] + lu(("bank", bi))) if k == 0 else []
                        cpe = p.add("tensor", lambda e, o=bank[:], w=wb[:, k * 128:(k + 1) * 128],
                                    r=xhatT[:, k * MC:(k + 1) * MC], s=(k == 0), st=(k == KDM - 1):
                                    e.matmul(o, w, r, start=s, stop=st),
                                    waits=w0, inc="s_pe" if k == KDM - 1 else None)
                    last_use[("wbf", eo % 2)] = [("s_pe", cpe)]
                    if eo < KD:
                        fb = xs_f[eo % 2]
                        ca = p.add("scalar", lambda e, o=fb[:], i=bank[:]:
                                   e.activation(out=o, in_=i, func=AF.Copy),
                                   waits=[("s_pe", cpe)] + lu(("xs_f", eo % 2)), inc="s_act")
                        cv = p.add("scalar", lambda e, o=xsbf[:, eo * MC:(eo + 1) * MC], i=bank[:]:
                                   e.activation(out=o, in_=i, func=AF.Copy),
                                   waits=(lu("xsbf") if eo == 0 else []), inc="s_act")
                        cd = p.dma("sync", xs_sc[eo * 128:(eo + 1) * 128, t0:t0 + MC], fb[:],
                                   waits=[("s_act", ca)], inc="d_wr")
                        c_xs_w[(eo, cp)] = cd
                        last_use[("xs_f", eo % 2)] = [("s_act", ca), cd]
                        last_use[("bank", bi)] = [("s_act", ca)]
                        if eo == KD - 1:
                            ctx["c_xsbf_part"] = cv
                    else:
                        zb = szb[eo % 2]
                        ca = p.add("scalar", lambda e, o=zb[:], i=bank[:]:
                                   e.activation(out=o, in_=i, func=AF.Silu),
                                   waits=[("s_pe", cpe)] + lu(("szb", eo % 2)), inc="s_act")
                        cd = p.dma("sync", sz_sc[(eo - KD) * 128:(eo - KD + 1) * 128, t0:t0 + MC], zb[:],
                                   waits=[("s_act", ca)], inc="d_wr")
                        c_sz_w[(eo - KD, cp)] = cd
                        last_use[("szb", eo % 2)] = [cd]
                        last_use[("bank", bi)] = [("s_act", ca)]
                        if eo == 2 * KD - 1:
                            ctx["c_xsbf"] = ctx["c_xsbf_part"]
                return f

            def dtm(eo, t0=t0, cp=cp):
                def f():
                    wb = wbf[eo % 2]
                    if cp == 0:
                        cw = p.dma("sync", wstage[:].rearrange("p (k m) -> p k m", k=KD),
                                   wdtT_h.rearrange("(k p) m -> p k m", p=128)[:, :, eo * 128:(eo + 1) * 128],
                                   waits=lu("wstage"), inc="d_w")
                        cv = p.add("vector", lambda e, o=wb[:], i=wstage[:]: e.tensor_copy(o, i),
                                   waits=[cw] + lu(("wbf", eo % 2)), inc="s_dve")
                        last_use["wstage"] = [("s_dve", cv)]
                        p.dma("sync", wdtb_sc[eo], wb[:], waits=[("s_dve", cv)], inc="d_wr")
                        rdy = [("s_dve", cv)]
                    else:
                        cw = p.dma("sync", wb[:], wdtb_sc[eo],
                                   waits=c_wcast_w[0] + lu(("wbf", eo % 2)), inc="d_w")
                        rdy = [cw]
                    bi = eo % 2
                    bank = banks[bi]
                    cpe = None
                    for k in range(KD):
                        w0 = (rdy + [("s_act", ctx["c_xsbf"])] + lu(("bank", bi))) if k == 0 else []
                        cpe = p.add("tensor", lambda e, o=bank[:], w=wb[:, k * 128:(k + 1) * 128],
                                    r=xsbf[:, k * MC:(k + 1) * MC], s=(k == 0), st=(k == KD - 1):
                                    e.matmul(o, w, r, start=s, stop=st),
                                    waits=w0, inc="s_pe" if k == KD - 1 else None)
                    last_use[("wbf", eo % 2)] = [("s_pe", cpe)]
                    fb = dt_f[eo % 2]
                    p.add("scalar", lambda e, o=xc_sb[:, 0:MC], i=bank[:], bi2=bdt_sb[:, eo:eo + 1]:
                          e.activation(out=o, in_=i, func=AF.Exp, bias=bi2),
                          waits=[("s_pe", cpe)] + lu("xc_sb"), inc="s_act")
                    ca = p.add("scalar", lambda e, o=fb[:], i=xc_sb[:, 0:MC], bi2=csts[:, 4:5]:
                               e.activation(out=o, in_=i, func=AF.Ln, bias=bi2),
                               waits=lu(("dt_f", eo % 2)), inc="s_act")
                    last_use["xc_sb"] = [("s_act", ca)]
                    cd = p.dma("sync", dt_sc[eo * 128:(eo + 1) * 128, t0:t0 + MC], fb[:],
                               waits=[("s_act", ca)], inc="d_wr")
                    c_dt_w[(eo, cp)] = cd
                    last_use[("dt_f", eo % 2)] = [cd]
                    last_use[("bank", bi)] = [("s_act", ca)]
                    if cp == 0 and eo == KD - 1:
                        c_wcast_w[0] = p.cs("d_wr")
                return f

            def bcm(t0=t0, cp=cp):
                def f():
                    bank = banks[0]
                    cpe = None
                    for k in range(KD):
                        w0 = lu(("bank", 0)) if k == 0 else []
                        cpe = p.add("tensor", lambda e, o=bank[0:32, :], w=wbc_bf[:, k * 32:(k + 1) * 32],
                                    r=xsbf[:, k * MC:(k + 1) * MC], s=(k == 0), st=(k == KD - 1):
                                    e.matmul(o, w, r, start=s, stop=st),
                                    waits=w0, inc="s_pe" if k == KD - 1 else None)
                    last_use["xsbf"] = [("s_pe", cpe)]
                    ca = p.add("scalar", lambda e, o=bc_f[:], i=bank[0:32, :]:
                               e.activation(out=o, in_=i, func=AF.Copy),
                               waits=[("s_pe", cpe)] + lu("bc_f"), inc="s_act")
                    cd = p.dma("sync", bc_sc[:, t0:t0 + MC], bc_f[:], waits=[("s_act", ca)], inc="d_wr")
                    last_use["bc_f"] = [cd]
                    last_use[("bank", 0)] = [("s_act", ca)]
                    c_bc_w[cp] = cd
                return f

            for tt in range(4):
                units.append(lnA(tt))
            units.append(lnRT())
            for tt in range(4):
                units.append(lnB(tt))
            for tt in range(4):
                units.append(lnC(tt))
            for eo in range(2 * KD):
                units.append(xz(eo))
            for eo in range(KD):
                units.append(dtm(eo))
            units.append(bcm())
            return units

        def post_units(sc):
            units = []
            s0 = sc * TC
            ctx = {}

            def attn_m(m, sc=sc, s0=s0):
                def f():
                    wm = wout_m[m % 2]
                    cw = p.dma("sync", wm[:], woutm_sc[m],
                               waits=c_wm_w + lu(("wout_m", m % 2)), inc="d_w")
                    bi = 4 + (m % 2)
                    pa = banks[bi][:, 0:TC]
                    cpm = None
                    for k in range(KD):
                        w0 = ([cw, ("s_dve", c_yg[sc])] + lu(("bank", bi))) if k == 0 else []
                        cpm = p.add("tensor", lambda e, o=pa, w=wm[:, k * 128:(k + 1) * 128],
                                    r=yg[sc % 2][:, k * TC:(k + 1) * TC], s=(k == 0), st=(k == KD - 1):
                                    e.matmul(o, w, r, start=s, stop=st),
                                    waits=w0, inc="s_pe" if k == KD - 1 else None)
                    last_use[("wout_m", m % 2)] = [("s_pe", cpm)]
                    ca = p.add("scalar", lambda e, o=at_f[m % 2][:], i=pa:
                               e.activation(out=o, in_=i, func=AF.Copy),
                               waits=[("s_pe", cpm)] + lu(("at_f", m % 2)), inc="s_act")
                    last_use[("bank", bi)] = [("s_act", ca)]
                    ctx[("at_evac", m)] = ca
                    if m == KDM - 1:
                        last_use[("yg", sc % 2)] = [("s_pe", cpm)]
                return f

            def xr_load(sc=sc, s0=s0):
                def f():
                    ctx["cx0"] = p.dma("sync", xr_sb[0][:], x_h[s0:s0 + 128, :],
                                       waits=lu(("xr", 0)), inc="d_x")
                    ctx["cx1"] = p.dma("sync", xr_sb[1][:], x_h[s0 + 128:s0 + 256, :],
                                       waits=lu(("xr", 1)), inc="d_x")
                return f

            def x1_pair(pr, sc=sc):
                def f():
                    tb = 2 + pr % 2
                    cpe = None
                    for j in range(2):
                        m = 2 * pr + j
                        for tt in range(2):
                            q = j * 2 + tt
                            cpe = p.add("tensor", lambda e, o=banks[tb][:, q * 128:(q + 1) * 128],
                                        i=at_f[m % 2][:, tt * 128:(tt + 1) * 128], idn=ident_f[:]:
                                        e.transpose(o, i, idn),
                                        waits=([("s_act", ctx[("at_evac", 2 * pr + 1)])] +
                                               lu(("bank", tb))) if q == 0 else [],
                                        inc="s_pe" if (j == 1 and tt == 1) else None)
                    cv = None
                    for j in range(2):
                        m = 2 * pr + j
                        for tt in range(2):
                            q = j * 2 + tt
                            cv = p.add("vector",
                                       lambda e, o=x1_sb[:, tt * D + m * 128:tt * D + (m + 1) * 128],
                                       i0=banks[tb][:, q * 128:(q + 1) * 128],
                                       i1=xr_sb[tt][:, m * 128:(m + 1) * 128]:
                                       e.tensor_tensor(out=o, in0=i0, in1=i1, op=ALU.add),
                                       waits=(([("s_pe", cpe), ctx["cx0"], ctx["cx1"]] +
                                               (lu("x1_sb") if pr == 0 else [])) if q == 0 else []),
                                       inc="s_dve" if (j == 1 and tt == 1) else None)
                    last_use[("bank", tb)] = [("s_dve", cv)]
                    last_use[("at_f", 0)] = [("s_pe", cpe)]
                    last_use[("at_f", 1)] = [("s_pe", cpe)]
                    ctx.setdefault("x1_done", []).append(cv)
                    if pr == 3:
                        ctx["c_x1"] = max(ctx["x1_done"])
                        last_use[("xr", 0)] = [("s_dve", ctx["c_x1"])]
                        last_use[("xr", 1)] = [("s_dve", ctx["c_x1"])]
                return f

            def ln2A():
                def f():
                    for tt in range(2):
                        sqb = xh2d if tt == 0 else xh2
                        sqk = "xh2d" if tt == 0 else "xh2"
                        p.add("vector", lambda e, o=ln_r2[:, 2 * tt:2 * tt + 1],
                              i=x1_sb[:, tt * D:(tt + 1) * D]:
                              e.reduce_sum(o, i, axis=AX.X),
                              waits=([("s_dve", ctx["c_x1"])] if tt == 0 else []) +
                                    (lu("lnr2") if tt == 0 else []),
                              inc="s_dve")
                        csq2 = p.add("scalar", lambda e, o=sqb[:], i=x1_sb[:, tt * D:(tt + 1) * D]:
                                     e.activation(out=o, in_=i, func=AF.Square),
                                     waits=([("s_dve", ctx["c_x1"])] if tt == 0 else []) + lu(sqk),
                                     inc="s_act")
                        cr2 = p.add("vector", lambda e, o=ln_r2[:, 2 * tt + 1:2 * tt + 2], i=sqb[:]:
                                    e.reduce_sum(o, i, axis=AX.X),
                                    waits=[("s_act", csq2)], inc="s_dve")
                        ctx["c_sums2"] = cr2
                        last_use[sqk] = [("s_dve", cr2)]
                return f

            def ln2RT():
                def f():
                    cd1 = p.dma("sync", lnsc2, ln_r2[:, 0:4], waits=[("s_dve", ctx["c_sums2"])],
                                inc="d_wr")
                    cd2 = p.dma("sync", ln_rb2[:], lnsc2,
                                waits=[cd1] + lu("ln_rb2"), inc="d_w")
                    ctx["c_lnrt2"] = cd2
                    last_use["lnr2"] = [cd1]
                return f

            def ln2BC(tt):
                def f():
                    if True:
                        w0 = [ctx["c_lnrt2"]] if tt == 0 else []
                        p.add("scalar", lambda e, o=ln_ra2[:, 4 * tt:4 * tt + 1],
                              i=ln_rb2[:, 2 * tt:2 * tt + 1], s=csts[:, 5:6]:
                              e.activation(out=o, in_=i, func=AF.Identity, scale=s),
                              waits=w0, inc=None)
                        p.add("scalar", lambda e, o=ln_tmp[:, 0:1], i=ln_ra2[:, 4 * tt:4 * tt + 1]:
                              e.activation(out=o, in_=i, func=AF.Square), inc=None)
                        p.add("scalar", lambda e, o=ln_tmp[:, 1:2], i=ln_tmp[:, 0:1],
                              s=csts[:, 3:4], bi=csts[:, 0:1]:
                              e.activation(out=o, in_=i, func=AF.Identity, scale=s, bias=bi), inc=None)
                        p.add("scalar", lambda e, o=ln_tmp[:, 0:1], i=ln_rb2[:, 2 * tt + 1:2 * tt + 2],
                              s=csts[:, 2:3], bi=ln_tmp[:, 1:2]:
                              e.activation(out=o, in_=i, func=AF.Ln, scale=s, bias=bi), inc=None)
                        p.add("scalar", lambda e, o=ln_ra2[:, 4 * tt + 2:4 * tt + 3], i=ln_tmp[:, 0:1],
                              s=csts[:, 1:2]:
                              e.activation(out=o, in_=i, func=AF.Exp, scale=s), inc=None)
                        ca = p.add("scalar", lambda e, o=ln_ra2[:, 4 * tt + 3:4 * tt + 4],
                                   i=ln_ra2[:, 4 * tt + 2:4 * tt + 3], s=ln_ra2[:, 4 * tt:4 * tt + 1]:
                                   e.activation(out=o, in_=i, func=AF.Identity, scale=s),
                                   inc="s_act" if tt == 1 else None)
                    if tt == 1:
                        last_use["ln_rb2"] = [("s_act", ca)]
                        last_use["xh2d"] = [("s_act", ca)]
                return f

            def ln2T(tt):
                def f():
                    c_ph = p.add("scalar", lambda e, o=xh2[:], i=x1_sb[:, tt * D:(tt + 1) * D],
                                 s=ln_ra2[:, 4 * tt + 2:4 * tt + 3],
                                 bi=ln_ra2[:, 4 * tt + 3:4 * tt + 4]:
                                 e.activation(out=o, in_=i, func=AF.Identity, scale=s, bias=bi),
                                 waits=lu("xh2"), inc="s_act")
                    tr2 = []
                    for b in range(2):
                        bk = banks[2 + b]
                        for q in range(4):
                            dk = b * 4 + q
                            cpe = p.add("tensor", lambda e, o=bk[:, q * 128:(q + 1) * 128],
                                        i=xh2[:, dk * 128:(dk + 1) * 128], idn=ident_f[:]:
                                        e.transpose(o, i, idn),
                                        waits=([("s_act", c_ph)] + lu(("bank", 2 + b))) if q == 0 else [],
                                        inc="s_pe" if q == 3 else None)
                        tr2.append(("s_pe", cpe))
                        ca2 = p.add("scalar", lambda e,
                                    o=xh2T[:].rearrange("p (dk t) -> p dk t", dk=KDM)[:, b * 4:(b + 1) * 4, tt * 128:(tt + 1) * 128],
                                    i=bk[:]:
                                    e.activation(out=o, in_=i, func=AF.Copy),
                                    waits=[("s_pe", cpe)], inc="s_act")
                        last_use[("bank", 2 + b)] = [("s_act", ca2)]
                    last_use["xh2"] = tr2
                    if tt == 1:
                        ctx["c_xh2T"] = p.c("s_act")
                return f

            def ffn_m(m):
                def f():
                    wm = wcm_m[m % 2]
                    cw = p.dma("sync", wm[:], wcm_sc[m],
                               waits=c_wm_w + lu(("wcm_m", m % 2)), inc="d_w")
                    bi = 4 + (m % 2)
                    pf = banks[bi][:, 0:TC]
                    cpm = None
                    for k in range(KDM):
                        w0 = ([cw, ("s_act", ctx["c_xh2T"])] + lu(("bank", bi))) if k == 0 else []
                        cpm = p.add("tensor", lambda e, o=pf, w=wm[:, k * 128:(k + 1) * 128],
                                    r=xh2T[:, k * TC:(k + 1) * TC], s=(k == 0), st=(k == KDM - 1):
                                    e.matmul(o, w, r, start=s, stop=st),
                                    waits=w0, inc="s_pe" if k == KDM - 1 else None)
                    last_use[("wcm_m", m % 2)] = [("s_pe", cpm)]
                    ca = p.add("scalar", lambda e, o=ff_f[m % 2][:], i=pf, bi2=bias_sb[:, m:m + 1]:
                               e.activation(out=o, in_=i, func=AF.Identity, bias=bi2),
                               waits=[("s_pe", cpm)] + lu(("ff_f", m % 2)), inc="s_act")
                    last_use[("bank", bi)] = [("s_act", ca)]
                    ctx[("ff_evac", m)] = ca
                return f

            def x2_pair(pr, sc=sc, s0=s0):
                def f():
                    tb = 2 + pr % 2
                    cpe = None
                    for j in range(2):
                        m = 2 * pr + j
                        for tt in range(2):
                            q = j * 2 + tt
                            cpe = p.add("tensor", lambda e, o=banks[tb][:, q * 128:(q + 1) * 128],
                                        i=ff_f[m % 2][:, tt * 128:(tt + 1) * 128], idn=ident_f[:]:
                                        e.transpose(o, i, idn),
                                        waits=([("s_act", ctx[("ff_evac", 2 * pr + 1)])] +
                                               lu(("bank", tb))) if q == 0 else [],
                                        inc="s_pe" if (j == 1 and tt == 1) else None)
                    cv = None
                    for j in range(2):
                        m = 2 * pr + j
                        for tt in range(2):
                            q = j * 2 + tt
                            dst = (x2_sb if tt == 0 else xh2d)[:, m * 128:(m + 1) * 128]
                            cv = p.add("vector",
                                       lambda e, o=dst, i0=banks[tb][:, q * 128:(q + 1) * 128],
                                       i1=x1_sb[:, tt * D + m * 128:tt * D + (m + 1) * 128]:
                                       e.tensor_tensor(out=o, in0=i0, in1=i1, op=ALU.add),
                                       waits=(([("s_pe", cpe)] +
                                               (lu("x2_sb") + lu("xh2d") if pr == 0 else []))
                                              if q == 0 else []),
                                       inc="s_dve" if (j == 1 and tt == 1) else None)
                    last_use[("bank", tb)] = [("s_dve", cv)]
                    last_use[("ff_f", 0)] = [("s_pe", cpe)]
                    last_use[("ff_f", 1)] = [("s_pe", cpe)]
                    ctx.setdefault("x2_done", []).append(cv)
                    if pr == 3:
                        cdone = max(ctx["x2_done"])
                        for tt in range(2):
                            buf = x2_sb if tt == 0 else xh2d
                            p.dma("sync", out_h[s0 + tt * 128:s0 + (tt + 1) * 128, :], buf[:],
                                  waits=[("s_dve", cdone)], inc="d_out")
                        last_use["x2_sb"] = [("d_out", p.c("d_out"))]
                        last_use["xh2d"] = [("d_out", p.c("d_out"))]
                        last_use["x1_sb"] = [("s_dve", cdone)]
                return f

            units.append(xr_load())
            for m in range(KDM):
                units.append(attn_m(m))
                if m % 2 == 1:
                    units.append(x1_pair(m // 2))
            units.append(ln2A())
            units.append(ln2RT())
            for tt in range(2):
                units.append(ln2BC(tt))
            for tt in range(2):
                units.append(ln2T(tt))
            for m in range(KDM):
                units.append(ffn_m(m))
                if m % 2 == 1:
                    units.append(x2_pair(m // 2))
            return units

        def scan_chunk(sc, bg):
            cp = sc // 2
            s0 = sc * TC
            HN = N // 2
            cbp = []
            ccp = []
            for h in range(2):
                cbp.append(p.dma("gpsimd",
                                 brep[:, h * HN * TC:(h + 1) * HN * TC].rearrange("p (n t) -> p n t", n=HN),
                                 bc_sc[h * HN:(h + 1) * HN, s0:s0 + TC][None, :, :].broadcast_to([128, HN, TC]),
                                 waits=[c_bc_w[cp]] + lu("brep"), inc="d_brep"))
            for h in range(2):
                ccp.append(p.dma("gpsimd",
                                 crep[:, h * HN * TC:(h + 1) * HN * TC].rearrange("p (n t) -> p n t", n=HN),
                                 bc_sc[N + h * HN:N + (h + 1) * HN, s0:s0 + TC][None, :, :].broadcast_to([128, HN, TC]),
                                 waits=lu("crep"), inc="d_brep"))

            pend = [None]
            loaded = {}

            def load_k(k):
                kb = k % 2
                cdt = p.dma("gpsimd", dt_t[kb][:], dt_sc[k * 128:(k + 1) * 128, s0:s0 + TC],
                            waits=[c_dt_w[(k, cp)]] + lu(("dt_t", kb)), inc="d_scr")
                cxs = p.dma("gpsimd", xs_t[kb][:], xs_sc[k * 128:(k + 1) * 128, s0:s0 + TC],
                            waits=[c_xs_w[(k, cp)]] + lu(("xs_t", kb)), inc="d_scr")
                csz = p.dma("gpsimd", sz_t[kb][:], sz_sc[k * 128:(k + 1) * 128, s0:s0 + TC],
                            waits=[c_sz_w[(k, cp)]] + lu(("sz_t", kb)), inc="d_scr")
                loaded[k] = (cdt, cxs, csz)

            def emit_yevac(rec):
                kk, kkb, cstop, csz_ = rec
                ybank = banks[6] if kk % 2 == 0 else banks[7]
                ykey = ("bank", 6) if kk % 2 == 0 else ("bank", 7)
                cyt = p.add("vector", lambda e, o=y_tot[:], i0=xs_t[kkb][:],
                            s=d_sb[:, kk:kk + 1], i1=ybank[:, 0:TC]:
                            e.scalar_tensor_tensor(out=o, in0=i0, scalar=s, in1=i1,
                                                   op0=ALU.mult, op1=ALU.add),
                            waits=[("s_pe", cstop), csz_],
                            inc="s_dve")
                last_use[ykey] = [("s_dve", cyt)]
                last_use[("xs_t", kkb)] = [("s_dve", cyt)]
                cyg = p.add("vector", lambda e, o=yg[sc % 2][:, kk * TC:(kk + 1) * TC],
                            i0=y_tot[:], i1=sz_t[kkb][:]:
                            e.tensor_tensor(out=o, in0=i0, in1=i1, op=ALU.mult),
                            waits=(lu(("yg", sc % 2)) if kk == 0 else []), inc="s_dve")
                last_use[("sz_t", kkb)] = [("s_dve", cyg)]

            load_k(0)
            load_k(1)
            for k in range(KD):
                kb = k % 2
                cdt, cxs, csz = loaded.pop(k)
                cdx = p.add("gpsimd", lambda e, o=dtx_t[kb][:], i0=dt_t[kb][:], i1=xs_t[kb][:]:
                            e.tensor_tensor(out=o, in0=i0, in1=i1, op=ALU.mult),
                            waits=[cdt, cxs] + lu(("dtx", kb)), inc="s_gp")
                ybank = banks[6] if k % 2 == 0 else banks[7]
                ykey = ("bank", 6) if k % 2 == 0 else ("bank", 7)
                cz = None
                cpe_g = None
                cbb_last = None
                cbb_of = {}

                def emit_bb(g):
                    bbuf = [bb_d, bb_g2, bb_d2][g % 3]
                    bkey = ("bbg", g % 3)
                    cbb = p.add("gpsimd", lambda e, o=bbuf[:],
                                i0=dtx_t[kb][:, None, :].broadcast_to([128, 4, TC]),
                                i1=brep[:].rearrange("p (n t) -> p n t", n=N)[:, g * 4:(g + 1) * 4, :]:
                                e.tensor_tensor(out=o.rearrange("p (n t) -> p n t", n=4), in0=i0, in1=i1, op=ALU.mult),
                                waits=(([cbp[g // 2]] if k <= 1 else []) +
                                       lu(bkey)),
                                inc="s_gp")
                    cbb_of[g] = (cbb, bbuf, bkey)
                    return cbb

                emit_bb(0)
                emit_bb(1)
                for g in range(4):
                    abuf = ab_g[g % 2]
                    caa = None
                    for j in range(4):
                        n = g * 4 + j
                        w0 = []
                        if j == 0:
                            w0 += lu(("ab", g % 2))
                            if g == 0:
                                w0 += [cdt]
                        caa = p.add("scalar", lambda e, o=abuf[:, j * TC:(j + 1) * TC],
                                    i=dt_t[kb][:], s=a_neg[:, k * N + n:k * N + n + 1]:
                                    e.activation(out=o, in_=i, func=AF.Exp, scale=s),
                                    waits=w0, inc="s_act" if j == 3 else None)
                    cbb, bbuf, bkey = cbb_of[g]
                    cbb_last = cbb
                    csn = None
                    for j in range(4):
                        n = g * 4 + j
                        w0 = []
                        if j == 0:
                            w0 = [("s_act", caa), ("s_gp", cbb)] + lu(("h", g))
                        csn = p.add("vector", lambda e, o=h_slab[:, n * TC:(n + 1) * TC],
                                    a=abuf[:, j * TC:(j + 1) * TC], b2=bbuf[:, j * TC:(j + 1) * TC],
                                    ini=state[:, k * N + n:k * N + n + 1]:
                                    e.tensor_tensor_scan(out=o, data0=a, data1=b2, initial=ini,
                                                         op0=ALU.mult, op1=ALU.add),
                                    waits=w0, inc="s_dve" if j == 3 else None)
                    last_use[("ab", g % 2)] = [("s_dve", csn)]
                    last_use[bkey] = [("s_dve", csn)]
                    if g == 0 and pend[0] is not None:
                        emit_yevac(pend[0])
                        pend[0] = None
                        if k + 1 < KD:
                            load_k(k + 1)
                    if g < 2:
                        emit_bb(g + 2)
                    zi = (k * 4 + g) % 3
                    zb = z4[zi]
                    cz = p.add("vector", lambda e, o=zb[:],
                               i0=h_slab[:, (g * 4) * TC:(g * 4 + 4) * TC],
                               i1=crep[:, (g * 4) * TC:(g * 4 + 4) * TC]:
                               e.tensor_tensor(out=o, in0=i0, in1=i1, op=ALU.mult),
                               waits=(([ccp[g // 2]] if k == 0 else []) +
                                      lu(("z4", zi))),
                               inc="s_dve")
                    last_use[("h", g)] = [("s_dve", cz)]
                    for j in range(4):
                        nn = g * 4 + j
                        w0 = [("s_dve", cz)] if j == 0 else []
                        if nn == 0:
                            w0 += lu(ykey)
                        cpe_g = p.add("tensor", lambda e, o=ybank[:, 0:TC], idn=ident_b[:],
                                      r=zb[:, j * TC:(j + 1) * TC],
                                      s=(nn == 0), st=(nn == N - 1):
                                      e.matmul(o, idn, r, start=s, stop=st),
                                      waits=w0, inc="s_pe" if j == 3 else None)
                    last_use[("z4", zi)] = [("s_pe", cpe_g)]
                    rem = (KD - k) * 4 - g - 1
                    if bg and rem > 0:
                        nbg = (len(bg) + rem - 1) // rem
                        for _ in range(min(nbg, len(bg))):
                            bg.pop(0)()
                cst = p.add("vector", lambda e, o=state[:, k * N:(k + 1) * N],
                            i=h_slab[:].rearrange("p (n t) -> p n t", n=N)[:, :, TC - 1]:
                            e.tensor_copy(o, i), inc="s_dve")
                last_use[("dt_t", kb)] = [("s_act", caa)]
                last_use[("dtx", kb)] = [("s_gp", cbb_last)]
                pend[0] = (k, kb, cpe_g, csz)
            emit_yevac(pend[0])
            pend[0] = None
            c_yg[sc] = p.c("s_dve")
            last_use["brep"] = [("s_dve", p.c("s_dve"))]
            last_use["crep"] = [("s_gp", p.c("s_gp"))]

        for u in stage1_units(0):
            u()
        p.mark("stage1_0_done")
        bg = []
        s1u = []
        for sc in range(nsc):
            cp = sc // 2
            if sc > 0:
                bg += post_units(sc - 1)
            if sc % 2 == 0 and cp + 1 < nmc:
                s1u = stage1_units(cp + 1)
            if s1u:
                take = (len(s1u) + 1) // 2 if sc % 2 == 0 else len(s1u)
                bg += s1u[:take]
                s1u = s1u[take:]
            scan_chunk(sc, bg)
            p.mark(f"scan_{sc}_done")
        for u in bg:
            u()
        for u in post_units(nsc - 1):
            u()
        p.mark("tail_done")

        if TRUNCATE is None:
            p.add("sync", lambda e: e.nop(), waits=[("d_out", p.c("d_out"))])

        with nc.Block() as block:
            p.emit(block, sems)

    return nc


_NC_CACHE = None


def kernel(L0_attn_x, L0_attn_W_in, L0_attn_A_log, L0_attn_W_B, L0_attn_W_C,
           L0_attn_W_dt, L0_attn_b_dt, L0_attn_D, L0_attn_output_gate,
           L0_attn_W_out, L0_ffn_cond, L0_ffn_mask, L0_ffn_W_shared,
           L0_ffn_W_expert, L0_ffn_W_gate, L0_ffn_bias):
    global _NC_CACHE
    if _NC_CACHE is None:
        _NC_CACHE = build_nc()
    nc = _NC_CACHE

    f32 = np.float32
    cont = np.ascontiguousarray
    winT = cont(L0_attn_W_in.T.astype(f32, copy=False))
    wdtT = cont(L0_attn_W_dt.T.astype(f32, copy=False))
    wbcT = cont(np.concatenate([L0_attn_W_B, L0_attn_W_C], axis=0).T.astype(f32, copy=False))
    woutT = cont(L0_attn_W_out.T.astype(f32, copy=False))
    wsT = cont(L0_ffn_W_shared.T.astype(f32, copy=False))
    weT = cont(L0_ffn_W_expert.T.astype(f32, copy=False))

    common = dict(winT=winT, wdtT=wdtT, wbcT=wbcT, woutT=woutT, wsT=wsT, weT=weT,
                  alog=cont(L0_attn_A_log.astype(f32, copy=False)),
                  bdt=cont(L0_attn_b_dt.astype(f32, copy=False)),
                  dvec=cont(L0_attn_D.astype(f32, copy=False)),
                  bias=cont(L0_ffn_bias.astype(f32, copy=False)),
                  gate=cont(L0_attn_output_gate.astype(f32, copy=False)).reshape(1, 1),
                  consts=np.tile(np.array([1e-5, -0.5, 1.0 / D, -1.0, 1.0, -1.0 / D, 0.0, 0.0],
                                          dtype=f32), (128, 1)),
                  identf=np.eye(128, dtype=f32),
                  identb=np.eye(128, dtype=ml_dtypes.bfloat16))
    in_maps = []
    for i in range(8):
        m = dict(common)
        m["x"] = cont(L0_attn_x[i].astype(f32, copy=False))
        in_maps.append(m)

    res = run_bass_kernel_spmd(nc, in_maps, core_ids=list(range(8)))
    return np.stack([res.results[i]["out"] for i in range(8)], axis=0)


# version 16
# speedup vs baseline: 1.0372x; 1.0191x over previous
import numpy as np
import ml_dtypes
import contextlib
import concourse.bass as bass
import concourse.mybir as mybir
from concourse.bass_utils import run_bass_kernel_spmd

F32 = mybir.dt.float32
BF16 = mybir.dt.bfloat16
AF = mybir.ActivationFunctionType
ALU = mybir.AluOpType
AX = mybir.AxisListType

T, D = 2048, 1024
E = 2048
N = 16
MC = 512
NMC = T // MC
TC = 256
NSC = T // TC
KD = E // 128
KDM = D // 128

TRUNCATE = None
DEBUG_SCRATCH = False
MARKS = {}

ENG_SEM = {"scalar": "s_act", "vector": "s_dve", "tensor": "s_pe", "gpsimd": "s_gp"}


DMA_ROT = {"d_x": 1, "d_w": 3, "d_scr": 3, "d_brep": 4, "d_wr": 2, "d_out": 1}


def dma_sem_names():
    out = []
    for base, n in DMA_ROT.items():
        if n == 1:
            out.append(base)
        else:
            out.extend(f"{base}{i}" for i in range(n))
    return out


class Prog:

    def __init__(self):
        self.ops = {k: [] for k in ("sync", "scalar", "vector", "tensor", "gpsimd")}
        self.cnt = {}
        self.gidx = 0
        self.rrc = {}

    def add(self, eng, fn, waits=(), inc=None, inc_k=1):
        self.gidx += 1
        new = None
        if inc is not None:
            self.cnt[inc] = self.cnt.get(inc, 0) + inc_k
            new = self.cnt[inc]
        if TRUNCATE is not None and self.gidx > TRUNCATE:
            return new
        own = ENG_SEM.get(eng)
        for (sem, val) in waits:
            if val and val > 0 and sem != own:
                self.ops[eng].append(("wait", sem, val))
        self.ops[eng].append(("op", fn, inc, new, inc_k))
        return new

    def dma(self, eng, out, in_, waits=(), inc=None, slow=False):
        def fn(e, out=out, in_=in_, slow=slow):
            if slow:
                return e.dma_start(out=out, in_=in_, allow_slow_non_contiguous=True)
            return e.dma_start(out=out, in_=in_)
        waits = list(waits)
        n = DMA_ROT.get(inc, 1)
        if n > 1:
            idx = self.rrc.get(inc, 0)
            self.rrc[inc] = (idx + 1) % n
            sem = f"{inc}{idx}"
        else:
            sem = inc
        prev = self.c(sem)
        if prev:
            waits.append((sem, prev))
        val = self.add(eng, fn, waits=waits, inc=sem, inc_k=16)
        return (sem, val)

    def c(self, sem):
        return self.cnt.get(sem, 0)

    def cs(self, base):
        n = DMA_ROT.get(base, 1)
        names = [base] if n == 1 else [f"{base}{i}" for i in range(n)]
        return [(s, self.cnt[s]) for s in names if self.cnt.get(s, 0) > 0]

    def mark(self, name):
        MARKS[name] = self.gidx

    def emit(self, block, sems):
        waited = {}
        for eng, items in self.ops.items():
            for it in items:
                if it[0] == "wait":
                    waited.setdefault(it[1], set()).add(it[2])
        rank = {}
        for sem, vals in waited.items():
            if not sem.startswith("d_"):
                rank[sem] = {v: i + 1 for i, v in enumerate(sorted(vals))}

        def mk(eng_name):
            def body(engobj):
                for item in self.ops[eng_name]:
                    if item[0] == "wait":
                        _, sem, val = item
                        if sem in rank:
                            val = rank[sem][val]
                        engobj.wait_ge(sems[sem], val)
                    else:
                        _, fn, inc, val, inc_k = item
                        instr = fn(engobj)
                        if inc is not None:
                            if inc.startswith("d_"):
                                instr.then_inc(sems[inc], inc_k)
                            elif val in rank.get(inc, ()):
                                instr.then_inc(sems[inc], inc_k)
            return body
        block.sync(mk("sync"))
        block.scalar(mk("scalar"))
        block.vector(mk("vector"))
        block.tensor(mk("tensor"))
        block.gpsimd(mk("gpsimd"))


def build_nc():
    nmc = T // MC
    nsc = T // TC
    nc = bass.Bass("TRN2", target_bir_lowering=False, detect_race_conditions=False)

    x_h = nc.dram_tensor("x", [T, D], F32, kind="ExternalInput").ap()
    winT_h = nc.dram_tensor("winT", [D, 2 * E], F32, kind="ExternalInput").ap()
    wdtT_h = nc.dram_tensor("wdtT", [E, E], F32, kind="ExternalInput").ap()
    wbcT_h = nc.dram_tensor("wbcT", [E, 2 * N], F32, kind="ExternalInput").ap()
    woutT_h = nc.dram_tensor("woutT", [E, D], F32, kind="ExternalInput").ap()
    wsT_h = nc.dram_tensor("wsT", [D, D], F32, kind="ExternalInput").ap()
    weT_h = nc.dram_tensor("weT", [D, D], F32, kind="ExternalInput").ap()
    alog_h = nc.dram_tensor("alog", [E, N], F32, kind="ExternalInput").ap()
    bdt_h = nc.dram_tensor("bdt", [E], F32, kind="ExternalInput").ap()
    dvec_h = nc.dram_tensor("dvec", [E], F32, kind="ExternalInput").ap()
    bias_h = nc.dram_tensor("bias", [D], F32, kind="ExternalInput").ap()
    gate_h = nc.dram_tensor("gate", [1, 1], F32, kind="ExternalInput").ap()
    consts_h = nc.dram_tensor("consts", [128, 8], F32, kind="ExternalInput").ap()
    identf_h = nc.dram_tensor("identf", [128, 128], F32, kind="ExternalInput").ap()
    identb_h = nc.dram_tensor("identb", [128, 128], BF16, kind="ExternalInput").ap()

    out_h = nc.dram_tensor("out", [T, D], F32, kind="ExternalOutput").ap()

    knd = "ExternalOutput" if DEBUG_SCRATCH else "Internal"
    xs_sc = nc.dram_tensor("xs_sc", [E, T], F32, kind=knd).ap()
    dt_sc = nc.dram_tensor("dt_sc", [E, T], F32, kind=knd).ap()
    sz_sc = nc.dram_tensor("sz_sc", [E, T], BF16, kind=knd).ap()
    bc_sc = nc.dram_tensor("bc_sc", [2 * N, T], F32, kind=knd).ap()
    sgate_sc = nc.dram_tensor("sgate_sc", [1, 1], F32, kind="Internal").ap()
    lnsc = nc.dram_tensor("lnsc", [128, 8], F32, kind="Internal").ap()
    lnsc2 = nc.dram_tensor("lnsc2", [128, 4], F32, kind="Internal").ap()
    winb_sc = nc.dram_tensor("winb_sc", [2 * KD, 128, KDM * 128], BF16, kind="Internal").ap()
    wdtb_sc = nc.dram_tensor("wdtb_sc", [KD, 128, KD * 128], BF16, kind="Internal").ap()
    woutm_sc = nc.dram_tensor("woutm_sc", [KDM, 128, KD * 128], BF16, kind=knd).ap()
    wcm_sc = nc.dram_tensor("wcm_sc", [KDM, 128, KDM * 128], BF16, kind=knd).ap()

    nc.reset()

    p = Prog()
    sem_names = dma_sem_names() + ["s_act", "s_dve", "s_pe", "s_gp"]

    stack = contextlib.ExitStack()
    sbuf = lambda name, shape, dt: stack.enter_context(nc.sbuf_tensor(name, shape, dt)).ap()

    with stack:
        a_neg = sbuf("a_neg", [128, KD * N], F32)
        bdt_sb = sbuf("bdt_sb", [128, KD], F32)
        d_sb = sbuf("d_sb", [128, KD], F32)
        bias_sb = sbuf("bias_sb", [128, KDM], F32)
        ident_f = sbuf("ident_f", [128, 128], F32)
        ident_b = sbuf("ident_b", [128, 128], BF16)
        g_sb = sbuf("g_sb", [1, 4], F32)
        csts = sbuf("cst", [128, 8], F32)
        ln_tmp = sbuf("ln_tmp", [128, 2], F32)
        ln_rb = sbuf("ln_rb", [128, 8], F32)
        ln_ra = sbuf("ln_ra", [128, 16], F32)
        ln_rb2 = sbuf("ln_rb2", [128, 4], F32)
        ln_ra2 = sbuf("ln_ra2", [128, 8], F32)
        sbc_sb = sbuf("sbc_sb", [128, 1], F32)

        wbc_bf = sbuf("wbc_bf", [128, KD * 2 * N], BF16)
        wstage = sbuf("wstage", [128, KD * 128], F32)
        wbf = [sbuf(f"wbf{i}", [128, KD * 128], BF16) for i in range(2)]
        wout_m = [sbuf(f"woutm{i}", [128, KD * 128], BF16) for i in range(2)]
        wcm_m = [sbuf(f"wcmm{i}", [128, KDM * 128], BF16) for i in range(2)]

        x_sb = sbuf("x_sb", [128, D], F32)
        xc_sb = sbuf("xc_sb", [128, D], F32)
        xhat = sbuf("xhat", [128, D], F32)
        xhatT = sbuf("xhatT", [128, KDM * MC], BF16)
        xsbf = sbuf("xsbf", [128, KD * MC], BF16)
        xs_f = [sbuf(f"xs_f{i}", [128, MC], F32) for i in range(2)]
        dt_f = [sbuf(f"dt_f{i}", [128, MC], F32) for i in range(2)]
        szb = [sbuf(f"szb{i}", [128, MC], BF16) for i in range(2)]
        bc_f = sbuf("bc_f", [32, MC], F32)
        ln_r = sbuf("ln_r", [128, 16], F32)
        ln_r2 = sbuf("ln_r2", [128, 8], F32)

        brep = sbuf("brep", [128, N * TC], F32)
        crep = sbuf("crep", [128, N * TC], F32)
        dt_t = [sbuf(f"dt_t{i}", [128, TC], F32) for i in range(2)]
        xs_t = [sbuf(f"xs_t{i}", [128, TC], F32) for i in range(2)]
        sz_t = [sbuf(f"sz_t{i}", [128, TC], BF16) for i in range(2)]
        dtx_t0 = sbuf("dtx0", [128, TC], F32)
        dtx_t1 = sbuf("dtx1", [128, TC], F32)
        ab_g = [sbuf(f"ab{i}", [128, 4 * TC], F32) for i in range(2)]
        bb_g2 = sbuf("bb_g2", [128, 4 * TC], F32)
        h_slab = sbuf("h_slab", [128, N * TC], F32)
        z4 = [sbuf(f"z4{i}", [128, 4 * TC], BF16) for i in range(3)]
        y_tot = sbuf("y_tot", [128, TC], F32)
        yg = [sbuf(f"yg{i}", [128, KD * TC], BF16) for i in range(2)]
        state = sbuf("state", [128, KD * N], F32)

        at_f = [sbuf(f"at_f{i}", [128, TC], F32) for i in range(2)]
        x1_sb = sbuf("x1_sb", [128, 2 * D], F32)
        xr_sb = [sbuf(f"xr_sb{i}", [128, D], F32) for i in range(2)]
        xh2 = sbuf("xh2", [128, D], F32)
        xh2d = sbuf("xh2d", [128, D], F32)
        xh2T = sbuf("xh2T", [128, KDM * TC], BF16)
        ff_f = [sbuf(f"ff_f{i}", [128, TC], F32) for i in range(2)]
        x2_sb = sbuf("x2_sb", [128, D], F32)

        bb_d = wstage[:, 0:4 * TC]
        bb_d2 = wstage[:, 4 * TC:8 * TC]
        dtx_t = [dtx_t0, dtx_t1]

        banks = [stack.enter_context(nc.psum_tensor(f"ps{i}", [128, 512], F32)).ap()
                 for i in range(8)]

        sems = {}
        for s in sem_names:
            sems[s] = stack.enter_context(nc.semaphore(s))

        last_use = {}
        def lu(key):
            return list(last_use.get(key, []))

        p.add("vector", lambda e, o=state[:]: e.memset(o, 0.0), inc="s_dve")

        for apx in (ab_g[0][:], ab_g[1][:], xhat[:], xc_sb[:], xh2[:], xh2d[:],
                    xhatT[:], xh2T[:], xs_f[0][:], xs_f[1][:], dt_f[0][:],
                    dt_f[1][:], szb[0][:], szb[1][:], bc_f[:], at_f[0][:],
                    at_f[1][:], ff_f[0][:], ff_f[1][:], a_neg[:], ln_tmp[:],
                    ln_ra[:], ln_ra2[:], g_sb[0:1, 1:4]):
            p.add("scalar", lambda e, o=apx: e.memzero(o), inc=None)
        for apx in (wbf[0], wbf[1], wbc_bf, h_slab, x1_sb, x2_sb, y_tot,
                    ln_r, ln_r2, z4[0], z4[1], z4[2], yg[0], yg[1]):
            p.add("vector", lambda e, o=apx[:]: e.memset(o, 0.0), inc=None)
        for apx in (dtx_t0, dtx_t1, bb_g2):
            p.add("gpsimd", lambda e, o=apx[:]: e.memset(o, 0.0), inc=None)
        p.add("scalar", lambda e, o=xsbf[:]: e.memzero(o), inc=None)
        cms = None
        for i in range(8):
            cms = p.add("vector", lambda e, o=banks[i][:]: e.memset(o, 0.0),
                        inc="s_dve" if i == 7 else None)
        for i in range(8):
            last_use[("bank", i)] = [("s_dve", cms)]

        p.dma("sync", ident_f[:], identf_h, inc="d_w")
        p.dma("sync", ident_b[:], identb_h, inc="d_w")
        p.dma("sync", csts[:], consts_h, inc="d_w")
        p.dma("sync", bdt_sb[:], bdt_h.rearrange("(k p) -> p k", p=128), inc="d_w", slow=True)
        p.dma("sync", d_sb[:], dvec_h.rearrange("(k p) -> p k", p=128), inc="d_w", slow=True)
        p.dma("sync", bias_sb[:], bias_h.rearrange("(k p) -> p k", p=128), inc="d_w", slow=True)
        p.dma("sync", g_sb[0:1, 0:1], gate_h, inc="d_w")
        c_w0 = p.cs("d_w")
        ca0 = p.dma("sync", wstage[:, 0:KD * N].rearrange("p (k n) -> p k n", k=KD),
                    alog_h.rearrange("(k p) n -> p k n", p=128), inc="d_w")
        ca1 = p.add("scalar", lambda e, o=a_neg[:], i=wstage[:, 0:KD * N]:
                    e.activation(out=o, in_=i, func=AF.Exp),
                    waits=[ca0], inc="s_act")
        last_use["wstage"] = [("s_act", ca1)]
        p.add("vector", lambda e, o=a_neg[:], i=a_neg[:]: e.tensor_scalar_mul(o, i, -1.0),
              waits=[("s_act", ca1)], inc="s_dve")
        cga = p.add("scalar", lambda e, o=g_sb[0:1, 1:2], i=g_sb[0:1, 0:1]:
                    e.activation(out=o, in_=i, func=AF.Sigmoid),
                    waits=c_w0, inc="s_act")
        csg = p.dma("sync", sgate_sc, g_sb[0:1, 1:2], waits=[("s_act", cga)], inc="d_wr")
        c_sbc = p.dma("sync", sbc_sb[:], sgate_sc.broadcast_to([128, 1]),
                      waits=[csg], inc="d_w")

        for k in range(KDM):
            c1 = p.dma("sync", wstage[:, 0:D], wsT_h[k * 128:(k + 1) * 128, :],
                       waits=lu("wstage"), inc="d_w")
            cw = p.dma("sync", wstage[:, D:2 * D], weT_h[k * 128:(k + 1) * 128, :], inc="d_w")
            cv = p.add("vector", lambda e, o=wbf[k % 2][:, 0:D], i0=wstage[:, 0:D],
                       i1=wstage[:, D:2 * D]: e.tensor_tensor(out=o, in0=i0, in1=i1, op=ALU.add),
                       waits=[c1, cw] + lu(("wbf", k % 2)), inc="s_dve")
            last_use["wstage"] = [("s_dve", cv)]
            cd = p.dma("sync", wcm_sc[:, :, k * 128:(k + 1) * 128].rearrange("m p j -> p m j"),
                       wbf[k % 2][:, 0:D].rearrange("p (m j) -> p m j", m=KDM),
                       waits=[("s_dve", cv)], inc="d_wr", slow=True)
            last_use[("wbf", k % 2)] = [cd]
        cw = p.dma("sync", wstage[:, 0:KD * 32].rearrange("p (k m) -> p k m", k=KD),
                   wbcT_h.rearrange("(k p) m -> p k m", p=128),
                   waits=lu("wstage"), inc="d_w")
        cv = p.add("vector", lambda e, o=wbc_bf[:], i=wstage[:, 0:KD * 32]: e.tensor_copy(o, i),
                   waits=[cw], inc="s_dve")
        last_use["wstage"] = [("s_dve", cv)]
        for k in range(KD):
            cw = p.dma("sync", wstage[:, 0:D], woutT_h[k * 128:(k + 1) * 128, :],
                       waits=lu("wstage"), inc="d_w")
            cv = p.add("vector", lambda e, o=wbf[k % 2][:, 0:D], i=wstage[:, 0:D], s=sbc_sb[:, 0:1]:
                       e.tensor_scalar(out=o, in0=i, scalar1=s, scalar2=None, op0=ALU.mult),
                       waits=[cw, c_sbc] + lu(("wbf", k % 2)), inc="s_dve")
            last_use["wstage"] = [("s_dve", cv)]
            cd = p.dma("sync", woutm_sc[:, :, k * 128:(k + 1) * 128].rearrange("m p j -> p m j"),
                       wbf[k % 2][:, 0:D].rearrange("p (m j) -> p m j", m=KDM),
                       waits=[("s_dve", cv)], inc="d_wr", slow=True)
            last_use[("wbf", k % 2)] = [cd]
        c_wm_w = p.cs("d_wr")
        c_wcast_w = [0]

        c_xs_w = {}
        c_dt_w = {}
        c_sz_w = {}
        c_bc_w = {}
        c_yg = {}

        def stage1_units(cp):
            units = []
            t0 = cp * MC
            ctx = {}

            def lnA(tt, t0=t0):
                def f():
                    r0 = t0 + tt * 128
                    sqb = xc_sb if tt % 2 == 0 else xhat
                    sqk = "xc_sb" if tt % 2 == 0 else "xhat"
                    cx = p.dma("sync", x_sb[:], x_h[r0:r0 + 128, :],
                               waits=lu("x_sb"), inc="d_x")
                    p.add("vector", lambda e, o=ln_r[:, 2 * tt:2 * tt + 1], i=x_sb[:]:
                          e.reduce_sum(o, i, axis=AX.X),
                          waits=[cx] + (lu("lnr1") if tt == 0 else []),
                          inc="s_dve")
                    c_sq = p.add("scalar", lambda e, o=sqb[:], i=x_sb[:]:
                                 e.activation(out=o, in_=i, func=AF.Square),
                                 waits=[cx] + lu(sqk), inc="s_act")
                    cr2 = p.add("vector", lambda e, o=ln_r[:, 2 * tt + 1:2 * tt + 2], i=sqb[:]:
                                e.reduce_sum(o, i, axis=AX.X),
                                waits=[("s_act", c_sq)], inc="s_dve")
                    ctx["c_sums"] = cr2
                    last_use[sqk] = [("s_dve", cr2)]
                    last_use["x_sb"] = [("s_dve", cr2), ("s_act", c_sq)]
                return f

            def lnRT():
                def f():
                    cd1 = p.dma("sync", lnsc, ln_r[:, 0:8], waits=[("s_dve", ctx["c_sums"])],
                                inc="d_wr")
                    cd2 = p.dma("sync", ln_rb[:], lnsc,
                                waits=[cd1] + lu("ln_rb"), inc="d_w")
                    ctx["c_lnrt"] = cd2
                    last_use["lnr1"] = [cd1]
                return f

            def lnB(tt):
                def f():
                    if True:
                        w0 = [ctx["c_lnrt"]] if tt == 0 else []
                        p.add("scalar", lambda e, o=ln_ra[:, 4 * tt:4 * tt + 1],
                              i=ln_rb[:, 2 * tt:2 * tt + 1], s=csts[:, 5:6]:
                              e.activation(out=o, in_=i, func=AF.Identity, scale=s),
                              waits=w0, inc=None)
                        p.add("scalar", lambda e, o=ln_tmp[:, 0:1], i=ln_ra[:, 4 * tt:4 * tt + 1]:
                              e.activation(out=o, in_=i, func=AF.Square), inc=None)
                        p.add("scalar", lambda e, o=ln_tmp[:, 1:2], i=ln_tmp[:, 0:1],
                              s=csts[:, 3:4], bi=csts[:, 0:1]:
                              e.activation(out=o, in_=i, func=AF.Identity, scale=s, bias=bi), inc=None)
                        p.add("scalar", lambda e, o=ln_tmp[:, 0:1], i=ln_rb[:, 2 * tt + 1:2 * tt + 2],
                              s=csts[:, 2:3], bi=ln_tmp[:, 1:2]:
                              e.activation(out=o, in_=i, func=AF.Ln, scale=s, bias=bi), inc=None)
                        p.add("scalar", lambda e, o=ln_ra[:, 4 * tt + 2:4 * tt + 3], i=ln_tmp[:, 0:1],
                              s=csts[:, 1:2]:
                              e.activation(out=o, in_=i, func=AF.Exp, scale=s), inc=None)
                        ca = p.add("scalar", lambda e, o=ln_ra[:, 4 * tt + 3:4 * tt + 4],
                                   i=ln_ra[:, 4 * tt + 2:4 * tt + 3], s=ln_ra[:, 4 * tt:4 * tt + 1]:
                                   e.activation(out=o, in_=i, func=AF.Identity, scale=s),
                                   inc="s_act" if tt == 3 else None)
                    if tt == 3:
                        last_use["ln_rb"] = [("s_act", ca)]
                        last_use["xc_sb"] = [("s_act", ca)]
                        last_use["x_sb"] = [("s_act", ca)]
                return f

            def lnC(tt, t0=t0):
                def f():
                    r0 = t0 + tt * 128
                    cx = p.dma("sync", x_sb[:], x_h[r0:r0 + 128, :],
                               waits=lu("x_sb"), inc="d_x")
                    c_ph = p.add("scalar", lambda e, o=xhat[:], i=x_sb[:],
                                 s=ln_ra[:, 4 * tt + 2:4 * tt + 3],
                                 bi=ln_ra[:, 4 * tt + 3:4 * tt + 4]:
                                 e.activation(out=o, in_=i, func=AF.Identity, scale=s, bias=bi),
                                 waits=[cx] + lu("xhat"),
                                 inc="s_act")
                    last_use["x_sb"] = [("s_act", c_ph)]
                    xh_done = []
                    for b in range(2):
                        bk = banks[2 + b]
                        for q in range(4):
                            dk = b * 4 + q
                            cpe = p.add("tensor", lambda e, o=bk[:, q * 128:(q + 1) * 128],
                                        i=xhat[:, dk * 128:(dk + 1) * 128], idn=ident_f[:]:
                                        e.transpose(o, i, idn),
                                        waits=([("s_act", c_ph)] + lu(("bank", 2 + b))) if q == 0 else [],
                                        inc="s_pe" if q == 3 else None)
                        xh_done.append(("s_pe", cpe))
                        ca2 = p.add("scalar", lambda e,
                                    o=xhatT[:].rearrange("p (dk t) -> p dk t", dk=KDM)[:, b * 4:(b + 1) * 4, tt * 128:(tt + 1) * 128],
                                    i=bk[:]:
                                    e.activation(out=o, in_=i, func=AF.Copy),
                                    waits=[("s_pe", cpe)], inc="s_act")
                        last_use[("bank", 2 + b)] = [("s_act", ca2)]
                    last_use["xhat"] = xh_done
                    if tt == 3:
                        ctx["c_xhatT"] = p.c("s_act")
                return f

            def xz(eo, t0=t0, cp=cp):
                def f():
                    wb = wbf[eo % 2]
                    if cp == 0:
                        cw = p.dma("sync", wstage[:, 0:KDM * 128].rearrange("p (k m) -> p k m", k=KDM),
                                   winT_h.rearrange("(k p) m -> p k m", p=128)[:, :, eo * 128:(eo + 1) * 128],
                                   waits=lu("wstage"), inc="d_w")
                        cv = p.add("vector", lambda e, o=wb[:, 0:KDM * 128], i=wstage[:, 0:KDM * 128]:
                                   e.tensor_copy(o, i),
                                   waits=[cw] + lu(("wbf", eo % 2)), inc="s_dve")
                        last_use["wstage"] = [("s_dve", cv)]
                        p.dma("sync", winb_sc[eo], wb[:, 0:KDM * 128], waits=[("s_dve", cv)], inc="d_wr")
                        rdy = [("s_dve", cv)]
                    else:
                        cw = p.dma("sync", wb[:, 0:KDM * 128], winb_sc[eo],
                                   waits=c_wcast_w[0] + lu(("wbf", eo % 2)), inc="d_w")
                        rdy = [cw]
                    bi = eo % 2
                    bank = banks[bi]
                    cpe = None
                    for k in range(KDM):
                        w0 = (rdy + [("s_act", ctx["c_xhatT"])] + lu(("bank", bi))) if k == 0 else []
                        cpe = p.add("tensor", lambda e, o=bank[:], w=wb[:, k * 128:(k + 1) * 128],
                                    r=xhatT[:, k * MC:(k + 1) * MC], s=(k == 0), st=(k == KDM - 1):
                                    e.matmul(o, w, r, start=s, stop=st),
                                    waits=w0, inc="s_pe" if k == KDM - 1 else None)
                    last_use[("wbf", eo % 2)] = [("s_pe", cpe)]
                    if eo < KD:
                        fb = xs_f[eo % 2]
                        ca = p.add("scalar", lambda e, o=fb[:], i=bank[:]:
                                   e.activation(out=o, in_=i, func=AF.Copy),
                                   waits=[("s_pe", cpe)] + lu(("xs_f", eo % 2)), inc="s_act")
                        cv = p.add("scalar", lambda e, o=xsbf[:, eo * MC:(eo + 1) * MC], i=bank[:]:
                                   e.activation(out=o, in_=i, func=AF.Copy),
                                   waits=(lu("xsbf") if eo == 0 else []), inc="s_act")
                        cd = p.dma("sync", xs_sc[eo * 128:(eo + 1) * 128, t0:t0 + MC], fb[:],
                                   waits=[("s_act", ca)], inc="d_wr")
                        c_xs_w[(eo, cp)] = cd
                        last_use[("xs_f", eo % 2)] = [("s_act", ca), cd]
                        last_use[("bank", bi)] = [("s_act", ca)]
                        if eo == KD - 1:
                            ctx["c_xsbf_part"] = cv
                return f

            def xz_z(eo0, t0=t0, cp=cp):
                def f():
                    cpes = []
                    for eo in (eo0, eo0 + 1):
                        wb = wbf[eo % 2]
                        if cp == 0:
                            cw = p.dma("sync", wstage[:, 0:KDM * 128].rearrange("p (k m) -> p k m", k=KDM),
                                       winT_h.rearrange("(k p) m -> p k m", p=128)[:, :, eo * 128:(eo + 1) * 128],
                                       waits=lu("wstage"), inc="d_w")
                            cv = p.add("vector", lambda e, o=wb[:, 0:KDM * 128], i=wstage[:, 0:KDM * 128]:
                                       e.tensor_copy(o, i),
                                       waits=[cw] + lu(("wbf", eo % 2)), inc="s_dve")
                            last_use["wstage"] = [("s_dve", cv)]
                            p.dma("sync", winb_sc[eo], wb[:, 0:KDM * 128], waits=[("s_dve", cv)], inc="d_wr")
                            rdy = [("s_dve", cv)]
                        else:
                            cw = p.dma("sync", wb[:, 0:KDM * 128], winb_sc[eo],
                                       waits=c_wcast_w[0] + lu(("wbf", eo % 2)), inc="d_w")
                            rdy = [cw]
                        bi = eo % 2
                        bank = banks[bi]
                        cpe = None
                        for k in range(KDM):
                            w0 = (rdy + [("s_act", ctx["c_xhatT"])] + lu(("bank", bi))) if k == 0 else []
                            cpe = p.add("tensor", lambda e, o=bank[:], w=wb[:, k * 128:(k + 1) * 128],
                                        r=xhatT[:, k * MC:(k + 1) * MC], s=(k == 0), st=(k == KDM - 1):
                                        e.matmul(o, w, r, start=s, stop=st),
                                        waits=w0, inc="s_pe" if k == KDM - 1 else None)
                        last_use[("wbf", eo % 2)] = [("s_pe", cpe)]
                        cpes.append(cpe)
                    for i, eo in enumerate((eo0, eo0 + 1)):
                        bi = eo % 2
                        zb = szb[eo % 2]
                        ca = p.add("scalar", lambda e, o=zb[:], i=banks[bi][:]:
                                   e.activation(out=o, in_=i, func=AF.Silu),
                                   waits=[("s_pe", cpes[i])] + lu(("szb", eo % 2)), inc="s_act")
                        cd = p.dma("sync", sz_sc[(eo - KD) * 128:(eo - KD + 1) * 128, t0:t0 + MC], zb[:],
                                   waits=[("s_act", ca)], inc="d_wr")
                        c_sz_w[(eo - KD, cp)] = cd
                        last_use[("szb", eo % 2)] = [cd]
                        last_use[("bank", bi)] = [("s_act", ca)]
                    if eo0 + 1 == 2 * KD - 1:
                        ctx["c_xsbf"] = ctx["c_xsbf_part"]
                return f

            def dtm(eo, t0=t0, cp=cp):
                def f():
                    wb = wbf[eo % 2]
                    if cp == 0:
                        cw = p.dma("sync", wstage[:].rearrange("p (k m) -> p k m", k=KD),
                                   wdtT_h.rearrange("(k p) m -> p k m", p=128)[:, :, eo * 128:(eo + 1) * 128],
                                   waits=lu("wstage"), inc="d_w")
                        cv = p.add("vector", lambda e, o=wb[:], i=wstage[:]: e.tensor_copy(o, i),
                                   waits=[cw] + lu(("wbf", eo % 2)), inc="s_dve")
                        last_use["wstage"] = [("s_dve", cv)]
                        p.dma("sync", wdtb_sc[eo], wb[:], waits=[("s_dve", cv)], inc="d_wr")
                        rdy = [("s_dve", cv)]
                    else:
                        cw = p.dma("sync", wb[:], wdtb_sc[eo],
                                   waits=c_wcast_w[0] + lu(("wbf", eo % 2)), inc="d_w")
                        rdy = [cw]
                    bi = eo % 2
                    bank = banks[bi]
                    cpe = None
                    for k in range(KD):
                        w0 = (rdy + [("s_act", ctx["c_xsbf"])] + lu(("bank", bi))) if k == 0 else []
                        cpe = p.add("tensor", lambda e, o=bank[:], w=wb[:, k * 128:(k + 1) * 128],
                                    r=xsbf[:, k * MC:(k + 1) * MC], s=(k == 0), st=(k == KD - 1):
                                    e.matmul(o, w, r, start=s, stop=st),
                                    waits=w0, inc="s_pe" if k == KD - 1 else None)
                    last_use[("wbf", eo % 2)] = [("s_pe", cpe)]
                    fb = dt_f[eo % 2]
                    p.add("scalar", lambda e, o=xc_sb[:, 0:MC], i=bank[:], bi2=bdt_sb[:, eo:eo + 1]:
                          e.activation(out=o, in_=i, func=AF.Exp, bias=bi2),
                          waits=[("s_pe", cpe)] + lu("xc_sb"), inc="s_act")
                    ca = p.add("scalar", lambda e, o=fb[:], i=xc_sb[:, 0:MC], bi2=csts[:, 4:5]:
                               e.activation(out=o, in_=i, func=AF.Ln, bias=bi2),
                               waits=lu(("dt_f", eo % 2)), inc="s_act")
                    last_use["xc_sb"] = [("s_act", ca)]
                    cd = p.dma("sync", dt_sc[eo * 128:(eo + 1) * 128, t0:t0 + MC], fb[:],
                               waits=[("s_act", ca)], inc="d_wr")
                    c_dt_w[(eo, cp)] = cd
                    last_use[("dt_f", eo % 2)] = [cd]
                    last_use[("bank", bi)] = [("s_act", ca)]
                    if cp == 0 and eo == KD - 1:
                        c_wcast_w[0] = p.cs("d_wr")
                return f

            def bcm(t0=t0, cp=cp):
                def f():
                    bank = banks[0]
                    cpe = None
                    for k in range(KD):
                        w0 = lu(("bank", 0)) if k == 0 else []
                        cpe = p.add("tensor", lambda e, o=bank[0:32, :], w=wbc_bf[:, k * 32:(k + 1) * 32],
                                    r=xsbf[:, k * MC:(k + 1) * MC], s=(k == 0), st=(k == KD - 1):
                                    e.matmul(o, w, r, start=s, stop=st),
                                    waits=w0, inc="s_pe" if k == KD - 1 else None)
                    last_use["xsbf"] = [("s_pe", cpe)]
                    ca = p.add("scalar", lambda e, o=bc_f[:], i=bank[0:32, :]:
                               e.activation(out=o, in_=i, func=AF.Copy),
                               waits=[("s_pe", cpe)] + lu("bc_f"), inc="s_act")
                    cd = p.dma("sync", bc_sc[:, t0:t0 + MC], bc_f[:], waits=[("s_act", ca)], inc="d_wr")
                    last_use["bc_f"] = [cd]
                    last_use[("bank", 0)] = [("s_act", ca)]
                    c_bc_w[cp] = cd
                return f

            for tt in range(4):
                units.append(lnA(tt))
            units.append(lnRT())
            for tt in range(4):
                units.append(lnB(tt))
            for tt in range(4):
                units.append(lnC(tt))
            for eo in range(KD):
                units.append(xz(eo))
            for eo in range(KD, 2 * KD, 2):
                units.append(xz_z(eo))
            for eo in range(KD):
                units.append(dtm(eo))
            units.append(bcm())
            return units

        def post_units(sc):
            units = []
            s0 = sc * TC
            ctx = {}

            def attn_m(m, sc=sc, s0=s0):
                def f():
                    wm = wout_m[m % 2]
                    cw = p.dma("sync", wm[:], woutm_sc[m],
                               waits=c_wm_w + lu(("wout_m", m % 2)), inc="d_w")
                    bi = 4 + (m % 2)
                    pa = banks[bi][:, 0:TC]
                    cpm = None
                    for k in range(KD):
                        w0 = ([cw, ("s_dve", c_yg[sc])] + lu(("bank", bi))) if k == 0 else []
                        cpm = p.add("tensor", lambda e, o=pa, w=wm[:, k * 128:(k + 1) * 128],
                                    r=yg[sc % 2][:, k * TC:(k + 1) * TC], s=(k == 0), st=(k == KD - 1):
                                    e.matmul(o, w, r, start=s, stop=st),
                                    waits=w0, inc="s_pe" if k == KD - 1 else None)
                    last_use[("wout_m", m % 2)] = [("s_pe", cpm)]
                    ca = p.add("scalar", lambda e, o=at_f[m % 2][:], i=pa:
                               e.activation(out=o, in_=i, func=AF.Copy),
                               waits=[("s_pe", cpm)] + lu(("at_f", m % 2)), inc="s_act")
                    last_use[("bank", bi)] = [("s_act", ca)]
                    ctx[("at_evac", m)] = ca
                    if m == KDM - 1:
                        last_use[("yg", sc % 2)] = [("s_pe", cpm)]
                return f

            def xr_load(sc=sc, s0=s0):
                def f():
                    ctx["cx0"] = p.dma("sync", xr_sb[0][:], x_h[s0:s0 + 128, :],
                                       waits=lu(("xr", 0)), inc="d_x")
                    ctx["cx1"] = p.dma("sync", xr_sb[1][:], x_h[s0 + 128:s0 + 256, :],
                                       waits=lu(("xr", 1)), inc="d_x")
                return f

            def x1_pair(pr, sc=sc):
                def f():
                    tb = 2 + pr % 2
                    cpe = None
                    for j in range(2):
                        m = 2 * pr + j
                        for tt in range(2):
                            q = j * 2 + tt
                            cpe = p.add("tensor", lambda e, o=banks[tb][:, q * 128:(q + 1) * 128],
                                        i=at_f[m % 2][:, tt * 128:(tt + 1) * 128], idn=ident_f[:]:
                                        e.transpose(o, i, idn),
                                        waits=([("s_act", ctx[("at_evac", 2 * pr + 1)])] +
                                               lu(("bank", tb))) if q == 0 else [],
                                        inc="s_pe" if (j == 1 and tt == 1) else None)
                    cv = None
                    for j in range(2):
                        m = 2 * pr + j
                        for tt in range(2):
                            q = j * 2 + tt
                            cv = p.add("vector",
                                       lambda e, o=x1_sb[:, tt * D + m * 128:tt * D + (m + 1) * 128],
                                       i0=banks[tb][:, q * 128:(q + 1) * 128],
                                       i1=xr_sb[tt][:, m * 128:(m + 1) * 128]:
                                       e.tensor_tensor(out=o, in0=i0, in1=i1, op=ALU.add),
                                       waits=(([("s_pe", cpe), ctx["cx0"], ctx["cx1"]] +
                                               (lu("x1_sb") if pr == 0 else [])) if q == 0 else []),
                                       inc="s_dve" if (j == 1 and tt == 1) else None)
                    last_use[("bank", tb)] = [("s_dve", cv)]
                    last_use[("at_f", 0)] = [("s_pe", cpe)]
                    last_use[("at_f", 1)] = [("s_pe", cpe)]
                    ctx.setdefault("x1_done", []).append(cv)
                    if pr == 3:
                        ctx["c_x1"] = max(ctx["x1_done"])
                        last_use[("xr", 0)] = [("s_dve", ctx["c_x1"])]
                        last_use[("xr", 1)] = [("s_dve", ctx["c_x1"])]
                return f

            def ln2A():
                def f():
                    for tt in range(2):
                        sqb = xh2d if tt == 0 else xh2
                        sqk = "xh2d" if tt == 0 else "xh2"
                        p.add("vector", lambda e, o=ln_r2[:, 2 * tt:2 * tt + 1],
                              i=x1_sb[:, tt * D:(tt + 1) * D]:
                              e.reduce_sum(o, i, axis=AX.X),
                              waits=([("s_dve", ctx["c_x1"])] if tt == 0 else []) +
                                    (lu("lnr2") if tt == 0 else []),
                              inc="s_dve")
                        csq2 = p.add("scalar", lambda e, o=sqb[:], i=x1_sb[:, tt * D:(tt + 1) * D]:
                                     e.activation(out=o, in_=i, func=AF.Square),
                                     waits=([("s_dve", ctx["c_x1"])] if tt == 0 else []) + lu(sqk),
                                     inc="s_act")
                        cr2 = p.add("vector", lambda e, o=ln_r2[:, 2 * tt + 1:2 * tt + 2], i=sqb[:]:
                                    e.reduce_sum(o, i, axis=AX.X),
                                    waits=[("s_act", csq2)], inc="s_dve")
                        ctx["c_sums2"] = cr2
                        last_use[sqk] = [("s_dve", cr2)]
                return f

            def ln2RT():
                def f():
                    cd1 = p.dma("sync", lnsc2, ln_r2[:, 0:4], waits=[("s_dve", ctx["c_sums2"])],
                                inc="d_wr")
                    cd2 = p.dma("sync", ln_rb2[:], lnsc2,
                                waits=[cd1] + lu("ln_rb2"), inc="d_w")
                    ctx["c_lnrt2"] = cd2
                    last_use["lnr2"] = [cd1]
                return f

            def ln2BC(tt):
                def f():
                    if True:
                        w0 = [ctx["c_lnrt2"]] if tt == 0 else []
                        p.add("scalar", lambda e, o=ln_ra2[:, 4 * tt:4 * tt + 1],
                              i=ln_rb2[:, 2 * tt:2 * tt + 1], s=csts[:, 5:6]:
                              e.activation(out=o, in_=i, func=AF.Identity, scale=s),
                              waits=w0, inc=None)
                        p.add("scalar", lambda e, o=ln_tmp[:, 0:1], i=ln_ra2[:, 4 * tt:4 * tt + 1]:
                              e.activation(out=o, in_=i, func=AF.Square), inc=None)
                        p.add("scalar", lambda e, o=ln_tmp[:, 1:2], i=ln_tmp[:, 0:1],
                              s=csts[:, 3:4], bi=csts[:, 0:1]:
                              e.activation(out=o, in_=i, func=AF.Identity, scale=s, bias=bi), inc=None)
                        p.add("scalar", lambda e, o=ln_tmp[:, 0:1], i=ln_rb2[:, 2 * tt + 1:2 * tt + 2],
                              s=csts[:, 2:3], bi=ln_tmp[:, 1:2]:
                              e.activation(out=o, in_=i, func=AF.Ln, scale=s, bias=bi), inc=None)
                        p.add("scalar", lambda e, o=ln_ra2[:, 4 * tt + 2:4 * tt + 3], i=ln_tmp[:, 0:1],
                              s=csts[:, 1:2]:
                              e.activation(out=o, in_=i, func=AF.Exp, scale=s), inc=None)
                        ca = p.add("scalar", lambda e, o=ln_ra2[:, 4 * tt + 3:4 * tt + 4],
                                   i=ln_ra2[:, 4 * tt + 2:4 * tt + 3], s=ln_ra2[:, 4 * tt:4 * tt + 1]:
                                   e.activation(out=o, in_=i, func=AF.Identity, scale=s),
                                   inc="s_act" if tt == 1 else None)
                    if tt == 1:
                        last_use["ln_rb2"] = [("s_act", ca)]
                        last_use["xh2d"] = [("s_act", ca)]
                return f

            def ln2T(tt):
                def f():
                    c_ph = p.add("scalar", lambda e, o=xh2[:], i=x1_sb[:, tt * D:(tt + 1) * D],
                                 s=ln_ra2[:, 4 * tt + 2:4 * tt + 3],
                                 bi=ln_ra2[:, 4 * tt + 3:4 * tt + 4]:
                                 e.activation(out=o, in_=i, func=AF.Identity, scale=s, bias=bi),
                                 waits=lu("xh2"), inc="s_act")
                    tr2 = []
                    for b in range(2):
                        bk = banks[2 + b]
                        for q in range(4):
                            dk = b * 4 + q
                            cpe = p.add("tensor", lambda e, o=bk[:, q * 128:(q + 1) * 128],
                                        i=xh2[:, dk * 128:(dk + 1) * 128], idn=ident_f[:]:
                                        e.transpose(o, i, idn),
                                        waits=([("s_act", c_ph)] + lu(("bank", 2 + b))) if q == 0 else [],
                                        inc="s_pe" if q == 3 else None)
                        tr2.append(("s_pe", cpe))
                        ca2 = p.add("scalar", lambda e,
                                    o=xh2T[:].rearrange("p (dk t) -> p dk t", dk=KDM)[:, b * 4:(b + 1) * 4, tt * 128:(tt + 1) * 128],
                                    i=bk[:]:
                                    e.activation(out=o, in_=i, func=AF.Copy),
                                    waits=[("s_pe", cpe)], inc="s_act")
                        last_use[("bank", 2 + b)] = [("s_act", ca2)]
                    last_use["xh2"] = tr2
                    if tt == 1:
                        ctx["c_xh2T"] = p.c("s_act")
                return f

            def ffn_m(m):
                def f():
                    wm = wcm_m[m % 2]
                    cw = p.dma("sync", wm[:], wcm_sc[m],
                               waits=c_wm_w + lu(("wcm_m", m % 2)), inc="d_w")
                    bi = 4 + (m % 2)
                    pf = banks[bi][:, 0:TC]
                    cpm = None
                    for k in range(KDM):
                        w0 = ([cw, ("s_act", ctx["c_xh2T"])] + lu(("bank", bi))) if k == 0 else []
                        cpm = p.add("tensor", lambda e, o=pf, w=wm[:, k * 128:(k + 1) * 128],
                                    r=xh2T[:, k * TC:(k + 1) * TC], s=(k == 0), st=(k == KDM - 1):
                                    e.matmul(o, w, r, start=s, stop=st),
                                    waits=w0, inc="s_pe" if k == KDM - 1 else None)
                    last_use[("wcm_m", m % 2)] = [("s_pe", cpm)]
                    ca = p.add("scalar", lambda e, o=ff_f[m % 2][:], i=pf, bi2=bias_sb[:, m:m + 1]:
                               e.activation(out=o, in_=i, func=AF.Identity, bias=bi2),
                               waits=[("s_pe", cpm)] + lu(("ff_f", m % 2)), inc="s_act")
                    last_use[("bank", bi)] = [("s_act", ca)]
                    ctx[("ff_evac", m)] = ca
                return f

            def x2_pair(pr, sc=sc, s0=s0):
                def f():
                    tb = 2 + pr % 2
                    cpe = None
                    for j in range(2):
                        m = 2 * pr + j
                        for tt in range(2):
                            q = j * 2 + tt
                            cpe = p.add("tensor", lambda e, o=banks[tb][:, q * 128:(q + 1) * 128],
                                        i=ff_f[m % 2][:, tt * 128:(tt + 1) * 128], idn=ident_f[:]:
                                        e.transpose(o, i, idn),
                                        waits=([("s_act", ctx[("ff_evac", 2 * pr + 1)])] +
                                               lu(("bank", tb))) if q == 0 else [],
                                        inc="s_pe" if (j == 1 and tt == 1) else None)
                    cv = None
                    for j in range(2):
                        m = 2 * pr + j
                        for tt in range(2):
                            q = j * 2 + tt
                            dst = (x2_sb if tt == 0 else xh2d)[:, m * 128:(m + 1) * 128]
                            cv = p.add("vector",
                                       lambda e, o=dst, i0=banks[tb][:, q * 128:(q + 1) * 128],
                                       i1=x1_sb[:, tt * D + m * 128:tt * D + (m + 1) * 128]:
                                       e.tensor_tensor(out=o, in0=i0, in1=i1, op=ALU.add),
                                       waits=(([("s_pe", cpe)] +
                                               (lu("x2_sb") + lu("xh2d") if pr == 0 else []))
                                              if q == 0 else []),
                                       inc="s_dve" if (j == 1 and tt == 1) else None)
                    last_use[("bank", tb)] = [("s_dve", cv)]
                    last_use[("ff_f", 0)] = [("s_pe", cpe)]
                    last_use[("ff_f", 1)] = [("s_pe", cpe)]
                    ctx.setdefault("x2_done", []).append(cv)
                    if pr == 3:
                        cdone = max(ctx["x2_done"])
                        for tt in range(2):
                            buf = x2_sb if tt == 0 else xh2d
                            p.dma("sync", out_h[s0 + tt * 128:s0 + (tt + 1) * 128, :], buf[:],
                                  waits=[("s_dve", cdone)], inc="d_out")
                        last_use["x2_sb"] = [("d_out", p.c("d_out"))]
                        last_use["xh2d"] = [("d_out", p.c("d_out"))]
                        last_use["x1_sb"] = [("s_dve", cdone)]
                return f

            units.append(xr_load())
            for m in range(KDM):
                units.append(attn_m(m))
                if m % 2 == 1:
                    units.append(x1_pair(m // 2))
            units.append(ln2A())
            units.append(ln2RT())
            for tt in range(2):
                units.append(ln2BC(tt))
            for tt in range(2):
                units.append(ln2T(tt))
            for m in range(KDM):
                units.append(ffn_m(m))
                if m % 2 == 1:
                    units.append(x2_pair(m // 2))
            return units

        def scan_chunk(sc, bg):
            cp = sc // 2
            s0 = sc * TC
            HN = N // 2
            cbp = []
            ccp = []
            for h in range(2):
                cbp.append(p.dma("gpsimd",
                                 brep[:, h * HN * TC:(h + 1) * HN * TC].rearrange("p (n t) -> p n t", n=HN),
                                 bc_sc[h * HN:(h + 1) * HN, s0:s0 + TC][None, :, :].broadcast_to([128, HN, TC]),
                                 waits=[c_bc_w[cp]] + lu("brep"), inc="d_brep"))
            for h in range(2):
                ccp.append(p.dma("gpsimd",
                                 crep[:, h * HN * TC:(h + 1) * HN * TC].rearrange("p (n t) -> p n t", n=HN),
                                 bc_sc[N + h * HN:N + (h + 1) * HN, s0:s0 + TC][None, :, :].broadcast_to([128, HN, TC]),
                                 waits=lu("crep"), inc="d_brep"))

            pend = [None]
            loaded = {}

            def load_k(k):
                kb = k % 2
                cdt = p.dma("gpsimd", dt_t[kb][:], dt_sc[k * 128:(k + 1) * 128, s0:s0 + TC],
                            waits=[c_dt_w[(k, cp)]] + lu(("dt_t", kb)), inc="d_scr")
                cxs = p.dma("gpsimd", xs_t[kb][:], xs_sc[k * 128:(k + 1) * 128, s0:s0 + TC],
                            waits=[c_xs_w[(k, cp)]] + lu(("xs_t", kb)), inc="d_scr")
                csz = p.dma("sync", sz_t[kb][:], sz_sc[k * 128:(k + 1) * 128, s0:s0 + TC],
                            waits=[c_sz_w[(k, cp)]] + lu(("sz_t", kb)), inc="d_scr")
                loaded[k] = (cdt, cxs, csz)

            def emit_yevac(rec):
                kk, kkb, cstop, csz_ = rec
                ybank = banks[6] if kk % 2 == 0 else banks[7]
                ykey = ("bank", 6) if kk % 2 == 0 else ("bank", 7)
                cyt = p.add("vector", lambda e, o=y_tot[:], i0=xs_t[kkb][:],
                            s=d_sb[:, kk:kk + 1], i1=ybank[:, 0:TC]:
                            e.scalar_tensor_tensor(out=o, in0=i0, scalar=s, in1=i1,
                                                   op0=ALU.mult, op1=ALU.add),
                            waits=[("s_pe", cstop), csz_],
                            inc="s_dve")
                last_use[ykey] = [("s_dve", cyt)]
                last_use[("xs_t", kkb)] = [("s_dve", cyt)]
                cyg = p.add("vector", lambda e, o=yg[sc % 2][:, kk * TC:(kk + 1) * TC],
                            i0=y_tot[:], i1=sz_t[kkb][:]:
                            e.tensor_tensor(out=o, in0=i0, in1=i1, op=ALU.mult),
                            waits=(lu(("yg", sc % 2)) if kk == 0 else []), inc="s_dve")
                last_use[("sz_t", kkb)] = [("s_dve", cyg)]

            load_k(0)
            load_k(1)
            for k in range(KD):
                kb = k % 2
                cdt, cxs, csz = loaded.pop(k)
                cdx = p.add("gpsimd", lambda e, o=dtx_t[kb][:], i0=dt_t[kb][:], i1=xs_t[kb][:]:
                            e.tensor_tensor(out=o, in0=i0, in1=i1, op=ALU.mult),
                            waits=[cdt, cxs] + lu(("dtx", kb)), inc="s_gp")
                ybank = banks[6] if k % 2 == 0 else banks[7]
                ykey = ("bank", 6) if k % 2 == 0 else ("bank", 7)
                cz = None
                cpe_g = None
                cbb_last = None
                cbb_of = {}

                def emit_bb(g):
                    bbuf = [bb_d, bb_g2, bb_d2][g % 3]
                    bkey = ("bbg", g % 3)
                    cbb = p.add("gpsimd", lambda e, o=bbuf[:],
                                i0=dtx_t[kb][:, None, :].broadcast_to([128, 4, TC]),
                                i1=brep[:].rearrange("p (n t) -> p n t", n=N)[:, g * 4:(g + 1) * 4, :]:
                                e.tensor_tensor(out=o.rearrange("p (n t) -> p n t", n=4), in0=i0, in1=i1, op=ALU.mult),
                                waits=(([cbp[g // 2]] if k <= 1 else []) +
                                       lu(bkey)),
                                inc="s_gp")
                    cbb_of[g] = (cbb, bbuf, bkey)
                    return cbb

                emit_bb(0)
                emit_bb(1)
                for g in range(4):
                    abuf = ab_g[g % 2]
                    caa = None
                    for j in range(4):
                        n = g * 4 + j
                        w0 = []
                        if j == 0:
                            w0 += lu(("ab", g % 2))
                            if g == 0:
                                w0 += [cdt]
                        caa = p.add("scalar", lambda e, o=abuf[:, j * TC:(j + 1) * TC],
                                    i=dt_t[kb][:], s=a_neg[:, k * N + n:k * N + n + 1]:
                                    e.activation(out=o, in_=i, func=AF.Exp, scale=s),
                                    waits=w0, inc="s_act" if j == 3 else None)
                    cbb, bbuf, bkey = cbb_of[g]
                    cbb_last = cbb
                    csn = None
                    for j in range(4):
                        n = g * 4 + j
                        w0 = []
                        if j == 0:
                            w0 = [("s_act", caa), ("s_gp", cbb)] + lu(("h", g))
                        csn = p.add("vector", lambda e, o=h_slab[:, n * TC:(n + 1) * TC],
                                    a=abuf[:, j * TC:(j + 1) * TC], b2=bbuf[:, j * TC:(j + 1) * TC],
                                    ini=state[:, k * N + n:k * N + n + 1]:
                                    e.tensor_tensor_scan(out=o, data0=a, data1=b2, initial=ini,
                                                         op0=ALU.mult, op1=ALU.add),
                                    waits=w0, inc="s_dve" if j == 3 else None)
                    last_use[("ab", g % 2)] = [("s_dve", csn)]
                    last_use[bkey] = [("s_dve", csn)]
                    if g == 0 and pend[0] is not None:
                        emit_yevac(pend[0])
                        pend[0] = None
                        if k + 1 < KD:
                            load_k(k + 1)
                    if g < 2:
                        emit_bb(g + 2)
                    zi = (k * 4 + g) % 3
                    zb = z4[zi]
                    cz = p.add("vector", lambda e, o=zb[:],
                               i0=h_slab[:, (g * 4) * TC:(g * 4 + 4) * TC],
                               i1=crep[:, (g * 4) * TC:(g * 4 + 4) * TC]:
                               e.tensor_tensor(out=o, in0=i0, in1=i1, op=ALU.mult),
                               waits=(([ccp[g // 2]] if k == 0 else []) +
                                      lu(("z4", zi))),
                               inc="s_dve")
                    last_use[("h", g)] = [("s_dve", cz)]
                    for j in range(4):
                        nn = g * 4 + j
                        w0 = [("s_dve", cz)] if j == 0 else []
                        if nn == 0:
                            w0 += lu(ykey)
                        cpe_g = p.add("tensor", lambda e, o=ybank[:, 0:TC], idn=ident_b[:],
                                      r=zb[:, j * TC:(j + 1) * TC],
                                      s=(nn == 0), st=(nn == N - 1):
                                      e.matmul(o, idn, r, start=s, stop=st),
                                      waits=w0, inc="s_pe" if j == 3 else None)
                    last_use[("z4", zi)] = [("s_pe", cpe_g)]
                    rem = (KD - k) * 4 - g - 1
                    if bg and rem > 0:
                        nbg = (len(bg) + rem - 1) // rem
                        for _ in range(min(nbg, len(bg))):
                            bg.pop(0)()
                cst = p.add("vector", lambda e, o=state[:, k * N:(k + 1) * N],
                            i=h_slab[:].rearrange("p (n t) -> p n t", n=N)[:, :, TC - 1]:
                            e.tensor_copy(o, i), inc="s_dve")
                last_use[("dt_t", kb)] = [("s_act", caa)]
                last_use[("dtx", kb)] = [("s_gp", cbb_last)]
                pend[0] = (k, kb, cpe_g, csz)
            emit_yevac(pend[0])
            pend[0] = None
            c_yg[sc] = p.c("s_dve")
            last_use["brep"] = [("s_gp", p.c("s_gp"))]
            last_use["crep"] = [("s_dve", p.c("s_dve"))]

        for u in stage1_units(0):
            u()
        p.mark("stage1_0_done")
        bg = []
        s1u = []
        for sc in range(nsc):
            cp = sc // 2
            if sc > 0:
                bg += post_units(sc - 1)
            if sc % 2 == 0 and cp + 1 < nmc:
                s1u = stage1_units(cp + 1)
            if s1u:
                take = (len(s1u) + 1) // 2 if sc % 2 == 0 else len(s1u)
                bg += s1u[:take]
                s1u = s1u[take:]
            scan_chunk(sc, bg)
            p.mark(f"scan_{sc}_done")
        for u in bg:
            u()
        for u in post_units(nsc - 1):
            u()
        p.mark("tail_done")

        if TRUNCATE is None:
            p.add("sync", lambda e: e.nop(), waits=[("d_out", p.c("d_out"))])

        with nc.Block() as block:
            p.emit(block, sems)

    return nc


_NC_CACHE = None


def kernel(L0_attn_x, L0_attn_W_in, L0_attn_A_log, L0_attn_W_B, L0_attn_W_C,
           L0_attn_W_dt, L0_attn_b_dt, L0_attn_D, L0_attn_output_gate,
           L0_attn_W_out, L0_ffn_cond, L0_ffn_mask, L0_ffn_W_shared,
           L0_ffn_W_expert, L0_ffn_W_gate, L0_ffn_bias):
    global _NC_CACHE
    if _NC_CACHE is None:
        _NC_CACHE = build_nc()
    nc = _NC_CACHE

    f32 = np.float32
    cont = np.ascontiguousarray
    winT = cont(L0_attn_W_in.T.astype(f32, copy=False))
    wdtT = cont(L0_attn_W_dt.T.astype(f32, copy=False))
    wbcT = cont(np.concatenate([L0_attn_W_B, L0_attn_W_C], axis=0).T.astype(f32, copy=False))
    woutT = cont(L0_attn_W_out.T.astype(f32, copy=False))
    wsT = cont(L0_ffn_W_shared.T.astype(f32, copy=False))
    weT = cont(L0_ffn_W_expert.T.astype(f32, copy=False))

    common = dict(winT=winT, wdtT=wdtT, wbcT=wbcT, woutT=woutT, wsT=wsT, weT=weT,
                  alog=cont(L0_attn_A_log.astype(f32, copy=False)),
                  bdt=cont(L0_attn_b_dt.astype(f32, copy=False)),
                  dvec=cont(L0_attn_D.astype(f32, copy=False)),
                  bias=cont(L0_ffn_bias.astype(f32, copy=False)),
                  gate=cont(L0_attn_output_gate.astype(f32, copy=False)).reshape(1, 1),
                  consts=np.tile(np.array([1e-5, -0.5, 1.0 / D, -1.0, 1.0, -1.0 / D, 0.0, 0.0],
                                          dtype=f32), (128, 1)),
                  identf=np.eye(128, dtype=f32),
                  identb=np.eye(128, dtype=ml_dtypes.bfloat16))
    in_maps = []
    for i in range(8):
        m = dict(common)
        m["x"] = cont(L0_attn_x[i].astype(f32, copy=False))
        in_maps.append(m)

    res = run_bass_kernel_spmd(nc, in_maps, core_ids=list(range(8)))
    return np.stack([res.results[i]["out"] for i in range(8)], axis=0)


# version 17
# speedup vs baseline: 1.0380x; 1.0008x over previous
import numpy as np
import ml_dtypes
import contextlib
import concourse.bass as bass
import concourse.mybir as mybir
from concourse.bass_utils import run_bass_kernel_spmd

F32 = mybir.dt.float32
BF16 = mybir.dt.bfloat16
AF = mybir.ActivationFunctionType
ALU = mybir.AluOpType
AX = mybir.AxisListType

T, D = 2048, 1024
E = 2048
N = 16
MC = 512
NMC = T // MC
TC = 256
NSC = T // TC
KD = E // 128
KDM = D // 128

TRUNCATE = None
DEBUG_SCRATCH = False
MARKS = {}

ENG_SEM = {"scalar": "s_act", "vector": "s_dve", "tensor": "s_pe", "gpsimd": "s_gp"}


DMA_ROT = {"d_x": 1, "d_w": 3, "d_scr": 3, "d_brep": 4, "d_wr": 2, "d_out": 1}


def dma_sem_names():
    out = []
    for base, n in DMA_ROT.items():
        if n == 1:
            out.append(base)
        else:
            out.extend(f"{base}{i}" for i in range(n))
    return out


class Prog:

    def __init__(self):
        self.ops = {k: [] for k in ("sync", "scalar", "vector", "tensor", "gpsimd")}
        self.cnt = {}
        self.gidx = 0
        self.rrc = {}

    def add(self, eng, fn, waits=(), inc=None, inc_k=1):
        self.gidx += 1
        new = None
        if inc is not None:
            self.cnt[inc] = self.cnt.get(inc, 0) + inc_k
            new = self.cnt[inc]
        if TRUNCATE is not None and self.gidx > TRUNCATE:
            return new
        own = ENG_SEM.get(eng)
        for (sem, val) in waits:
            if val and val > 0 and sem != own:
                self.ops[eng].append(("wait", sem, val))
        self.ops[eng].append(("op", fn, inc, new, inc_k))
        return new

    def dma(self, eng, out, in_, waits=(), inc=None, slow=False):
        def fn(e, out=out, in_=in_, slow=slow):
            if slow:
                return e.dma_start(out=out, in_=in_, allow_slow_non_contiguous=True)
            return e.dma_start(out=out, in_=in_)
        waits = list(waits)
        n = DMA_ROT.get(inc, 1)
        if n > 1:
            idx = self.rrc.get(inc, 0)
            self.rrc[inc] = (idx + 1) % n
            sem = f"{inc}{idx}"
        else:
            sem = inc
        prev = self.c(sem)
        if prev:
            waits.append((sem, prev))
        val = self.add(eng, fn, waits=waits, inc=sem, inc_k=16)
        return (sem, val)

    def c(self, sem):
        return self.cnt.get(sem, 0)

    def cs(self, base):
        n = DMA_ROT.get(base, 1)
        names = [base] if n == 1 else [f"{base}{i}" for i in range(n)]
        return [(s, self.cnt[s]) for s in names if self.cnt.get(s, 0) > 0]

    def mark(self, name):
        MARKS[name] = self.gidx

    def emit(self, block, sems):
        waited = {}
        for eng, items in self.ops.items():
            for it in items:
                if it[0] == "wait":
                    waited.setdefault(it[1], set()).add(it[2])
        rank = {}
        for sem, vals in waited.items():
            if not sem.startswith("d_"):
                rank[sem] = {v: i + 1 for i, v in enumerate(sorted(vals))}

        def mk(eng_name):
            def body(engobj):
                for item in self.ops[eng_name]:
                    if item[0] == "wait":
                        _, sem, val = item
                        if sem in rank:
                            val = rank[sem][val]
                        engobj.wait_ge(sems[sem], val)
                    else:
                        _, fn, inc, val, inc_k = item
                        instr = fn(engobj)
                        if inc is not None:
                            if inc.startswith("d_"):
                                instr.then_inc(sems[inc], inc_k)
                            elif val in rank.get(inc, ()):
                                instr.then_inc(sems[inc], inc_k)
            return body
        block.sync(mk("sync"))
        block.scalar(mk("scalar"))
        block.vector(mk("vector"))
        block.tensor(mk("tensor"))
        block.gpsimd(mk("gpsimd"))


def build_nc():
    nmc = T // MC
    nsc = T // TC
    nc = bass.Bass("TRN2", target_bir_lowering=False, detect_race_conditions=False)

    x_h = nc.dram_tensor("x", [T, D], F32, kind="ExternalInput").ap()
    winT_h = nc.dram_tensor("winT", [D, 2 * E], F32, kind="ExternalInput").ap()
    wdtT_h = nc.dram_tensor("wdtT", [E, E], F32, kind="ExternalInput").ap()
    wbcT_h = nc.dram_tensor("wbcT", [E, 2 * N], F32, kind="ExternalInput").ap()
    woutT_h = nc.dram_tensor("woutT", [E, D], F32, kind="ExternalInput").ap()
    wsT_h = nc.dram_tensor("wsT", [D, D], F32, kind="ExternalInput").ap()
    weT_h = nc.dram_tensor("weT", [D, D], F32, kind="ExternalInput").ap()
    alog_h = nc.dram_tensor("alog", [E, N], F32, kind="ExternalInput").ap()
    bdt_h = nc.dram_tensor("bdt", [E], F32, kind="ExternalInput").ap()
    dvec_h = nc.dram_tensor("dvec", [E], F32, kind="ExternalInput").ap()
    bias_h = nc.dram_tensor("bias", [D], F32, kind="ExternalInput").ap()
    gate_h = nc.dram_tensor("gate", [1, 1], F32, kind="ExternalInput").ap()
    consts_h = nc.dram_tensor("consts", [128, 8], F32, kind="ExternalInput").ap()
    identf_h = nc.dram_tensor("identf", [128, 128], F32, kind="ExternalInput").ap()
    identb_h = nc.dram_tensor("identb", [128, 128], BF16, kind="ExternalInput").ap()

    out_h = nc.dram_tensor("out", [T, D], F32, kind="ExternalOutput").ap()

    knd = "ExternalOutput" if DEBUG_SCRATCH else "Internal"
    xs_sc = nc.dram_tensor("xs_sc", [E, T], F32, kind=knd).ap()
    dt_sc = nc.dram_tensor("dt_sc", [E, T], F32, kind=knd).ap()
    sz_sc = nc.dram_tensor("sz_sc", [E, T], BF16, kind=knd).ap()
    bc_sc = nc.dram_tensor("bc_sc", [2 * N, T], F32, kind=knd).ap()
    sgate_sc = nc.dram_tensor("sgate_sc", [1, 1], F32, kind="Internal").ap()
    lnsc = nc.dram_tensor("lnsc", [128, 8], F32, kind="Internal").ap()
    lnsc2 = nc.dram_tensor("lnsc2", [128, 4], F32, kind="Internal").ap()
    winb_sc = nc.dram_tensor("winb_sc", [2 * KD, 128, KDM * 128], BF16, kind="Internal").ap()
    wdtb_sc = nc.dram_tensor("wdtb_sc", [KD, 128, KD * 128], BF16, kind="Internal").ap()
    woutm_sc = nc.dram_tensor("woutm_sc", [KDM, 128, KD * 128], BF16, kind=knd).ap()
    wcm_sc = nc.dram_tensor("wcm_sc", [KDM, 128, KDM * 128], BF16, kind=knd).ap()

    nc.reset()

    p = Prog()
    sem_names = dma_sem_names() + ["s_act", "s_dve", "s_pe", "s_gp"]

    stack = contextlib.ExitStack()
    sbuf = lambda name, shape, dt: stack.enter_context(nc.sbuf_tensor(name, shape, dt)).ap()

    with stack:
        a_neg = sbuf("a_neg", [128, KD * N], F32)
        bdt_sb = sbuf("bdt_sb", [128, KD], F32)
        d_sb = sbuf("d_sb", [128, KD], F32)
        bias_sb = sbuf("bias_sb", [128, KDM], F32)
        ident_f = sbuf("ident_f", [128, 128], F32)
        ident_b = sbuf("ident_b", [128, 128], BF16)
        g_sb = sbuf("g_sb", [1, 4], F32)
        csts = sbuf("cst", [128, 8], F32)
        ln_tmp = sbuf("ln_tmp", [128, 2], F32)
        ln_rb = sbuf("ln_rb", [128, 8], F32)
        ln_ra = sbuf("ln_ra", [128, 16], F32)
        ln_rb2 = sbuf("ln_rb2", [128, 4], F32)
        ln_ra2 = sbuf("ln_ra2", [128, 8], F32)
        sbc_sb = sbuf("sbc_sb", [128, 1], F32)

        wbc_bf = sbuf("wbc_bf", [128, KD * 2 * N], BF16)
        wstage = sbuf("wstage", [128, KD * 128], F32)
        wbf = [sbuf(f"wbf{i}", [128, KD * 128], BF16) for i in range(2)]
        wout_m = [sbuf(f"woutm{i}", [128, KD * 128], BF16) for i in range(2)]
        wcm_m = [sbuf(f"wcmm{i}", [128, KDM * 128], BF16) for i in range(2)]

        x_sb = sbuf("x_sb", [128, D], F32)
        xc_sb = sbuf("xc_sb", [128, D], F32)
        xhat = sbuf("xhat", [128, D], F32)
        xhatT = sbuf("xhatT", [128, KDM * MC], BF16)
        xsbf = sbuf("xsbf", [128, KD * MC], BF16)
        dt_f = [sbuf(f"dt_f{i}", [128, MC], F32) for i in range(2)]
        szb = [sbuf(f"szb{i}", [128, MC], BF16) for i in range(2)]
        bc_f = sbuf("bc_f", [32, MC], F32)
        ln_r = sbuf("ln_r", [128, 16], F32)
        ln_r2 = sbuf("ln_r2", [128, 8], F32)

        brep = sbuf("brep", [128, N * TC], F32)
        crep = sbuf("crep", [128, N * TC], F32)
        dt_t = [sbuf(f"dt_t{i}", [128, TC], F32) for i in range(2)]
        xs_t = [sbuf(f"xs_t{i}", [128, TC], F32) for i in range(2)]
        sz_t = [sbuf(f"sz_t{i}", [128, TC], BF16) for i in range(2)]
        dtx_t0 = sbuf("dtx0", [128, TC], F32)
        dtx_t1 = sbuf("dtx1", [128, TC], F32)
        ab_g = [sbuf(f"ab{i}", [128, 4 * TC], F32) for i in range(2)]
        xs_f = [sbuf(f"xs_f{i}", [128, MC], F32) for i in range(2)]
        bb_g2 = sbuf("bb_g2", [128, 4 * TC], F32)
        h_slab = sbuf("h_slab", [128, N * TC], F32)
        z4 = [sbuf(f"z4{i}", [128, 4 * TC], BF16) for i in range(3)]
        y_tot = sbuf("y_tot", [128, TC], F32)
        yg = [sbuf(f"yg{i}", [128, KD * TC], BF16) for i in range(2)]
        state = sbuf("state", [128, KD * N], F32)

        at_f = [sbuf(f"at_f{i}", [128, TC], F32) for i in range(2)]
        x1_sb = sbuf("x1_sb", [128, 2 * D], F32)
        xr_sb = [sbuf(f"xr_sb{i}", [128, D], F32) for i in range(2)]
        xh2 = sbuf("xh2", [128, D], F32)
        xh2d = sbuf("xh2d", [128, D], F32)
        xh2T = sbuf("xh2T", [128, KDM * TC], BF16)
        ff_f = [sbuf(f"ff_f{i}", [128, TC], F32) for i in range(2)]
        x2_sb = sbuf("x2_sb", [128, D], F32)

        bb_d = wstage[:, 0:4 * TC]
        bb_d2 = wstage[:, 4 * TC:8 * TC]
        dtx_t = [dtx_t0, dtx_t1]

        banks = [stack.enter_context(nc.psum_tensor(f"ps{i}", [128, 512], F32)).ap()
                 for i in range(8)]

        sems = {}
        for s in sem_names:
            sems[s] = stack.enter_context(nc.semaphore(s))

        last_use = {}
        def lu(key):
            return list(last_use.get(key, []))

        p.add("vector", lambda e, o=state[:]: e.memset(o, 0.0), inc="s_dve")

        for apx in (ab_g[0][:], ab_g[1][:], xhat[:], xc_sb[:],
                    xh2[:], xh2d[:], xhatT[:], xh2T[:], xs_f[0][:], xs_f[1][:],
                    dt_f[0][:], dt_f[1][:], szb[0][:], szb[1][:], bc_f[:],
                    at_f[0][:], at_f[1][:], ff_f[0][:], ff_f[1][:], a_neg[:],
                    ln_tmp[:], ln_ra[:], ln_ra2[:], g_sb[0:1, 1:4]):
            p.add("scalar", lambda e, o=apx: e.memzero(o), inc=None)
        for apx in (wbf[0], wbf[1], wbc_bf, h_slab, x1_sb, x2_sb, y_tot,
                    ln_r, ln_r2, z4[0], z4[1], z4[2], yg[0], yg[1]):
            p.add("vector", lambda e, o=apx[:]: e.memset(o, 0.0), inc=None)
        for apx in (dtx_t0, dtx_t1, bb_g2):
            p.add("gpsimd", lambda e, o=apx[:]: e.memset(o, 0.0), inc=None)
        p.add("scalar", lambda e, o=xsbf[:]: e.memzero(o), inc=None)
        cms = None
        for i in range(8):
            cms = p.add("vector", lambda e, o=banks[i][:]: e.memset(o, 0.0),
                        inc="s_dve" if i == 7 else None)
        for i in range(8):
            last_use[("bank", i)] = [("s_dve", cms)]

        p.dma("sync", ident_f[:], identf_h, inc="d_w")
        p.dma("sync", ident_b[:], identb_h, inc="d_w")
        p.dma("sync", csts[:], consts_h, inc="d_w")
        p.dma("sync", bdt_sb[:], bdt_h.rearrange("(k p) -> p k", p=128), inc="d_w", slow=True)
        p.dma("sync", d_sb[:], dvec_h.rearrange("(k p) -> p k", p=128), inc="d_w", slow=True)
        p.dma("sync", bias_sb[:], bias_h.rearrange("(k p) -> p k", p=128), inc="d_w", slow=True)
        p.dma("sync", g_sb[0:1, 0:1], gate_h, inc="d_w")
        c_w0 = p.cs("d_w")
        ca0 = p.dma("sync", wstage[:, 0:KD * N].rearrange("p (k n) -> p k n", k=KD),
                    alog_h.rearrange("(k p) n -> p k n", p=128), inc="d_w")
        ca1 = p.add("scalar", lambda e, o=a_neg[:], i=wstage[:, 0:KD * N]:
                    e.activation(out=o, in_=i, func=AF.Exp),
                    waits=[ca0], inc="s_act")
        last_use["wstage"] = [("s_act", ca1)]
        p.add("vector", lambda e, o=a_neg[:], i=a_neg[:]: e.tensor_scalar_mul(o, i, -1.0),
              waits=[("s_act", ca1)], inc="s_dve")
        cga = p.add("scalar", lambda e, o=g_sb[0:1, 1:2], i=g_sb[0:1, 0:1]:
                    e.activation(out=o, in_=i, func=AF.Sigmoid),
                    waits=c_w0, inc="s_act")
        csg = p.dma("sync", sgate_sc, g_sb[0:1, 1:2], waits=[("s_act", cga)], inc="d_wr")
        c_sbc = p.dma("sync", sbc_sb[:], sgate_sc.broadcast_to([128, 1]),
                      waits=[csg], inc="d_w")

        for k in range(KDM):
            c1 = p.dma("sync", wstage[:, 0:D], wsT_h[k * 128:(k + 1) * 128, :],
                       waits=lu("wstage"), inc="d_w")
            cw = p.dma("sync", wstage[:, D:2 * D], weT_h[k * 128:(k + 1) * 128, :], inc="d_w")
            cv = p.add("vector", lambda e, o=wbf[k % 2][:, 0:D], i0=wstage[:, 0:D],
                       i1=wstage[:, D:2 * D]: e.tensor_tensor(out=o, in0=i0, in1=i1, op=ALU.add),
                       waits=[c1, cw] + lu(("wbf", k % 2)), inc="s_dve")
            last_use["wstage"] = [("s_dve", cv)]
            cd = p.dma("sync", wcm_sc[:, :, k * 128:(k + 1) * 128].rearrange("m p j -> p m j"),
                       wbf[k % 2][:, 0:D].rearrange("p (m j) -> p m j", m=KDM),
                       waits=[("s_dve", cv)], inc="d_wr", slow=True)
            last_use[("wbf", k % 2)] = [cd]
        cw = p.dma("sync", wstage[:, 0:KD * 32].rearrange("p (k m) -> p k m", k=KD),
                   wbcT_h.rearrange("(k p) m -> p k m", p=128),
                   waits=lu("wstage"), inc="d_w")
        cv = p.add("vector", lambda e, o=wbc_bf[:], i=wstage[:, 0:KD * 32]: e.tensor_copy(o, i),
                   waits=[cw], inc="s_dve")
        last_use["wstage"] = [("s_dve", cv)]
        for k in range(KD):
            cw = p.dma("sync", wstage[:, 0:D], woutT_h[k * 128:(k + 1) * 128, :],
                       waits=lu("wstage"), inc="d_w")
            cv = p.add("vector", lambda e, o=wbf[k % 2][:, 0:D], i=wstage[:, 0:D], s=sbc_sb[:, 0:1]:
                       e.tensor_scalar(out=o, in0=i, scalar1=s, scalar2=None, op0=ALU.mult),
                       waits=[cw, c_sbc] + lu(("wbf", k % 2)), inc="s_dve")
            last_use["wstage"] = [("s_dve", cv)]
            cd = p.dma("sync", woutm_sc[:, :, k * 128:(k + 1) * 128].rearrange("m p j -> p m j"),
                       wbf[k % 2][:, 0:D].rearrange("p (m j) -> p m j", m=KDM),
                       waits=[("s_dve", cv)], inc="d_wr", slow=True)
            last_use[("wbf", k % 2)] = [cd]
        c_wm_w = p.cs("d_wr")
        c_wcast_w = [0]

        c_xs_w = {}
        c_dt_w = {}
        c_sz_w = {}
        c_bc_w = {}
        c_yg = {}

        def stage1_units(cp):
            units = []
            t0 = cp * MC
            ctx = {}

            def lnA(tt, t0=t0):
                def f():
                    r0 = t0 + tt * 128
                    sqb = xc_sb if tt % 2 == 0 else xhat
                    sqk = "xc_sb" if tt % 2 == 0 else "xhat"
                    cx = p.dma("sync", x_sb[:], x_h[r0:r0 + 128, :],
                               waits=lu("x_sb"), inc="d_x")
                    p.add("vector", lambda e, o=ln_r[:, 2 * tt:2 * tt + 1], i=x_sb[:]:
                          e.reduce_sum(o, i, axis=AX.X),
                          waits=[cx] + (lu("lnr1") if tt == 0 else []),
                          inc="s_dve")
                    c_sq = p.add("scalar", lambda e, o=sqb[:], i=x_sb[:]:
                                 e.activation(out=o, in_=i, func=AF.Square),
                                 waits=[cx] + lu(sqk), inc="s_act")
                    cr2 = p.add("vector", lambda e, o=ln_r[:, 2 * tt + 1:2 * tt + 2], i=sqb[:]:
                                e.reduce_sum(o, i, axis=AX.X),
                                waits=[("s_act", c_sq)], inc="s_dve")
                    ctx["c_sums"] = cr2
                    last_use[sqk] = [("s_dve", cr2)]
                    last_use["x_sb"] = [("s_dve", cr2), ("s_act", c_sq)]
                return f

            def lnRT():
                def f():
                    cd1 = p.dma("sync", lnsc, ln_r[:, 0:8], waits=[("s_dve", ctx["c_sums"])],
                                inc="d_wr")
                    cd2 = p.dma("sync", ln_rb[:], lnsc,
                                waits=[cd1] + lu("ln_rb"), inc="d_w")
                    ctx["c_lnrt"] = cd2
                    last_use["lnr1"] = [cd1]
                return f

            def lnB(tt):
                def f():
                    if True:
                        w0 = [ctx["c_lnrt"]] if tt == 0 else []
                        p.add("scalar", lambda e, o=ln_ra[:, 4 * tt:4 * tt + 1],
                              i=ln_rb[:, 2 * tt:2 * tt + 1], s=csts[:, 5:6]:
                              e.activation(out=o, in_=i, func=AF.Identity, scale=s),
                              waits=w0, inc=None)
                        p.add("scalar", lambda e, o=ln_tmp[:, 0:1], i=ln_ra[:, 4 * tt:4 * tt + 1]:
                              e.activation(out=o, in_=i, func=AF.Square), inc=None)
                        p.add("scalar", lambda e, o=ln_tmp[:, 1:2], i=ln_tmp[:, 0:1],
                              s=csts[:, 3:4], bi=csts[:, 0:1]:
                              e.activation(out=o, in_=i, func=AF.Identity, scale=s, bias=bi), inc=None)
                        p.add("scalar", lambda e, o=ln_tmp[:, 0:1], i=ln_rb[:, 2 * tt + 1:2 * tt + 2],
                              s=csts[:, 2:3], bi=ln_tmp[:, 1:2]:
                              e.activation(out=o, in_=i, func=AF.Ln, scale=s, bias=bi), inc=None)
                        p.add("scalar", lambda e, o=ln_ra[:, 4 * tt + 2:4 * tt + 3], i=ln_tmp[:, 0:1],
                              s=csts[:, 1:2]:
                              e.activation(out=o, in_=i, func=AF.Exp, scale=s), inc=None)
                        ca = p.add("scalar", lambda e, o=ln_ra[:, 4 * tt + 3:4 * tt + 4],
                                   i=ln_ra[:, 4 * tt + 2:4 * tt + 3], s=ln_ra[:, 4 * tt:4 * tt + 1]:
                                   e.activation(out=o, in_=i, func=AF.Identity, scale=s),
                                   inc="s_act" if tt == 3 else None)
                    if tt == 3:
                        last_use["ln_rb"] = [("s_act", ca)]
                        last_use["xc_sb"] = [("s_act", ca)]
                        last_use["x_sb"] = [("s_act", ca)]
                return f

            def lnC(tt, t0=t0):
                def f():
                    r0 = t0 + tt * 128
                    cx = p.dma("sync", x_sb[:], x_h[r0:r0 + 128, :],
                               waits=lu("x_sb"), inc="d_x")
                    c_ph = p.add("scalar", lambda e, o=xhat[:], i=x_sb[:],
                                 s=ln_ra[:, 4 * tt + 2:4 * tt + 3],
                                 bi=ln_ra[:, 4 * tt + 3:4 * tt + 4]:
                                 e.activation(out=o, in_=i, func=AF.Identity, scale=s, bias=bi),
                                 waits=[cx] + lu("xhat"),
                                 inc="s_act")
                    last_use["x_sb"] = [("s_act", c_ph)]
                    xh_done = []
                    for b in range(2):
                        bk = banks[2 + b]
                        for q in range(4):
                            dk = b * 4 + q
                            cpe = p.add("tensor", lambda e, o=bk[:, q * 128:(q + 1) * 128],
                                        i=xhat[:, dk * 128:(dk + 1) * 128], idn=ident_f[:]:
                                        e.transpose(o, i, idn),
                                        waits=([("s_act", c_ph)] + lu(("bank", 2 + b))) if q == 0 else [],
                                        inc="s_pe" if q == 3 else None)
                        xh_done.append(("s_pe", cpe))
                        ca2 = p.add("scalar", lambda e,
                                    o=xhatT[:].rearrange("p (dk t) -> p dk t", dk=KDM)[:, b * 4:(b + 1) * 4, tt * 128:(tt + 1) * 128],
                                    i=bk[:]:
                                    e.activation(out=o, in_=i, func=AF.Copy),
                                    waits=[("s_pe", cpe)], inc="s_act")
                        last_use[("bank", 2 + b)] = [("s_act", ca2)]
                    last_use["xhat"] = xh_done
                    if tt == 3:
                        ctx["c_xhatT"] = p.c("s_act")
                return f

            def xz(eo, t0=t0, cp=cp):
                def f():
                    wb = wbf[eo % 2]
                    if cp == 0:
                        cw = p.dma("sync", wstage[:, 0:KDM * 128].rearrange("p (k m) -> p k m", k=KDM),
                                   winT_h.rearrange("(k p) m -> p k m", p=128)[:, :, eo * 128:(eo + 1) * 128],
                                   waits=lu("wstage"), inc="d_w")
                        cv = p.add("vector", lambda e, o=wb[:, 0:KDM * 128], i=wstage[:, 0:KDM * 128]:
                                   e.tensor_copy(o, i),
                                   waits=[cw] + lu(("wbf", eo % 2)), inc="s_dve")
                        last_use["wstage"] = [("s_dve", cv)]
                        p.dma("sync", winb_sc[eo], wb[:, 0:KDM * 128], waits=[("s_dve", cv)], inc="d_wr")
                        rdy = [("s_dve", cv)]
                    else:
                        cw = p.dma("sync", wb[:, 0:KDM * 128], winb_sc[eo],
                                   waits=c_wcast_w[0] + lu(("wbf", eo % 2)), inc="d_w")
                        rdy = [cw]
                    bi = eo % 2
                    bank = banks[bi]
                    cpe = None
                    for k in range(KDM):
                        w0 = (rdy + [("s_act", ctx["c_xhatT"])] + lu(("bank", bi))) if k == 0 else []
                        cpe = p.add("tensor", lambda e, o=bank[:], w=wb[:, k * 128:(k + 1) * 128],
                                    r=xhatT[:, k * MC:(k + 1) * MC], s=(k == 0), st=(k == KDM - 1):
                                    e.matmul(o, w, r, start=s, stop=st),
                                    waits=w0, inc="s_pe" if k == KDM - 1 else None)
                    last_use[("wbf", eo % 2)] = [("s_pe", cpe)]
                    if eo < KD:
                        fb = xs_f[eo % 2]
                        ca = p.add("scalar", lambda e, o=fb[:], i=bank[:]:
                                   e.activation(out=o, in_=i, func=AF.Copy),
                                   waits=[("s_pe", cpe)] + lu(("xs_f", eo % 2)), inc="s_act")
                        cv = p.add("scalar", lambda e, o=xsbf[:, eo * MC:(eo + 1) * MC], i=bank[:]:
                                   e.activation(out=o, in_=i, func=AF.Copy),
                                   waits=(lu("xsbf") if eo == 0 else []), inc="s_act")
                        cd = p.dma("sync", xs_sc[eo * 128:(eo + 1) * 128, t0:t0 + MC], fb[:],
                                   waits=[("s_act", ca)], inc="d_wr")
                        c_xs_w[(eo, cp)] = cd
                        last_use[("xs_f", eo % 2)] = [("s_act", ca), cd]
                        last_use[("bank", bi)] = [("s_act", cv)]
                        if eo == KD - 1:
                            ctx["c_xsbf_part"] = cv
                return f

            def xz_z(eo0, t0=t0, cp=cp):
                def f():
                    cpes = []
                    for eo in (eo0, eo0 + 1):
                        wb = wbf[eo % 2]
                        if cp == 0:
                            cw = p.dma("sync", wstage[:, 0:KDM * 128].rearrange("p (k m) -> p k m", k=KDM),
                                       winT_h.rearrange("(k p) m -> p k m", p=128)[:, :, eo * 128:(eo + 1) * 128],
                                       waits=lu("wstage"), inc="d_w")
                            cv = p.add("vector", lambda e, o=wb[:, 0:KDM * 128], i=wstage[:, 0:KDM * 128]:
                                       e.tensor_copy(o, i),
                                       waits=[cw] + lu(("wbf", eo % 2)), inc="s_dve")
                            last_use["wstage"] = [("s_dve", cv)]
                            p.dma("sync", winb_sc[eo], wb[:, 0:KDM * 128], waits=[("s_dve", cv)], inc="d_wr")
                            rdy = [("s_dve", cv)]
                        else:
                            cw = p.dma("sync", wb[:, 0:KDM * 128], winb_sc[eo],
                                       waits=c_wcast_w[0] + lu(("wbf", eo % 2)), inc="d_w")
                            rdy = [cw]
                        bi = eo % 2
                        bank = banks[bi]
                        cpe = None
                        for k in range(KDM):
                            w0 = (rdy + [("s_act", ctx["c_xhatT"])] + lu(("bank", bi))) if k == 0 else []
                            cpe = p.add("tensor", lambda e, o=bank[:], w=wb[:, k * 128:(k + 1) * 128],
                                        r=xhatT[:, k * MC:(k + 1) * MC], s=(k == 0), st=(k == KDM - 1):
                                        e.matmul(o, w, r, start=s, stop=st),
                                        waits=w0, inc="s_pe" if k == KDM - 1 else None)
                        last_use[("wbf", eo % 2)] = [("s_pe", cpe)]
                        cpes.append(cpe)
                    for i, eo in enumerate((eo0, eo0 + 1)):
                        bi = eo % 2
                        zb = szb[eo % 2]
                        ca = p.add("scalar", lambda e, o=zb[:], i=banks[bi][:]:
                                   e.activation(out=o, in_=i, func=AF.Silu),
                                   waits=[("s_pe", cpes[i])] + lu(("szb", eo % 2)), inc="s_act")
                        cd = p.dma("sync", sz_sc[(eo - KD) * 128:(eo - KD + 1) * 128, t0:t0 + MC], zb[:],
                                   waits=[("s_act", ca)], inc="d_wr")
                        c_sz_w[(eo - KD, cp)] = cd
                        last_use[("szb", eo % 2)] = [cd]
                        last_use[("bank", bi)] = [("s_act", ca)]
                    if eo0 + 1 == 2 * KD - 1:
                        ctx["c_xsbf"] = ctx["c_xsbf_part"]
                return f

            def dtm(eo, t0=t0, cp=cp):
                def f():
                    wb = wbf[eo % 2]
                    if cp == 0:
                        cw = p.dma("sync", wstage[:].rearrange("p (k m) -> p k m", k=KD),
                                   wdtT_h.rearrange("(k p) m -> p k m", p=128)[:, :, eo * 128:(eo + 1) * 128],
                                   waits=lu("wstage"), inc="d_w")
                        cv = p.add("vector", lambda e, o=wb[:], i=wstage[:]: e.tensor_copy(o, i),
                                   waits=[cw] + lu(("wbf", eo % 2)), inc="s_dve")
                        last_use["wstage"] = [("s_dve", cv)]
                        p.dma("sync", wdtb_sc[eo], wb[:], waits=[("s_dve", cv)], inc="d_wr")
                        rdy = [("s_dve", cv)]
                    else:
                        cw = p.dma("sync", wb[:], wdtb_sc[eo],
                                   waits=c_wcast_w[0] + lu(("wbf", eo % 2)), inc="d_w")
                        rdy = [cw]
                    bi = eo % 2
                    bank = banks[bi]
                    cpe = None
                    for k in range(KD):
                        w0 = (rdy + [("s_act", ctx["c_xsbf"])] + lu(("bank", bi))) if k == 0 else []
                        cpe = p.add("tensor", lambda e, o=bank[:], w=wb[:, k * 128:(k + 1) * 128],
                                    r=xsbf[:, k * MC:(k + 1) * MC], s=(k == 0), st=(k == KD - 1):
                                    e.matmul(o, w, r, start=s, stop=st),
                                    waits=w0, inc="s_pe" if k == KD - 1 else None)
                    last_use[("wbf", eo % 2)] = [("s_pe", cpe)]
                    fb = dt_f[eo % 2]
                    p.add("scalar", lambda e, o=xc_sb[:, 0:MC], i=bank[:], bi2=bdt_sb[:, eo:eo + 1]:
                          e.activation(out=o, in_=i, func=AF.Exp, bias=bi2),
                          waits=[("s_pe", cpe)] + lu("xc_sb"), inc="s_act")
                    ca = p.add("scalar", lambda e, o=fb[:], i=xc_sb[:, 0:MC], bi2=csts[:, 4:5]:
                               e.activation(out=o, in_=i, func=AF.Ln, bias=bi2),
                               waits=lu(("dt_f", eo % 2)), inc="s_act")
                    last_use["xc_sb"] = [("s_act", ca)]
                    cd = p.dma("sync", dt_sc[eo * 128:(eo + 1) * 128, t0:t0 + MC], fb[:],
                               waits=[("s_act", ca)], inc="d_wr")
                    c_dt_w[(eo, cp)] = cd
                    last_use[("dt_f", eo % 2)] = [cd]
                    last_use[("bank", bi)] = [("s_act", ca)]
                    if cp == 0 and eo == KD - 1:
                        c_wcast_w[0] = p.cs("d_wr")
                return f

            def bcm(t0=t0, cp=cp):
                def f():
                    bank = banks[0]
                    cpe = None
                    for k in range(KD):
                        w0 = lu(("bank", 0)) if k == 0 else []
                        cpe = p.add("tensor", lambda e, o=bank[0:32, :], w=wbc_bf[:, k * 32:(k + 1) * 32],
                                    r=xsbf[:, k * MC:(k + 1) * MC], s=(k == 0), st=(k == KD - 1):
                                    e.matmul(o, w, r, start=s, stop=st),
                                    waits=w0, inc="s_pe" if k == KD - 1 else None)
                    last_use["xsbf"] = [("s_pe", cpe)]
                    ca = p.add("scalar", lambda e, o=bc_f[:], i=bank[0:32, :]:
                               e.activation(out=o, in_=i, func=AF.Copy),
                               waits=[("s_pe", cpe)] + lu("bc_f"), inc="s_act")
                    cd = p.dma("sync", bc_sc[:, t0:t0 + MC], bc_f[:], waits=[("s_act", ca)], inc="d_wr")
                    last_use["bc_f"] = [cd]
                    last_use[("bank", 0)] = [("s_act", ca)]
                    c_bc_w[cp] = cd
                return f

            for tt in range(4):
                units.append(lnA(tt))
            units.append(lnRT())
            for tt in range(4):
                units.append(lnB(tt))
            for tt in range(4):
                units.append(lnC(tt))
            for eo in range(KD):
                units.append(xz(eo))
            for eo in range(KD, 2 * KD, 2):
                units.append(xz_z(eo))
            for eo in range(KD):
                units.append(dtm(eo))
            units.append(bcm())
            return units

        def post_units(sc):
            units = []
            s0 = sc * TC
            ctx = {}

            def attn_m(m, sc=sc, s0=s0):
                def f():
                    wm = wout_m[m % 2]
                    cw = p.dma("sync", wm[:], woutm_sc[m],
                               waits=c_wm_w + lu(("wout_m", m % 2)), inc="d_w")
                    bi = 4 + (m % 2)
                    pa = banks[bi][:, 0:TC]
                    cpm = None
                    for k in range(KD):
                        w0 = ([cw, ("s_dve", c_yg[sc])] + lu(("bank", bi))) if k == 0 else []
                        cpm = p.add("tensor", lambda e, o=pa, w=wm[:, k * 128:(k + 1) * 128],
                                    r=yg[sc % 2][:, k * TC:(k + 1) * TC], s=(k == 0), st=(k == KD - 1):
                                    e.matmul(o, w, r, start=s, stop=st),
                                    waits=w0, inc="s_pe" if k == KD - 1 else None)
                    last_use[("wout_m", m % 2)] = [("s_pe", cpm)]
                    ca = p.add("scalar", lambda e, o=at_f[m % 2][:], i=pa:
                               e.activation(out=o, in_=i, func=AF.Copy),
                               waits=[("s_pe", cpm)] + lu(("at_f", m % 2)), inc="s_act")
                    last_use[("bank", bi)] = [("s_act", ca)]
                    ctx[("at_evac", m)] = ca
                    if m == KDM - 1:
                        last_use[("yg", sc % 2)] = [("s_pe", cpm)]
                return f

            def xr_load(sc=sc, s0=s0):
                def f():
                    ctx["cx0"] = p.dma("sync", xr_sb[0][:], x_h[s0:s0 + 128, :],
                                       waits=lu(("xr", 0)), inc="d_x")
                    ctx["cx1"] = p.dma("sync", xr_sb[1][:], x_h[s0 + 128:s0 + 256, :],
                                       waits=lu(("xr", 1)), inc="d_x")
                return f

            def x1_pair(pr, sc=sc):
                def f():
                    tb = 2 + pr % 2
                    cpe = None
                    for j in range(2):
                        m = 2 * pr + j
                        for tt in range(2):
                            q = j * 2 + tt
                            cpe = p.add("tensor", lambda e, o=banks[tb][:, q * 128:(q + 1) * 128],
                                        i=at_f[m % 2][:, tt * 128:(tt + 1) * 128], idn=ident_f[:]:
                                        e.transpose(o, i, idn),
                                        waits=([("s_act", ctx[("at_evac", 2 * pr + 1)])] +
                                               lu(("bank", tb))) if q == 0 else [],
                                        inc="s_pe" if (j == 1 and tt == 1) else None)
                    cv = None
                    for j in range(2):
                        m = 2 * pr + j
                        for tt in range(2):
                            q = j * 2 + tt
                            cv = p.add("vector",
                                       lambda e, o=x1_sb[:, tt * D + m * 128:tt * D + (m + 1) * 128],
                                       i0=banks[tb][:, q * 128:(q + 1) * 128],
                                       i1=xr_sb[tt][:, m * 128:(m + 1) * 128]:
                                       e.tensor_tensor(out=o, in0=i0, in1=i1, op=ALU.add),
                                       waits=(([("s_pe", cpe), ctx["cx0"], ctx["cx1"]] +
                                               (lu("x1_sb") if pr == 0 else [])) if q == 0 else []),
                                       inc="s_dve" if (j == 1 and tt == 1) else None)
                    last_use[("bank", tb)] = [("s_dve", cv)]
                    last_use[("at_f", 0)] = [("s_pe", cpe)]
                    last_use[("at_f", 1)] = [("s_pe", cpe)]
                    ctx.setdefault("x1_done", []).append(cv)
                    if pr == 3:
                        ctx["c_x1"] = max(ctx["x1_done"])
                        last_use[("xr", 0)] = [("s_dve", ctx["c_x1"])]
                        last_use[("xr", 1)] = [("s_dve", ctx["c_x1"])]
                return f

            def ln2A():
                def f():
                    for tt in range(2):
                        sqb = xh2d if tt == 0 else xh2
                        sqk = "xh2d" if tt == 0 else "xh2"
                        p.add("vector", lambda e, o=ln_r2[:, 2 * tt:2 * tt + 1],
                              i=x1_sb[:, tt * D:(tt + 1) * D]:
                              e.reduce_sum(o, i, axis=AX.X),
                              waits=([("s_dve", ctx["c_x1"])] if tt == 0 else []) +
                                    (lu("lnr2") if tt == 0 else []),
                              inc="s_dve")
                        csq2 = p.add("scalar", lambda e, o=sqb[:], i=x1_sb[:, tt * D:(tt + 1) * D]:
                                     e.activation(out=o, in_=i, func=AF.Square),
                                     waits=([("s_dve", ctx["c_x1"])] if tt == 0 else []) + lu(sqk),
                                     inc="s_act")
                        cr2 = p.add("vector", lambda e, o=ln_r2[:, 2 * tt + 1:2 * tt + 2], i=sqb[:]:
                                    e.reduce_sum(o, i, axis=AX.X),
                                    waits=[("s_act", csq2)], inc="s_dve")
                        ctx["c_sums2"] = cr2
                        last_use[sqk] = [("s_dve", cr2)]
                return f

            def ln2RT():
                def f():
                    cd1 = p.dma("sync", lnsc2, ln_r2[:, 0:4], waits=[("s_dve", ctx["c_sums2"])],
                                inc="d_wr")
                    cd2 = p.dma("sync", ln_rb2[:], lnsc2,
                                waits=[cd1] + lu("ln_rb2"), inc="d_w")
                    ctx["c_lnrt2"] = cd2
                    last_use["lnr2"] = [cd1]
                return f

            def ln2BC(tt):
                def f():
                    if True:
                        w0 = [ctx["c_lnrt2"]] if tt == 0 else []
                        p.add("scalar", lambda e, o=ln_ra2[:, 4 * tt:4 * tt + 1],
                              i=ln_rb2[:, 2 * tt:2 * tt + 1], s=csts[:, 5:6]:
                              e.activation(out=o, in_=i, func=AF.Identity, scale=s),
                              waits=w0, inc=None)
                        p.add("scalar", lambda e, o=ln_tmp[:, 0:1], i=ln_ra2[:, 4 * tt:4 * tt + 1]:
                              e.activation(out=o, in_=i, func=AF.Square), inc=None)
                        p.add("scalar", lambda e, o=ln_tmp[:, 1:2], i=ln_tmp[:, 0:1],
                              s=csts[:, 3:4], bi=csts[:, 0:1]:
                              e.activation(out=o, in_=i, func=AF.Identity, scale=s, bias=bi), inc=None)
                        p.add("scalar", lambda e, o=ln_tmp[:, 0:1], i=ln_rb2[:, 2 * tt + 1:2 * tt + 2],
                              s=csts[:, 2:3], bi=ln_tmp[:, 1:2]:
                              e.activation(out=o, in_=i, func=AF.Ln, scale=s, bias=bi), inc=None)
                        p.add("scalar", lambda e, o=ln_ra2[:, 4 * tt + 2:4 * tt + 3], i=ln_tmp[:, 0:1],
                              s=csts[:, 1:2]:
                              e.activation(out=o, in_=i, func=AF.Exp, scale=s), inc=None)
                        ca = p.add("scalar", lambda e, o=ln_ra2[:, 4 * tt + 3:4 * tt + 4],
                                   i=ln_ra2[:, 4 * tt + 2:4 * tt + 3], s=ln_ra2[:, 4 * tt:4 * tt + 1]:
                                   e.activation(out=o, in_=i, func=AF.Identity, scale=s),
                                   inc="s_act" if tt == 1 else None)
                    if tt == 1:
                        last_use["ln_rb2"] = [("s_act", ca)]
                        last_use["xh2d"] = [("s_act", ca)]
                return f

            def ln2T(tt):
                def f():
                    c_ph = p.add("scalar", lambda e, o=xh2[:], i=x1_sb[:, tt * D:(tt + 1) * D],
                                 s=ln_ra2[:, 4 * tt + 2:4 * tt + 3],
                                 bi=ln_ra2[:, 4 * tt + 3:4 * tt + 4]:
                                 e.activation(out=o, in_=i, func=AF.Identity, scale=s, bias=bi),
                                 waits=lu("xh2"), inc="s_act")
                    tr2 = []
                    for b in range(2):
                        bk = banks[2 + b]
                        for q in range(4):
                            dk = b * 4 + q
                            cpe = p.add("tensor", lambda e, o=bk[:, q * 128:(q + 1) * 128],
                                        i=xh2[:, dk * 128:(dk + 1) * 128], idn=ident_f[:]:
                                        e.transpose(o, i, idn),
                                        waits=([("s_act", c_ph)] + lu(("bank", 2 + b))) if q == 0 else [],
                                        inc="s_pe" if q == 3 else None)
                        tr2.append(("s_pe", cpe))
                        ca2 = p.add("scalar", lambda e,
                                    o=xh2T[:].rearrange("p (dk t) -> p dk t", dk=KDM)[:, b * 4:(b + 1) * 4, tt * 128:(tt + 1) * 128],
                                    i=bk[:]:
                                    e.activation(out=o, in_=i, func=AF.Copy),
                                    waits=[("s_pe", cpe)], inc="s_act")
                        last_use[("bank", 2 + b)] = [("s_act", ca2)]
                    last_use["xh2"] = tr2
                    if tt == 1:
                        ctx["c_xh2T"] = p.c("s_act")
                return f

            def ffn_m(m):
                def f():
                    wm = wcm_m[m % 2]
                    cw = p.dma("sync", wm[:], wcm_sc[m],
                               waits=c_wm_w + lu(("wcm_m", m % 2)), inc="d_w")
                    bi = 4 + (m % 2)
                    pf = banks[bi][:, 0:TC]
                    cpm = None
                    for k in range(KDM):
                        w0 = ([cw, ("s_act", ctx["c_xh2T"])] + lu(("bank", bi))) if k == 0 else []
                        cpm = p.add("tensor", lambda e, o=pf, w=wm[:, k * 128:(k + 1) * 128],
                                    r=xh2T[:, k * TC:(k + 1) * TC], s=(k == 0), st=(k == KDM - 1):
                                    e.matmul(o, w, r, start=s, stop=st),
                                    waits=w0, inc="s_pe" if k == KDM - 1 else None)
                    last_use[("wcm_m", m % 2)] = [("s_pe", cpm)]
                    ca = p.add("scalar", lambda e, o=ff_f[m % 2][:], i=pf, bi2=bias_sb[:, m:m + 1]:
                               e.activation(out=o, in_=i, func=AF.Identity, bias=bi2),
                               waits=[("s_pe", cpm)] + lu(("ff_f", m % 2)), inc="s_act")
                    last_use[("bank", bi)] = [("s_act", ca)]
                    ctx[("ff_evac", m)] = ca
                return f

            def x2_pair(pr, sc=sc, s0=s0):
                def f():
                    tb = 2 + pr % 2
                    cpe = None
                    for j in range(2):
                        m = 2 * pr + j
                        for tt in range(2):
                            q = j * 2 + tt
                            cpe = p.add("tensor", lambda e, o=banks[tb][:, q * 128:(q + 1) * 128],
                                        i=ff_f[m % 2][:, tt * 128:(tt + 1) * 128], idn=ident_f[:]:
                                        e.transpose(o, i, idn),
                                        waits=([("s_act", ctx[("ff_evac", 2 * pr + 1)])] +
                                               lu(("bank", tb))) if q == 0 else [],
                                        inc="s_pe" if (j == 1 and tt == 1) else None)
                    cv = None
                    for j in range(2):
                        m = 2 * pr + j
                        for tt in range(2):
                            q = j * 2 + tt
                            dst = (x2_sb if tt == 0 else xh2d)[:, m * 128:(m + 1) * 128]
                            cv = p.add("vector",
                                       lambda e, o=dst, i0=banks[tb][:, q * 128:(q + 1) * 128],
                                       i1=x1_sb[:, tt * D + m * 128:tt * D + (m + 1) * 128]:
                                       e.tensor_tensor(out=o, in0=i0, in1=i1, op=ALU.add),
                                       waits=(([("s_pe", cpe)] +
                                               (lu("x2_sb") + lu("xh2d") if pr == 0 else []))
                                              if q == 0 else []),
                                       inc="s_dve" if (j == 1 and tt == 1) else None)
                    last_use[("bank", tb)] = [("s_dve", cv)]
                    last_use[("ff_f", 0)] = [("s_pe", cpe)]
                    last_use[("ff_f", 1)] = [("s_pe", cpe)]
                    ctx.setdefault("x2_done", []).append(cv)
                    if pr == 3:
                        cdone = max(ctx["x2_done"])
                        for tt in range(2):
                            buf = x2_sb if tt == 0 else xh2d
                            p.dma("sync", out_h[s0 + tt * 128:s0 + (tt + 1) * 128, :], buf[:],
                                  waits=[("s_dve", cdone)], inc="d_out")
                        last_use["x2_sb"] = [("d_out", p.c("d_out"))]
                        last_use["xh2d"] = [("d_out", p.c("d_out"))]
                        last_use["x1_sb"] = [("s_dve", cdone)]
                return f

            units.append(xr_load())
            for m in range(KDM):
                units.append(attn_m(m))
                if m % 2 == 1:
                    units.append(x1_pair(m // 2))
            units.append(ln2A())
            units.append(ln2RT())
            for tt in range(2):
                units.append(ln2BC(tt))
            for tt in range(2):
                units.append(ln2T(tt))
            for m in range(KDM):
                units.append(ffn_m(m))
                if m % 2 == 1:
                    units.append(x2_pair(m // 2))
            return units

        def scan_chunk(sc, bg):
            cp = sc // 2
            s0 = sc * TC
            HN = N // 2
            cbp = []
            ccp = []
            for h in range(2):
                cbp.append(p.dma("gpsimd",
                                 brep[:, h * HN * TC:(h + 1) * HN * TC].rearrange("p (n t) -> p n t", n=HN),
                                 bc_sc[h * HN:(h + 1) * HN, s0:s0 + TC][None, :, :].broadcast_to([128, HN, TC]),
                                 waits=[c_bc_w[cp]] + lu("brep"), inc="d_brep"))
            for h in range(2):
                ccp.append(p.dma("gpsimd",
                                 crep[:, h * HN * TC:(h + 1) * HN * TC].rearrange("p (n t) -> p n t", n=HN),
                                 bc_sc[N + h * HN:N + (h + 1) * HN, s0:s0 + TC][None, :, :].broadcast_to([128, HN, TC]),
                                 waits=lu("crep"), inc="d_brep"))

            pend = [None]
            loaded = {}

            def load_k(k):
                kb = k % 2
                cdt = p.dma("gpsimd", dt_t[kb][:], dt_sc[k * 128:(k + 1) * 128, s0:s0 + TC],
                            waits=[c_dt_w[(k, cp)]] + lu(("dt_t", kb)), inc="d_scr")
                cxs = p.dma("gpsimd", xs_t[kb][:], xs_sc[k * 128:(k + 1) * 128, s0:s0 + TC],
                            waits=[c_xs_w[(k, cp)]] + lu(("xs_t", kb)), inc="d_scr")
                csz = p.dma("sync", sz_t[kb][:], sz_sc[k * 128:(k + 1) * 128, s0:s0 + TC],
                            waits=[c_sz_w[(k, cp)]] + lu(("sz_t", kb)), inc="d_scr")
                loaded[k] = (cdt, cxs, csz)

            def emit_yevac(rec):
                kk, kkb, cstop, csz_ = rec
                ybank = banks[6] if kk % 2 == 0 else banks[7]
                ykey = ("bank", 6) if kk % 2 == 0 else ("bank", 7)
                cyt = p.add("vector", lambda e, o=y_tot[:], i0=xs_t[kkb][:],
                            s=d_sb[:, kk:kk + 1], i1=ybank[:, 0:TC]:
                            e.scalar_tensor_tensor(out=o, in0=i0, scalar=s, in1=i1,
                                                   op0=ALU.mult, op1=ALU.add),
                            waits=[("s_pe", cstop), csz_],
                            inc="s_dve")
                last_use[ykey] = [("s_dve", cyt)]
                last_use[("xs_t", kkb)] = [("s_dve", cyt)]
                cyg = p.add("vector", lambda e, o=yg[sc % 2][:, kk * TC:(kk + 1) * TC],
                            i0=y_tot[:], i1=sz_t[kkb][:]:
                            e.tensor_tensor(out=o, in0=i0, in1=i1, op=ALU.mult),
                            waits=(lu(("yg", sc % 2)) if kk == 0 else []), inc="s_dve")
                last_use[("sz_t", kkb)] = [("s_dve", cyg)]

            load_k(0)
            load_k(1)
            for k in range(KD):
                kb = k % 2
                cdt, cxs, csz = loaded.pop(k)
                cdx = p.add("gpsimd", lambda e, o=dtx_t[kb][:], i0=dt_t[kb][:], i1=xs_t[kb][:]:
                            e.tensor_tensor(out=o, in0=i0, in1=i1, op=ALU.mult),
                            waits=[cdt, cxs] + lu(("dtx", kb)), inc="s_gp")
                ybank = banks[6] if k % 2 == 0 else banks[7]
                ykey = ("bank", 6) if k % 2 == 0 else ("bank", 7)
                cz = None
                cpe_g = None
                cbb_last = None
                cbb_of = {}

                def emit_bb(g):
                    bbuf = [bb_d, bb_g2, bb_d2][g % 3]
                    bkey = ("bbg", g % 3)
                    cbb = p.add("gpsimd", lambda e, o=bbuf[:],
                                i0=dtx_t[kb][:, None, :].broadcast_to([128, 4, TC]),
                                i1=brep[:].rearrange("p (n t) -> p n t", n=N)[:, g * 4:(g + 1) * 4, :]:
                                e.tensor_tensor(out=o.rearrange("p (n t) -> p n t", n=4), in0=i0, in1=i1, op=ALU.mult),
                                waits=(([cbp[g // 2]] if k <= 1 else []) +
                                       lu(bkey)),
                                inc="s_gp")
                    cbb_of[g] = (cbb, bbuf, bkey)
                    return cbb

                emit_bb(0)
                emit_bb(1)
                for g in range(4):
                    abuf = ab_g[g % 2]
                    caa = None
                    for j in range(4):
                        n = g * 4 + j
                        w0 = []
                        if j == 0:
                            w0 += lu(("ab", g % 2))
                            if g == 0:
                                w0 += [cdt]
                        caa = p.add("scalar", lambda e, o=abuf[:, j * TC:(j + 1) * TC],
                                    i=dt_t[kb][:], s=a_neg[:, k * N + n:k * N + n + 1]:
                                    e.activation(out=o, in_=i, func=AF.Exp, scale=s),
                                    waits=w0, inc="s_act" if j == 3 else None)
                    cbb, bbuf, bkey = cbb_of[g]
                    cbb_last = cbb
                    csn = None
                    for j in range(4):
                        n = g * 4 + j
                        w0 = []
                        if j == 0:
                            w0 = [("s_act", caa), ("s_gp", cbb)] + lu(("h", g))
                        csn = p.add("vector", lambda e, o=h_slab[:, n * TC:(n + 1) * TC],
                                    a=abuf[:, j * TC:(j + 1) * TC], b2=bbuf[:, j * TC:(j + 1) * TC],
                                    ini=state[:, k * N + n:k * N + n + 1]:
                                    e.tensor_tensor_scan(out=o, data0=a, data1=b2, initial=ini,
                                                         op0=ALU.mult, op1=ALU.add),
                                    waits=w0, inc="s_dve" if j == 3 else None)
                    last_use[("ab", g % 2)] = [("s_dve", csn)]
                    last_use[bkey] = [("s_dve", csn)]
                    if g == 0 and pend[0] is not None:
                        emit_yevac(pend[0])
                        pend[0] = None
                        if k + 1 < KD:
                            load_k(k + 1)
                    if g < 2:
                        emit_bb(g + 2)
                    zi = (k * 4 + g) % 3
                    zb = z4[zi]
                    cz = p.add("vector", lambda e, o=zb[:],
                               i0=h_slab[:, (g * 4) * TC:(g * 4 + 4) * TC],
                               i1=crep[:, (g * 4) * TC:(g * 4 + 4) * TC]:
                               e.tensor_tensor(out=o, in0=i0, in1=i1, op=ALU.mult),
                               waits=(([ccp[g // 2]] if k == 0 else []) +
                                      lu(("z4", zi))),
                               inc="s_dve")
                    last_use[("h", g)] = [("s_dve", cz)]
                    for j in range(4):
                        nn = g * 4 + j
                        w0 = [("s_dve", cz)] if j == 0 else []
                        if nn == 0:
                            w0 += lu(ykey)
                        cpe_g = p.add("tensor", lambda e, o=ybank[:, 0:TC], idn=ident_b[:],
                                      r=zb[:, j * TC:(j + 1) * TC],
                                      s=(nn == 0), st=(nn == N - 1):
                                      e.matmul(o, idn, r, start=s, stop=st),
                                      waits=w0, inc="s_pe" if j == 3 else None)
                    last_use[("z4", zi)] = [("s_pe", cpe_g)]
                    rem = (KD - k) * 4 - g - 1
                    if bg and rem > 0:
                        nbg = (len(bg) + rem - 1) // rem
                        for _ in range(min(nbg, len(bg))):
                            bg.pop(0)()
                cst = p.add("vector", lambda e, o=state[:, k * N:(k + 1) * N],
                            i=h_slab[:].rearrange("p (n t) -> p n t", n=N)[:, :, TC - 1]:
                            e.tensor_copy(o, i), inc="s_dve")
                last_use[("dt_t", kb)] = [("s_act", caa)]
                last_use[("dtx", kb)] = [("s_gp", cbb_last)]
                pend[0] = (k, kb, cpe_g, csz)
            emit_yevac(pend[0])
            pend[0] = None
            c_yg[sc] = p.c("s_dve")
            last_use["brep"] = [("s_gp", p.c("s_gp"))]
            last_use["crep"] = [("s_dve", p.c("s_dve"))]

        for u in stage1_units(0):
            u()
        p.mark("stage1_0_done")
        bg = []
        s1u = []
        for sc in range(nsc):
            cp = sc // 2
            if sc > 0:
                bg += post_units(sc - 1)
            if sc % 2 == 0 and cp + 1 < nmc:
                s1u = stage1_units(cp + 1)
            if s1u:
                take = (len(s1u) + 1) // 2 if sc % 2 == 0 else len(s1u)
                bg += s1u[:take]
                s1u = s1u[take:]
            scan_chunk(sc, bg)
            p.mark(f"scan_{sc}_done")
        for u in bg:
            u()
        for u in post_units(nsc - 1):
            u()
        p.mark("tail_done")

        if TRUNCATE is None:
            p.add("sync", lambda e: e.nop(), waits=[("d_out", p.c("d_out"))])

        with nc.Block() as block:
            p.emit(block, sems)

    return nc


_NC_CACHE = None


def kernel(L0_attn_x, L0_attn_W_in, L0_attn_A_log, L0_attn_W_B, L0_attn_W_C,
           L0_attn_W_dt, L0_attn_b_dt, L0_attn_D, L0_attn_output_gate,
           L0_attn_W_out, L0_ffn_cond, L0_ffn_mask, L0_ffn_W_shared,
           L0_ffn_W_expert, L0_ffn_W_gate, L0_ffn_bias):
    global _NC_CACHE
    if _NC_CACHE is None:
        _NC_CACHE = build_nc()
    nc = _NC_CACHE

    f32 = np.float32
    cont = np.ascontiguousarray
    winT = cont(L0_attn_W_in.T.astype(f32, copy=False))
    wdtT = cont(L0_attn_W_dt.T.astype(f32, copy=False))
    wbcT = cont(np.concatenate([L0_attn_W_B, L0_attn_W_C], axis=0).T.astype(f32, copy=False))
    woutT = cont(L0_attn_W_out.T.astype(f32, copy=False))
    wsT = cont(L0_ffn_W_shared.T.astype(f32, copy=False))
    weT = cont(L0_ffn_W_expert.T.astype(f32, copy=False))

    common = dict(winT=winT, wdtT=wdtT, wbcT=wbcT, woutT=woutT, wsT=wsT, weT=weT,
                  alog=cont(L0_attn_A_log.astype(f32, copy=False)),
                  bdt=cont(L0_attn_b_dt.astype(f32, copy=False)),
                  dvec=cont(L0_attn_D.astype(f32, copy=False)),
                  bias=cont(L0_ffn_bias.astype(f32, copy=False)),
                  gate=cont(L0_attn_output_gate.astype(f32, copy=False)).reshape(1, 1),
                  consts=np.tile(np.array([1e-5, -0.5, 1.0 / D, -1.0, 1.0, -1.0 / D, 0.0, 0.0],
                                          dtype=f32), (128, 1)),
                  identf=np.eye(128, dtype=f32),
                  identb=np.eye(128, dtype=ml_dtypes.bfloat16))
    in_maps = []
    for i in range(8):
        m = dict(common)
        m["x"] = cont(L0_attn_x[i].astype(f32, copy=False))
        in_maps.append(m)

    res = run_bass_kernel_spmd(nc, in_maps, core_ids=list(range(8)))
    return np.stack([res.results[i]["out"] for i in range(8)], axis=0)


# version 18
# speedup vs baseline: 1.0461x; 1.0079x over previous
import numpy as np
import ml_dtypes
import contextlib
import concourse.bass as bass
import concourse.mybir as mybir
from concourse.bass_utils import run_bass_kernel_spmd

F32 = mybir.dt.float32
BF16 = mybir.dt.bfloat16
AF = mybir.ActivationFunctionType
ALU = mybir.AluOpType
AX = mybir.AxisListType

T, D = 2048, 1024
E = 2048
N = 16
MC = 512
NMC = T // MC
TC = 256
NSC = T // TC
KD = E // 128
KDM = D // 128

TRUNCATE = None
DEBUG_SCRATCH = False
MARKS = {}

ENG_SEM = {"scalar": "s_act", "vector": "s_dve", "tensor": "s_pe", "gpsimd": "s_gp"}


DMA_ROT = {"d_x": 1, "d_w": 3, "d_scr": 3, "d_brep": 4, "d_wr": 2, "d_out": 1}


def dma_sem_names():
    out = []
    for base, n in DMA_ROT.items():
        if n == 1:
            out.append(base)
        else:
            out.extend(f"{base}{i}" for i in range(n))
    return out


class Prog:

    def __init__(self):
        self.ops = {k: [] for k in ("sync", "scalar", "vector", "tensor", "gpsimd")}
        self.cnt = {}
        self.gidx = 0
        self.rrc = {}

    def add(self, eng, fn, waits=(), inc=None, inc_k=1):
        self.gidx += 1
        new = None
        if inc is not None:
            self.cnt[inc] = self.cnt.get(inc, 0) + inc_k
            new = self.cnt[inc]
        if TRUNCATE is not None and self.gidx > TRUNCATE:
            return new
        own = ENG_SEM.get(eng)
        for (sem, val) in waits:
            if val and val > 0 and sem != own:
                self.ops[eng].append(("wait", sem, val))
        self.ops[eng].append(("op", fn, inc, new, inc_k))
        return new

    def dma(self, eng, out, in_, waits=(), inc=None, slow=False):
        def fn(e, out=out, in_=in_, slow=slow):
            if slow:
                return e.dma_start(out=out, in_=in_, allow_slow_non_contiguous=True)
            return e.dma_start(out=out, in_=in_)
        waits = list(waits)
        n = DMA_ROT.get(inc, 1)
        if n > 1:
            idx = self.rrc.get(inc, 0)
            self.rrc[inc] = (idx + 1) % n
            sem = f"{inc}{idx}"
        else:
            sem = inc
        prev = self.c(sem)
        if prev:
            waits.append((sem, prev))
        val = self.add(eng, fn, waits=waits, inc=sem, inc_k=16)
        return (sem, val)

    def c(self, sem):
        return self.cnt.get(sem, 0)

    def cs(self, base):
        n = DMA_ROT.get(base, 1)
        names = [base] if n == 1 else [f"{base}{i}" for i in range(n)]
        return [(s, self.cnt[s]) for s in names if self.cnt.get(s, 0) > 0]

    def mark(self, name):
        MARKS[name] = self.gidx

    def emit(self, block, sems):
        waited = {}
        for eng, items in self.ops.items():
            for it in items:
                if it[0] == "wait":
                    waited.setdefault(it[1], set()).add(it[2])
        rank = {}
        for sem, vals in waited.items():
            if not sem.startswith("d_"):
                rank[sem] = {v: i + 1 for i, v in enumerate(sorted(vals))}

        def mk(eng_name):
            def body(engobj):
                for item in self.ops[eng_name]:
                    if item[0] == "wait":
                        _, sem, val = item
                        if sem in rank:
                            val = rank[sem][val]
                        engobj.wait_ge(sems[sem], val)
                    else:
                        _, fn, inc, val, inc_k = item
                        instr = fn(engobj)
                        if inc is not None:
                            if inc.startswith("d_"):
                                instr.then_inc(sems[inc], inc_k)
                            elif val in rank.get(inc, ()):
                                instr.then_inc(sems[inc], inc_k)
            return body
        block.sync(mk("sync"))
        block.scalar(mk("scalar"))
        block.vector(mk("vector"))
        block.tensor(mk("tensor"))
        block.gpsimd(mk("gpsimd"))


def build_nc():
    nmc = T // MC
    nsc = T // TC
    nc = bass.Bass("TRN2", target_bir_lowering=False, detect_race_conditions=False)

    x_h = nc.dram_tensor("x", [T, D], F32, kind="ExternalInput").ap()
    winT_h = nc.dram_tensor("winT", [D, 2 * E], F32, kind="ExternalInput").ap()
    wdtT_h = nc.dram_tensor("wdtT", [E, E], F32, kind="ExternalInput").ap()
    wbcT_h = nc.dram_tensor("wbcT", [E, 2 * N], F32, kind="ExternalInput").ap()
    woutT_h = nc.dram_tensor("woutT", [E, D], F32, kind="ExternalInput").ap()
    wsT_h = nc.dram_tensor("wsT", [D, D], F32, kind="ExternalInput").ap()
    weT_h = nc.dram_tensor("weT", [D, D], F32, kind="ExternalInput").ap()
    alog_h = nc.dram_tensor("alog", [E, N], F32, kind="ExternalInput").ap()
    bdt_h = nc.dram_tensor("bdt", [E], F32, kind="ExternalInput").ap()
    dvec_h = nc.dram_tensor("dvec", [E], F32, kind="ExternalInput").ap()
    bias_h = nc.dram_tensor("bias", [D], F32, kind="ExternalInput").ap()
    gate_h = nc.dram_tensor("gate", [1, 1], F32, kind="ExternalInput").ap()
    consts_h = nc.dram_tensor("consts", [128, 8], F32, kind="ExternalInput").ap()
    identf_h = nc.dram_tensor("identf", [128, 128], F32, kind="ExternalInput").ap()
    identb_h = nc.dram_tensor("identb", [128, 128], BF16, kind="ExternalInput").ap()

    out_h = nc.dram_tensor("out", [T, D], F32, kind="ExternalOutput").ap()

    knd = "ExternalOutput" if DEBUG_SCRATCH else "Internal"
    xs_sc = nc.dram_tensor("xs_sc", [E, T], F32, kind=knd).ap()
    dt_sc = nc.dram_tensor("dt_sc", [E, T], F32, kind=knd).ap()
    sz_sc = nc.dram_tensor("sz_sc", [E, T], BF16, kind=knd).ap()
    bc_sc = nc.dram_tensor("bc_sc", [2 * N, T], F32, kind=knd).ap()
    sgate_sc = nc.dram_tensor("sgate_sc", [1, 1], F32, kind="Internal").ap()
    lnsc = nc.dram_tensor("lnsc", [128, 8], F32, kind="Internal").ap()
    lnsc2 = nc.dram_tensor("lnsc2", [128, 4], F32, kind="Internal").ap()
    winb_sc = nc.dram_tensor("winb_sc", [2 * KD, 128, KDM * 128], BF16, kind="Internal").ap()
    wdtb_sc = nc.dram_tensor("wdtb_sc", [KD, 128, KD * 128], BF16, kind="Internal").ap()
    woutm_sc = nc.dram_tensor("woutm_sc", [KDM, 128, KD * 128], BF16, kind=knd).ap()
    wcm_sc = nc.dram_tensor("wcm_sc", [KDM, 128, KDM * 128], BF16, kind=knd).ap()

    nc.reset()

    p = Prog()
    sem_names = dma_sem_names() + ["s_act", "s_dve", "s_pe", "s_gp"]

    stack = contextlib.ExitStack()
    sbuf = lambda name, shape, dt: stack.enter_context(nc.sbuf_tensor(name, shape, dt)).ap()

    with stack:
        a_neg = sbuf("a_neg", [128, KD * N], F32)
        bdt_sb = sbuf("bdt_sb", [128, KD], F32)
        d_sb = sbuf("d_sb", [128, KD], F32)
        bias_sb = sbuf("bias_sb", [128, KDM], F32)
        ident_f = sbuf("ident_f", [128, 128], F32)
        ident_b = sbuf("ident_b", [128, 128], BF16)
        g_sb = sbuf("g_sb", [1, 4], F32)
        csts = sbuf("cst", [128, 8], F32)
        ln_tmp = sbuf("ln_tmp", [128, 2], F32)
        ln_rb = sbuf("ln_rb", [128, 8], F32)
        ln_ra = sbuf("ln_ra", [128, 16], F32)
        ln_rb2 = sbuf("ln_rb2", [128, 4], F32)
        ln_ra2 = sbuf("ln_ra2", [128, 8], F32)
        sbc_sb = sbuf("sbc_sb", [128, 1], F32)

        wbc_bf = sbuf("wbc_bf", [128, KD * 2 * N], BF16)
        wstage = sbuf("wstage", [128, KD * 128], F32)
        wbf = [sbuf(f"wbf{i}", [128, KD * 128], BF16) for i in range(2)]
        wout_m = [sbuf(f"woutm{i}", [128, KD * 128], BF16) for i in range(2)]
        wcm_m = [sbuf(f"wcmm{i}", [128, KDM * 128], BF16) for i in range(2)]

        x_sb = sbuf("x_sb", [128, D], F32)
        xc_sb = sbuf("xc_sb", [128, D], F32)
        xhat = sbuf("xhat", [128, D], F32)
        xhatT = sbuf("xhatT", [128, KDM * MC], BF16)
        xsbf = sbuf("xsbf", [128, KD * MC], BF16)
        szb = [sbuf(f"szb{i}", [128, MC], BF16) for i in range(2)]
        bc_f = sbuf("bc_f", [32, MC], F32)
        ln_r = sbuf("ln_r", [128, 16], F32)
        ln_r2 = sbuf("ln_r2", [128, 8], F32)

        brep = sbuf("brep", [128, N * TC], F32)
        crep = sbuf("crep", [128, N * TC], F32)
        dt_t = [sbuf(f"dt_t{i}", [128, TC], F32) for i in range(2)]
        xs_t = [sbuf(f"xs_t{i}", [128, TC], F32) for i in range(2)]
        sz_t = [sbuf(f"sz_t{i}", [128, TC], BF16) for i in range(2)]
        dtx_t0 = sbuf("dtx0", [128, TC], F32)
        dtx_t1 = sbuf("dtx1", [128, TC], F32)
        ab_g = [sbuf(f"ab{i}", [128, 4 * TC], F32) for i in range(3)]
        xs_f = [sbuf(f"xs_f{i}", [128, MC], F32) for i in range(2)]
        bb_g2 = sbuf("bb_g2", [128, 4 * TC], F32)
        h_slab = sbuf("h_slab", [128, N * TC], F32)
        z4 = [sbuf(f"z4{i}", [128, 4 * TC], BF16) for i in range(3)]
        y_tot = sbuf("y_tot", [128, TC], F32)
        yg = [sbuf(f"yg{i}", [128, KD * TC], BF16) for i in range(2)]
        state = sbuf("state", [128, KD * N], F32)

        at_f = [sbuf(f"at_f{i}", [128, TC], F32) for i in range(2)]
        x1_sb = sbuf("x1_sb", [128, 2 * D], F32)
        xr_sb = [sbuf(f"xr_sb{i}", [128, D], F32) for i in range(2)]
        xh2 = sbuf("xh2", [128, D], F32)
        xh2d = sbuf("xh2d", [128, D], F32)
        xh2T = sbuf("xh2T", [128, KDM * TC], BF16)
        ff_f = [sbuf(f"ff_f{i}", [128, TC], F32) for i in range(2)]
        x2_sb = sbuf("x2_sb", [128, D], F32)

        bb_d = wstage[:, 0:4 * TC]
        bb_d2 = wstage[:, 4 * TC:8 * TC]
        dtx_t = [dtx_t0, dtx_t1]

        banks = [stack.enter_context(nc.psum_tensor(f"ps{i}", [128, 512], F32)).ap()
                 for i in range(8)]

        sems = {}
        for s in sem_names:
            sems[s] = stack.enter_context(nc.semaphore(s))

        last_use = {}
        def lu(key):
            return list(last_use.get(key, []))

        p.add("vector", lambda e, o=state[:]: e.memset(o, 0.0), inc="s_dve")

        for apx in (ab_g[0][:], ab_g[1][:], ab_g[2][:], xhat[:], xc_sb[:],
                    xh2[:], xh2d[:], xhatT[:], xh2T[:], xs_f[0][:], xs_f[1][:],
                    szb[0][:], szb[1][:], bc_f[:],
                    at_f[0][:], at_f[1][:], ff_f[0][:], ff_f[1][:], a_neg[:],
                    ln_tmp[:], ln_ra[:], ln_ra2[:], g_sb[0:1, 1:4]):
            p.add("scalar", lambda e, o=apx: e.memzero(o), inc=None)
        for apx in (wbf[0], wbf[1], wbc_bf, h_slab, x1_sb, x2_sb, y_tot,
                    ln_r, ln_r2, z4[0], z4[1], z4[2], yg[0], yg[1]):
            p.add("vector", lambda e, o=apx[:]: e.memset(o, 0.0), inc=None)
        for apx in (dtx_t0, dtx_t1, bb_g2):
            p.add("gpsimd", lambda e, o=apx[:]: e.memset(o, 0.0), inc=None)
        p.add("scalar", lambda e, o=xsbf[:]: e.memzero(o), inc=None)
        cms = None
        for i in range(8):
            cms = p.add("vector", lambda e, o=banks[i][:]: e.memset(o, 0.0),
                        inc="s_dve" if i == 7 else None)
        for i in range(8):
            last_use[("bank", i)] = [("s_dve", cms)]

        p.dma("sync", ident_f[:], identf_h, inc="d_w")
        p.dma("sync", ident_b[:], identb_h, inc="d_w")
        p.dma("sync", csts[:], consts_h, inc="d_w")
        p.dma("sync", bdt_sb[:], bdt_h.rearrange("(k p) -> p k", p=128), inc="d_w", slow=True)
        p.dma("sync", d_sb[:], dvec_h.rearrange("(k p) -> p k", p=128), inc="d_w", slow=True)
        p.dma("sync", bias_sb[:], bias_h.rearrange("(k p) -> p k", p=128), inc="d_w", slow=True)
        p.dma("sync", g_sb[0:1, 0:1], gate_h, inc="d_w")
        c_w0 = p.cs("d_w")
        ca0 = p.dma("sync", wstage[:, 0:KD * N].rearrange("p (k n) -> p k n", k=KD),
                    alog_h.rearrange("(k p) n -> p k n", p=128), inc="d_w")
        ca1 = p.add("scalar", lambda e, o=a_neg[:], i=wstage[:, 0:KD * N]:
                    e.activation(out=o, in_=i, func=AF.Exp),
                    waits=[ca0], inc="s_act")
        last_use["wstage"] = [("s_act", ca1)]
        p.add("vector", lambda e, o=a_neg[:], i=a_neg[:]: e.tensor_scalar_mul(o, i, -1.0),
              waits=[("s_act", ca1)], inc="s_dve")
        cga = p.add("scalar", lambda e, o=g_sb[0:1, 1:2], i=g_sb[0:1, 0:1]:
                    e.activation(out=o, in_=i, func=AF.Sigmoid),
                    waits=c_w0, inc="s_act")
        csg = p.dma("sync", sgate_sc, g_sb[0:1, 1:2], waits=[("s_act", cga)], inc="d_wr")
        c_sbc = p.dma("sync", sbc_sb[:], sgate_sc.broadcast_to([128, 1]),
                      waits=[csg], inc="d_w")

        for k in range(KDM):
            c1 = p.dma("sync", wstage[:, 0:D], wsT_h[k * 128:(k + 1) * 128, :],
                       waits=lu("wstage"), inc="d_w")
            cw = p.dma("sync", wstage[:, D:2 * D], weT_h[k * 128:(k + 1) * 128, :], inc="d_w")
            cv = p.add("vector", lambda e, o=wbf[k % 2][:, 0:D], i0=wstage[:, 0:D],
                       i1=wstage[:, D:2 * D]: e.tensor_tensor(out=o, in0=i0, in1=i1, op=ALU.add),
                       waits=[c1, cw] + lu(("wbf", k % 2)), inc="s_dve")
            last_use["wstage"] = [("s_dve", cv)]
            cd = p.dma("sync", wcm_sc[:, :, k * 128:(k + 1) * 128].rearrange("m p j -> p m j"),
                       wbf[k % 2][:, 0:D].rearrange("p (m j) -> p m j", m=KDM),
                       waits=[("s_dve", cv)], inc="d_wr", slow=True)
            last_use[("wbf", k % 2)] = [cd]
        cw = p.dma("sync", wstage[:, 0:KD * 32].rearrange("p (k m) -> p k m", k=KD),
                   wbcT_h.rearrange("(k p) m -> p k m", p=128),
                   waits=lu("wstage"), inc="d_w")
        cv = p.add("vector", lambda e, o=wbc_bf[:], i=wstage[:, 0:KD * 32]: e.tensor_copy(o, i),
                   waits=[cw], inc="s_dve")
        last_use["wstage"] = [("s_dve", cv)]
        for k in range(KD):
            cw = p.dma("sync", wstage[:, 0:D], woutT_h[k * 128:(k + 1) * 128, :],
                       waits=lu("wstage"), inc="d_w")
            cv = p.add("vector", lambda e, o=wbf[k % 2][:, 0:D], i=wstage[:, 0:D], s=sbc_sb[:, 0:1]:
                       e.tensor_scalar(out=o, in0=i, scalar1=s, scalar2=None, op0=ALU.mult),
                       waits=[cw, c_sbc] + lu(("wbf", k % 2)), inc="s_dve")
            last_use["wstage"] = [("s_dve", cv)]
            cd = p.dma("sync", woutm_sc[:, :, k * 128:(k + 1) * 128].rearrange("m p j -> p m j"),
                       wbf[k % 2][:, 0:D].rearrange("p (m j) -> p m j", m=KDM),
                       waits=[("s_dve", cv)], inc="d_wr", slow=True)
            last_use[("wbf", k % 2)] = [cd]
        c_wm_w = p.cs("d_wr")
        c_wcast_w = [0]

        c_xs_w = {}
        c_dt_w = {}
        c_sz_w = {}
        c_bc_w = {}
        c_yg = {}

        def stage1_units(cp):
            units = []
            t0 = cp * MC
            ctx = {}

            def lnA(tt, t0=t0):
                def f():
                    r0 = t0 + tt * 128
                    sqb = xc_sb if tt % 2 == 0 else xhat
                    sqk = "xc_sb" if tt % 2 == 0 else "xhat"
                    cx = p.dma("sync", x_sb[:], x_h[r0:r0 + 128, :],
                               waits=lu("x_sb"), inc="d_x")
                    p.add("vector", lambda e, o=ln_r[:, 2 * tt:2 * tt + 1], i=x_sb[:]:
                          e.reduce_sum(o, i, axis=AX.X),
                          waits=[cx] + (lu("lnr1") if tt == 0 else []),
                          inc="s_dve")
                    c_sq = p.add("scalar", lambda e, o=sqb[:], i=x_sb[:]:
                                 e.activation(out=o, in_=i, func=AF.Square),
                                 waits=[cx] + lu(sqk), inc="s_act")
                    cr2 = p.add("vector", lambda e, o=ln_r[:, 2 * tt + 1:2 * tt + 2], i=sqb[:]:
                                e.reduce_sum(o, i, axis=AX.X),
                                waits=[("s_act", c_sq)], inc="s_dve")
                    ctx["c_sums"] = cr2
                    last_use[sqk] = [("s_dve", cr2)]
                    last_use["x_sb"] = [("s_dve", cr2), ("s_act", c_sq)]
                return f

            def lnRT():
                def f():
                    cd1 = p.dma("sync", lnsc, ln_r[:, 0:8], waits=[("s_dve", ctx["c_sums"])],
                                inc="d_wr")
                    cd2 = p.dma("sync", ln_rb[:], lnsc,
                                waits=[cd1] + lu("ln_rb"), inc="d_w")
                    ctx["c_lnrt"] = cd2
                    last_use["lnr1"] = [cd1]
                return f

            def lnB(tt):
                def f():
                    if True:
                        w0 = [ctx["c_lnrt"]] if tt == 0 else []
                        p.add("scalar", lambda e, o=ln_ra[:, 4 * tt:4 * tt + 1],
                              i=ln_rb[:, 2 * tt:2 * tt + 1], s=csts[:, 5:6]:
                              e.activation(out=o, in_=i, func=AF.Identity, scale=s),
                              waits=w0, inc=None)
                        p.add("scalar", lambda e, o=ln_tmp[:, 0:1], i=ln_ra[:, 4 * tt:4 * tt + 1]:
                              e.activation(out=o, in_=i, func=AF.Square), inc=None)
                        p.add("scalar", lambda e, o=ln_tmp[:, 1:2], i=ln_tmp[:, 0:1],
                              s=csts[:, 3:4], bi=csts[:, 0:1]:
                              e.activation(out=o, in_=i, func=AF.Identity, scale=s, bias=bi), inc=None)
                        p.add("scalar", lambda e, o=ln_tmp[:, 0:1], i=ln_rb[:, 2 * tt + 1:2 * tt + 2],
                              s=csts[:, 2:3], bi=ln_tmp[:, 1:2]:
                              e.activation(out=o, in_=i, func=AF.Ln, scale=s, bias=bi), inc=None)
                        p.add("scalar", lambda e, o=ln_ra[:, 4 * tt + 2:4 * tt + 3], i=ln_tmp[:, 0:1],
                              s=csts[:, 1:2]:
                              e.activation(out=o, in_=i, func=AF.Exp, scale=s), inc=None)
                        ca = p.add("scalar", lambda e, o=ln_ra[:, 4 * tt + 3:4 * tt + 4],
                                   i=ln_ra[:, 4 * tt + 2:4 * tt + 3], s=ln_ra[:, 4 * tt:4 * tt + 1]:
                                   e.activation(out=o, in_=i, func=AF.Identity, scale=s),
                                   inc="s_act" if tt == 3 else None)
                    if tt == 3:
                        last_use["ln_rb"] = [("s_act", ca)]
                        last_use["xc_sb"] = [("s_act", ca)]
                        last_use["x_sb"] = [("s_act", ca)]
                return f

            def lnC(tt, t0=t0):
                def f():
                    r0 = t0 + tt * 128
                    cx = p.dma("sync", x_sb[:], x_h[r0:r0 + 128, :],
                               waits=lu("x_sb"), inc="d_x")
                    c_ph = p.add("scalar", lambda e, o=xhat[:], i=x_sb[:],
                                 s=ln_ra[:, 4 * tt + 2:4 * tt + 3],
                                 bi=ln_ra[:, 4 * tt + 3:4 * tt + 4]:
                                 e.activation(out=o, in_=i, func=AF.Identity, scale=s, bias=bi),
                                 waits=[cx] + lu("xhat"),
                                 inc="s_act")
                    last_use["x_sb"] = [("s_act", c_ph)]
                    xh_done = []
                    for b in range(2):
                        bk = banks[2 + b]
                        for q in range(4):
                            dk = b * 4 + q
                            cpe = p.add("tensor", lambda e, o=bk[:, q * 128:(q + 1) * 128],
                                        i=xhat[:, dk * 128:(dk + 1) * 128], idn=ident_f[:]:
                                        e.transpose(o, i, idn),
                                        waits=([("s_act", c_ph)] + lu(("bank", 2 + b))) if q == 0 else [],
                                        inc="s_pe" if q == 3 else None)
                        xh_done.append(("s_pe", cpe))
                        ca2 = p.add("scalar", lambda e,
                                    o=xhatT[:].rearrange("p (dk t) -> p dk t", dk=KDM)[:, b * 4:(b + 1) * 4, tt * 128:(tt + 1) * 128],
                                    i=bk[:]:
                                    e.activation(out=o, in_=i, func=AF.Copy),
                                    waits=[("s_pe", cpe)], inc="s_act")
                        last_use[("bank", 2 + b)] = [("s_act", ca2)]
                    last_use["xhat"] = xh_done
                    if tt == 3:
                        ctx["c_xhatT"] = p.c("s_act")
                return f

            def xz(eo, t0=t0, cp=cp):
                def f():
                    wb = wbf[eo % 2]
                    if cp == 0:
                        cw = p.dma("sync", wstage[:, 0:KDM * 128].rearrange("p (k m) -> p k m", k=KDM),
                                   winT_h.rearrange("(k p) m -> p k m", p=128)[:, :, eo * 128:(eo + 1) * 128],
                                   waits=lu("wstage"), inc="d_w")
                        cv = p.add("vector", lambda e, o=wb[:, 0:KDM * 128], i=wstage[:, 0:KDM * 128]:
                                   e.tensor_copy(o, i),
                                   waits=[cw] + lu(("wbf", eo % 2)), inc="s_dve")
                        last_use["wstage"] = [("s_dve", cv)]
                        p.dma("sync", winb_sc[eo], wb[:, 0:KDM * 128], waits=[("s_dve", cv)], inc="d_wr")
                        rdy = [("s_dve", cv)]
                    else:
                        cw = p.dma("sync", wb[:, 0:KDM * 128], winb_sc[eo],
                                   waits=c_wcast_w[0] + lu(("wbf", eo % 2)), inc="d_w")
                        rdy = [cw]
                    bi = eo % 2
                    bank = banks[bi]
                    cpe = None
                    for k in range(KDM):
                        w0 = (rdy + [("s_act", ctx["c_xhatT"])] + lu(("bank", bi))) if k == 0 else []
                        cpe = p.add("tensor", lambda e, o=bank[:], w=wb[:, k * 128:(k + 1) * 128],
                                    r=xhatT[:, k * MC:(k + 1) * MC], s=(k == 0), st=(k == KDM - 1):
                                    e.matmul(o, w, r, start=s, stop=st),
                                    waits=w0, inc="s_pe" if k == KDM - 1 else None)
                    last_use[("wbf", eo % 2)] = [("s_pe", cpe)]
                    if eo < KD:
                        fb = xs_f[eo % 2]
                        ca = p.add("scalar", lambda e, o=fb[:], i=bank[:]:
                                   e.activation(out=o, in_=i, func=AF.Copy),
                                   waits=[("s_pe", cpe)] + lu(("xs_f", eo % 2)), inc="s_act")
                        cv = p.add("scalar", lambda e, o=xsbf[:, eo * MC:(eo + 1) * MC], i=bank[:]:
                                   e.activation(out=o, in_=i, func=AF.Copy),
                                   waits=(lu("xsbf") if eo == 0 else []), inc="s_act")
                        cd = p.dma("sync", xs_sc[eo * 128:(eo + 1) * 128, t0:t0 + MC], fb[:],
                                   waits=[("s_act", ca)], inc="d_wr")
                        c_xs_w[(eo, cp)] = cd
                        last_use[("xs_f", eo % 2)] = [("s_act", ca), cd]
                        last_use[("bank", bi)] = [("s_act", cv)]
                        if eo == KD - 1:
                            ctx["c_xsbf_part"] = cv
                return f

            def xz_z(eo0, t0=t0, cp=cp):
                def f():
                    cpes = []
                    for eo in (eo0, eo0 + 1):
                        wb = wbf[eo % 2]
                        if cp == 0:
                            cw = p.dma("sync", wstage[:, 0:KDM * 128].rearrange("p (k m) -> p k m", k=KDM),
                                       winT_h.rearrange("(k p) m -> p k m", p=128)[:, :, eo * 128:(eo + 1) * 128],
                                       waits=lu("wstage"), inc="d_w")
                            cv = p.add("vector", lambda e, o=wb[:, 0:KDM * 128], i=wstage[:, 0:KDM * 128]:
                                       e.tensor_copy(o, i),
                                       waits=[cw] + lu(("wbf", eo % 2)), inc="s_dve")
                            last_use["wstage"] = [("s_dve", cv)]
                            p.dma("sync", winb_sc[eo], wb[:, 0:KDM * 128], waits=[("s_dve", cv)], inc="d_wr")
                            rdy = [("s_dve", cv)]
                        else:
                            cw = p.dma("sync", wb[:, 0:KDM * 128], winb_sc[eo],
                                       waits=c_wcast_w[0] + lu(("wbf", eo % 2)), inc="d_w")
                            rdy = [cw]
                        bi = eo % 2
                        bank = banks[bi]
                        cpe = None
                        for k in range(KDM):
                            w0 = (rdy + [("s_act", ctx["c_xhatT"])] + lu(("bank", bi))) if k == 0 else []
                            cpe = p.add("tensor", lambda e, o=bank[:], w=wb[:, k * 128:(k + 1) * 128],
                                        r=xhatT[:, k * MC:(k + 1) * MC], s=(k == 0), st=(k == KDM - 1):
                                        e.matmul(o, w, r, start=s, stop=st),
                                        waits=w0, inc="s_pe" if k == KDM - 1 else None)
                        last_use[("wbf", eo % 2)] = [("s_pe", cpe)]
                        cpes.append(cpe)
                    for i, eo in enumerate((eo0, eo0 + 1)):
                        bi = eo % 2
                        zb = szb[eo % 2]
                        ca = p.add("scalar", lambda e, o=zb[:], i=banks[bi][:]:
                                   e.activation(out=o, in_=i, func=AF.Silu),
                                   waits=[("s_pe", cpes[i])] + lu(("szb", eo % 2)), inc="s_act")
                        cd = p.dma("sync", sz_sc[(eo - KD) * 128:(eo - KD + 1) * 128, t0:t0 + MC], zb[:],
                                   waits=[("s_act", ca)], inc="d_wr")
                        c_sz_w[(eo - KD, cp)] = cd
                        last_use[("szb", eo % 2)] = [cd]
                        last_use[("bank", bi)] = [("s_act", ca)]
                    if eo0 + 1 == 2 * KD - 1:
                        ctx["c_xsbf"] = ctx["c_xsbf_part"]
                return f

            def dtm(eo, t0=t0, cp=cp):
                def f():
                    wb = wbf[eo % 2]
                    if cp == 0:
                        cw = p.dma("sync", wstage[:].rearrange("p (k m) -> p k m", k=KD),
                                   wdtT_h.rearrange("(k p) m -> p k m", p=128)[:, :, eo * 128:(eo + 1) * 128],
                                   waits=lu("wstage"), inc="d_w")
                        cv = p.add("vector", lambda e, o=wb[:], i=wstage[:]: e.tensor_copy(o, i),
                                   waits=[cw] + lu(("wbf", eo % 2)), inc="s_dve")
                        last_use["wstage"] = [("s_dve", cv)]
                        p.dma("sync", wdtb_sc[eo], wb[:], waits=[("s_dve", cv)], inc="d_wr")
                        rdy = [("s_dve", cv)]
                    else:
                        cw = p.dma("sync", wb[:], wdtb_sc[eo],
                                   waits=c_wcast_w[0] + lu(("wbf", eo % 2)), inc="d_w")
                        rdy = [cw]
                    bi = eo % 2
                    bank = banks[bi]
                    cpe = None
                    for k in range(KD):
                        w0 = (rdy + [("s_act", ctx["c_xsbf"])] + lu(("bank", bi))) if k == 0 else []
                        cpe = p.add("tensor", lambda e, o=bank[:], w=wb[:, k * 128:(k + 1) * 128],
                                    r=xsbf[:, k * MC:(k + 1) * MC], s=(k == 0), st=(k == KD - 1):
                                    e.matmul(o, w, r, start=s, stop=st),
                                    waits=w0, inc="s_pe" if k == KD - 1 else None)
                    last_use[("wbf", eo % 2)] = [("s_pe", cpe)]
                    p.add("scalar", lambda e, o=xc_sb[:, 0:MC], i=bank[:], bi2=bdt_sb[:, eo:eo + 1]:
                          e.activation(out=o, in_=i, func=AF.Exp, bias=bi2),
                          waits=[("s_pe", cpe)] + lu("xc_sb"), inc="s_act")
                    ca = p.add("scalar", lambda e, o=xc_sb[:, MC:2 * MC], i=xc_sb[:, 0:MC],
                               bi2=csts[:, 4:5]:
                               e.activation(out=o, in_=i, func=AF.Ln, bias=bi2),
                               waits=lu("xcdt"), inc="s_act")
                    cd = p.dma("sync", dt_sc[eo * 128:(eo + 1) * 128, t0:t0 + MC],
                               xc_sb[:, MC:2 * MC],
                               waits=[("s_act", ca)], inc="d_wr")
                    c_dt_w[(eo, cp)] = cd
                    last_use["xcdt"] = [cd]
                    last_use["xc_sb"] = [("s_act", ca), cd]
                    last_use[("bank", bi)] = [("s_act", ca)]
                    if cp == 0 and eo == KD - 1:
                        c_wcast_w[0] = p.cs("d_wr")
                return f

            def bcm(t0=t0, cp=cp):
                def f():
                    bank = banks[0]
                    cpe = None
                    for k in range(KD):
                        w0 = lu(("bank", 0)) if k == 0 else []
                        cpe = p.add("tensor", lambda e, o=bank[0:32, :], w=wbc_bf[:, k * 32:(k + 1) * 32],
                                    r=xsbf[:, k * MC:(k + 1) * MC], s=(k == 0), st=(k == KD - 1):
                                    e.matmul(o, w, r, start=s, stop=st),
                                    waits=w0, inc="s_pe" if k == KD - 1 else None)
                    last_use["xsbf"] = [("s_pe", cpe)]
                    ca = p.add("scalar", lambda e, o=bc_f[:], i=bank[0:32, :]:
                               e.activation(out=o, in_=i, func=AF.Copy),
                               waits=[("s_pe", cpe)] + lu("bc_f"), inc="s_act")
                    cd = p.dma("sync", bc_sc[:, t0:t0 + MC], bc_f[:], waits=[("s_act", ca)], inc="d_wr")
                    last_use["bc_f"] = [cd]
                    last_use[("bank", 0)] = [("s_act", ca)]
                    c_bc_w[cp] = cd
                return f

            for tt in range(4):
                units.append(lnA(tt))
            units.append(lnRT())
            for tt in range(4):
                units.append(lnB(tt))
            for tt in range(4):
                units.append(lnC(tt))
            for eo in range(KD):
                units.append(xz(eo))
            for eo in range(KD, 2 * KD, 2):
                units.append(xz_z(eo))
            for eo in range(KD):
                units.append(dtm(eo))
            units.append(bcm())
            return units

        def post_units(sc):
            units = []
            s0 = sc * TC
            ctx = {}

            def attn_m(m, sc=sc, s0=s0):
                def f():
                    wm = wout_m[m % 2]
                    cw = p.dma("sync", wm[:], woutm_sc[m],
                               waits=c_wm_w + lu(("wout_m", m % 2)), inc="d_w")
                    bi = 4 + (m % 2)
                    pa = banks[bi][:, 0:TC]
                    cpm = None
                    for k in range(KD):
                        w0 = ([cw, ("s_dve", c_yg[sc])] + lu(("bank", bi))) if k == 0 else []
                        cpm = p.add("tensor", lambda e, o=pa, w=wm[:, k * 128:(k + 1) * 128],
                                    r=yg[sc % 2][:, k * TC:(k + 1) * TC], s=(k == 0), st=(k == KD - 1):
                                    e.matmul(o, w, r, start=s, stop=st),
                                    waits=w0, inc="s_pe" if k == KD - 1 else None)
                    last_use[("wout_m", m % 2)] = [("s_pe", cpm)]
                    ca = p.add("scalar", lambda e, o=at_f[m % 2][:], i=pa:
                               e.activation(out=o, in_=i, func=AF.Copy),
                               waits=[("s_pe", cpm)] + lu(("at_f", m % 2)), inc="s_act")
                    last_use[("bank", bi)] = [("s_act", ca)]
                    ctx[("at_evac", m)] = ca
                    if m == KDM - 1:
                        last_use[("yg", sc % 2)] = [("s_pe", cpm)]
                return f

            def xr_load(sc=sc, s0=s0):
                def f():
                    ctx["cx0"] = p.dma("sync", xr_sb[0][:], x_h[s0:s0 + 128, :],
                                       waits=lu(("xr", 0)), inc="d_x")
                    ctx["cx1"] = p.dma("sync", xr_sb[1][:], x_h[s0 + 128:s0 + 256, :],
                                       waits=lu(("xr", 1)), inc="d_x")
                return f

            def x1_pair(pr, sc=sc):
                def f():
                    tb = 2 + pr % 2
                    cpe = None
                    for j in range(2):
                        m = 2 * pr + j
                        for tt in range(2):
                            q = j * 2 + tt
                            cpe = p.add("tensor", lambda e, o=banks[tb][:, q * 128:(q + 1) * 128],
                                        i=at_f[m % 2][:, tt * 128:(tt + 1) * 128], idn=ident_f[:]:
                                        e.transpose(o, i, idn),
                                        waits=([("s_act", ctx[("at_evac", 2 * pr + 1)])] +
                                               lu(("bank", tb))) if q == 0 else [],
                                        inc="s_pe" if (j == 1 and tt == 1) else None)
                    cv = None
                    for j in range(2):
                        m = 2 * pr + j
                        for tt in range(2):
                            q = j * 2 + tt
                            cv = p.add("vector",
                                       lambda e, o=x1_sb[:, tt * D + m * 128:tt * D + (m + 1) * 128],
                                       i0=banks[tb][:, q * 128:(q + 1) * 128],
                                       i1=xr_sb[tt][:, m * 128:(m + 1) * 128]:
                                       e.tensor_tensor(out=o, in0=i0, in1=i1, op=ALU.add),
                                       waits=(([("s_pe", cpe), ctx["cx0"], ctx["cx1"]] +
                                               (lu("x1_sb") if pr == 0 else [])) if q == 0 else []),
                                       inc="s_dve" if (j == 1 and tt == 1) else None)
                    last_use[("bank", tb)] = [("s_dve", cv)]
                    last_use[("at_f", 0)] = [("s_pe", cpe)]
                    last_use[("at_f", 1)] = [("s_pe", cpe)]
                    ctx.setdefault("x1_done", []).append(cv)
                    if pr == 3:
                        ctx["c_x1"] = max(ctx["x1_done"])
                        last_use[("xr", 0)] = [("s_dve", ctx["c_x1"])]
                        last_use[("xr", 1)] = [("s_dve", ctx["c_x1"])]
                return f

            def ln2A():
                def f():
                    for tt in range(2):
                        sqb = xh2d if tt == 0 else xh2
                        sqk = "xh2d" if tt == 0 else "xh2"
                        p.add("vector", lambda e, o=ln_r2[:, 2 * tt:2 * tt + 1],
                              i=x1_sb[:, tt * D:(tt + 1) * D]:
                              e.reduce_sum(o, i, axis=AX.X),
                              waits=([("s_dve", ctx["c_x1"])] if tt == 0 else []) +
                                    (lu("lnr2") if tt == 0 else []),
                              inc="s_dve")
                        csq2 = p.add("scalar", lambda e, o=sqb[:], i=x1_sb[:, tt * D:(tt + 1) * D]:
                                     e.activation(out=o, in_=i, func=AF.Square),
                                     waits=([("s_dve", ctx["c_x1"])] if tt == 0 else []) + lu(sqk),
                                     inc="s_act")
                        cr2 = p.add("vector", lambda e, o=ln_r2[:, 2 * tt + 1:2 * tt + 2], i=sqb[:]:
                                    e.reduce_sum(o, i, axis=AX.X),
                                    waits=[("s_act", csq2)], inc="s_dve")
                        ctx["c_sums2"] = cr2
                        last_use[sqk] = [("s_dve", cr2)]
                return f

            def ln2RT():
                def f():
                    cd1 = p.dma("sync", lnsc2, ln_r2[:, 0:4], waits=[("s_dve", ctx["c_sums2"])],
                                inc="d_wr")
                    cd2 = p.dma("sync", ln_rb2[:], lnsc2,
                                waits=[cd1] + lu("ln_rb2"), inc="d_w")
                    ctx["c_lnrt2"] = cd2
                    last_use["lnr2"] = [cd1]
                return f

            def ln2BC(tt):
                def f():
                    if True:
                        w0 = [ctx["c_lnrt2"]] if tt == 0 else []
                        p.add("scalar", lambda e, o=ln_ra2[:, 4 * tt:4 * tt + 1],
                              i=ln_rb2[:, 2 * tt:2 * tt + 1], s=csts[:, 5:6]:
                              e.activation(out=o, in_=i, func=AF.Identity, scale=s),
                              waits=w0, inc=None)
                        p.add("scalar", lambda e, o=ln_tmp[:, 0:1], i=ln_ra2[:, 4 * tt:4 * tt + 1]:
                              e.activation(out=o, in_=i, func=AF.Square), inc=None)
                        p.add("scalar", lambda e, o=ln_tmp[:, 1:2], i=ln_tmp[:, 0:1],
                              s=csts[:, 3:4], bi=csts[:, 0:1]:
                              e.activation(out=o, in_=i, func=AF.Identity, scale=s, bias=bi), inc=None)
                        p.add("scalar", lambda e, o=ln_tmp[:, 0:1], i=ln_rb2[:, 2 * tt + 1:2 * tt + 2],
                              s=csts[:, 2:3], bi=ln_tmp[:, 1:2]:
                              e.activation(out=o, in_=i, func=AF.Ln, scale=s, bias=bi), inc=None)
                        p.add("scalar", lambda e, o=ln_ra2[:, 4 * tt + 2:4 * tt + 3], i=ln_tmp[:, 0:1],
                              s=csts[:, 1:2]:
                              e.activation(out=o, in_=i, func=AF.Exp, scale=s), inc=None)
                        ca = p.add("scalar", lambda e, o=ln_ra2[:, 4 * tt + 3:4 * tt + 4],
                                   i=ln_ra2[:, 4 * tt + 2:4 * tt + 3], s=ln_ra2[:, 4 * tt:4 * tt + 1]:
                                   e.activation(out=o, in_=i, func=AF.Identity, scale=s),
                                   inc="s_act" if tt == 1 else None)
                    if tt == 1:
                        last_use["ln_rb2"] = [("s_act", ca)]
                        last_use["xh2d"] = [("s_act", ca)]
                return f

            def ln2T(tt):
                def f():
                    c_ph = p.add("scalar", lambda e, o=xh2[:], i=x1_sb[:, tt * D:(tt + 1) * D],
                                 s=ln_ra2[:, 4 * tt + 2:4 * tt + 3],
                                 bi=ln_ra2[:, 4 * tt + 3:4 * tt + 4]:
                                 e.activation(out=o, in_=i, func=AF.Identity, scale=s, bias=bi),
                                 waits=lu("xh2"), inc="s_act")
                    tr2 = []
                    for b in range(2):
                        bk = banks[2 + b]
                        for q in range(4):
                            dk = b * 4 + q
                            cpe = p.add("tensor", lambda e, o=bk[:, q * 128:(q + 1) * 128],
                                        i=xh2[:, dk * 128:(dk + 1) * 128], idn=ident_f[:]:
                                        e.transpose(o, i, idn),
                                        waits=([("s_act", c_ph)] + lu(("bank", 2 + b))) if q == 0 else [],
                                        inc="s_pe" if q == 3 else None)
                        tr2.append(("s_pe", cpe))
                        ca2 = p.add("scalar", lambda e,
                                    o=xh2T[:].rearrange("p (dk t) -> p dk t", dk=KDM)[:, b * 4:(b + 1) * 4, tt * 128:(tt + 1) * 128],
                                    i=bk[:]:
                                    e.activation(out=o, in_=i, func=AF.Copy),
                                    waits=[("s_pe", cpe)], inc="s_act")
                        last_use[("bank", 2 + b)] = [("s_act", ca2)]
                    last_use["xh2"] = tr2
                    if tt == 1:
                        ctx["c_xh2T"] = p.c("s_act")
                return f

            def ffn_m(m):
                def f():
                    wm = wcm_m[m % 2]
                    cw = p.dma("sync", wm[:], wcm_sc[m],
                               waits=c_wm_w + lu(("wcm_m", m % 2)), inc="d_w")
                    bi = 4 + (m % 2)
                    pf = banks[bi][:, 0:TC]
                    cpm = None
                    for k in range(KDM):
                        w0 = ([cw, ("s_act", ctx["c_xh2T"])] + lu(("bank", bi))) if k == 0 else []
                        cpm = p.add("tensor", lambda e, o=pf, w=wm[:, k * 128:(k + 1) * 128],
                                    r=xh2T[:, k * TC:(k + 1) * TC], s=(k == 0), st=(k == KDM - 1):
                                    e.matmul(o, w, r, start=s, stop=st),
                                    waits=w0, inc="s_pe" if k == KDM - 1 else None)
                    last_use[("wcm_m", m % 2)] = [("s_pe", cpm)]
                    ca = p.add("scalar", lambda e, o=ff_f[m % 2][:], i=pf, bi2=bias_sb[:, m:m + 1]:
                               e.activation(out=o, in_=i, func=AF.Identity, bias=bi2),
                               waits=[("s_pe", cpm)] + lu(("ff_f", m % 2)), inc="s_act")
                    last_use[("bank", bi)] = [("s_act", ca)]
                    ctx[("ff_evac", m)] = ca
                return f

            def x2_pair(pr, sc=sc, s0=s0):
                def f():
                    tb = 2 + pr % 2
                    cpe = None
                    for j in range(2):
                        m = 2 * pr + j
                        for tt in range(2):
                            q = j * 2 + tt
                            cpe = p.add("tensor", lambda e, o=banks[tb][:, q * 128:(q + 1) * 128],
                                        i=ff_f[m % 2][:, tt * 128:(tt + 1) * 128], idn=ident_f[:]:
                                        e.transpose(o, i, idn),
                                        waits=([("s_act", ctx[("ff_evac", 2 * pr + 1)])] +
                                               lu(("bank", tb))) if q == 0 else [],
                                        inc="s_pe" if (j == 1 and tt == 1) else None)
                    cv = None
                    for j in range(2):
                        m = 2 * pr + j
                        for tt in range(2):
                            q = j * 2 + tt
                            dst = (x2_sb if tt == 0 else xh2d)[:, m * 128:(m + 1) * 128]
                            cv = p.add("vector",
                                       lambda e, o=dst, i0=banks[tb][:, q * 128:(q + 1) * 128],
                                       i1=x1_sb[:, tt * D + m * 128:tt * D + (m + 1) * 128]:
                                       e.tensor_tensor(out=o, in0=i0, in1=i1, op=ALU.add),
                                       waits=(([("s_pe", cpe)] +
                                               (lu("x2_sb") + lu("xh2d") if pr == 0 else []))
                                              if q == 0 else []),
                                       inc="s_dve" if (j == 1 and tt == 1) else None)
                    last_use[("bank", tb)] = [("s_dve", cv)]
                    last_use[("ff_f", 0)] = [("s_pe", cpe)]
                    last_use[("ff_f", 1)] = [("s_pe", cpe)]
                    ctx.setdefault("x2_done", []).append(cv)
                    if pr == 3:
                        cdone = max(ctx["x2_done"])
                        for tt in range(2):
                            buf = x2_sb if tt == 0 else xh2d
                            p.dma("sync", out_h[s0 + tt * 128:s0 + (tt + 1) * 128, :], buf[:],
                                  waits=[("s_dve", cdone)], inc="d_out")
                        last_use["x2_sb"] = [("d_out", p.c("d_out"))]
                        last_use["xh2d"] = [("d_out", p.c("d_out"))]
                        last_use["x1_sb"] = [("s_dve", cdone)]
                return f

            units.append(xr_load())
            for m in range(KDM):
                units.append(attn_m(m))
                if m % 2 == 1:
                    units.append(x1_pair(m // 2))
            units.append(ln2A())
            units.append(ln2RT())
            for tt in range(2):
                units.append(ln2BC(tt))
            for tt in range(2):
                units.append(ln2T(tt))
            for m in range(KDM):
                units.append(ffn_m(m))
                if m % 2 == 1:
                    units.append(x2_pair(m // 2))
            return units

        def scan_chunk(sc, bg):
            cp = sc // 2
            s0 = sc * TC
            HN = N // 2
            cbp = []
            ccp = []
            for h in range(2):
                cbp.append(p.dma("gpsimd",
                                 brep[:, h * HN * TC:(h + 1) * HN * TC].rearrange("p (n t) -> p n t", n=HN),
                                 bc_sc[h * HN:(h + 1) * HN, s0:s0 + TC][None, :, :].broadcast_to([128, HN, TC]),
                                 waits=[c_bc_w[cp]] + lu("brep"), inc="d_brep"))
            for h in range(2):
                ccp.append(p.dma("gpsimd",
                                 crep[:, h * HN * TC:(h + 1) * HN * TC].rearrange("p (n t) -> p n t", n=HN),
                                 bc_sc[N + h * HN:N + (h + 1) * HN, s0:s0 + TC][None, :, :].broadcast_to([128, HN, TC]),
                                 waits=lu("crep"), inc="d_brep"))

            pend = [None]
            loaded = {}

            def load_k(k):
                kb = k % 2
                cdt = p.dma("gpsimd", dt_t[kb][:], dt_sc[k * 128:(k + 1) * 128, s0:s0 + TC],
                            waits=[c_dt_w[(k, cp)]] + lu(("dt_t", kb)), inc="d_scr")
                cxs = p.dma("gpsimd", xs_t[kb][:], xs_sc[k * 128:(k + 1) * 128, s0:s0 + TC],
                            waits=[c_xs_w[(k, cp)]] + lu(("xs_t", kb)), inc="d_scr")
                csz = p.dma("sync", sz_t[kb][:], sz_sc[k * 128:(k + 1) * 128, s0:s0 + TC],
                            waits=[c_sz_w[(k, cp)]] + lu(("sz_t", kb)), inc="d_scr")
                loaded[k] = (cdt, cxs, csz)

            def emit_yevac(rec):
                kk, kkb, cstop, csz_ = rec
                ybank = banks[6] if kk % 2 == 0 else banks[7]
                ykey = ("bank", 6) if kk % 2 == 0 else ("bank", 7)
                cyt = p.add("vector", lambda e, o=y_tot[:], i0=xs_t[kkb][:],
                            s=d_sb[:, kk:kk + 1], i1=ybank[:, 0:TC]:
                            e.scalar_tensor_tensor(out=o, in0=i0, scalar=s, in1=i1,
                                                   op0=ALU.mult, op1=ALU.add),
                            waits=[("s_pe", cstop), csz_],
                            inc="s_dve")
                last_use[ykey] = [("s_dve", cyt)]
                last_use[("xs_t", kkb)] = [("s_dve", cyt)]
                cyg = p.add("vector", lambda e, o=yg[sc % 2][:, kk * TC:(kk + 1) * TC],
                            i0=y_tot[:], i1=sz_t[kkb][:]:
                            e.tensor_tensor(out=o, in0=i0, in1=i1, op=ALU.mult),
                            waits=(lu(("yg", sc % 2)) if kk == 0 else []), inc="s_dve")
                last_use[("sz_t", kkb)] = [("s_dve", cyg)]

            load_k(0)
            load_k(1)
            for k in range(KD):
                kb = k % 2
                cdt, cxs, csz = loaded.pop(k)
                cdx = p.add("gpsimd", lambda e, o=dtx_t[kb][:], i0=dt_t[kb][:], i1=xs_t[kb][:]:
                            e.tensor_tensor(out=o, in0=i0, in1=i1, op=ALU.mult),
                            waits=[cdt, cxs] + lu(("dtx", kb)), inc="s_gp")
                ybank = banks[6] if k % 2 == 0 else banks[7]
                ykey = ("bank", 6) if k % 2 == 0 else ("bank", 7)
                cz = None
                cpe_g = None
                cbb_last = None
                cbb_of = {}

                def emit_bb(g):
                    bbuf = [bb_d, bb_g2, bb_d2][g % 3]
                    bkey = ("bbg", g % 3)
                    cbb = p.add("gpsimd", lambda e, o=bbuf[:],
                                i0=dtx_t[kb][:, None, :].broadcast_to([128, 4, TC]),
                                i1=brep[:].rearrange("p (n t) -> p n t", n=N)[:, g * 4:(g + 1) * 4, :]:
                                e.tensor_tensor(out=o.rearrange("p (n t) -> p n t", n=4), in0=i0, in1=i1, op=ALU.mult),
                                waits=(([cbp[g // 2]] if k <= 1 else []) +
                                       lu(bkey)),
                                inc="s_gp")
                    cbb_of[g] = (cbb, bbuf, bkey)
                    return cbb

                emit_bb(0)
                emit_bb(1)
                for g in range(4):
                    abuf = ab_g[g % 3]
                    caa = None
                    for j in range(4):
                        n = g * 4 + j
                        w0 = []
                        if j == 0:
                            w0 += lu(("ab", g % 3))
                            if g == 0:
                                w0 += [cdt]
                        caa = p.add("scalar", lambda e, o=abuf[:, j * TC:(j + 1) * TC],
                                    i=dt_t[kb][:], s=a_neg[:, k * N + n:k * N + n + 1]:
                                    e.activation(out=o, in_=i, func=AF.Exp, scale=s),
                                    waits=w0, inc="s_act" if j == 3 else None)
                    cbb, bbuf, bkey = cbb_of[g]
                    cbb_last = cbb
                    csn = None
                    for j in range(4):
                        n = g * 4 + j
                        w0 = []
                        if j == 0:
                            w0 = [("s_act", caa), ("s_gp", cbb)] + lu(("h", g))
                        csn = p.add("vector", lambda e, o=h_slab[:, n * TC:(n + 1) * TC],
                                    a=abuf[:, j * TC:(j + 1) * TC], b2=bbuf[:, j * TC:(j + 1) * TC],
                                    ini=state[:, k * N + n:k * N + n + 1]:
                                    e.tensor_tensor_scan(out=o, data0=a, data1=b2, initial=ini,
                                                         op0=ALU.mult, op1=ALU.add),
                                    waits=w0, inc="s_dve" if j == 3 else None)
                    last_use[("ab", g % 3)] = [("s_dve", csn)]
                    last_use[bkey] = [("s_dve", csn)]
                    if g == 0 and pend[0] is not None:
                        emit_yevac(pend[0])
                        pend[0] = None
                        if k + 1 < KD:
                            load_k(k + 1)
                    if g < 2:
                        emit_bb(g + 2)
                    zi = (k * 4 + g) % 3
                    zb = z4[zi]
                    cz = p.add("vector", lambda e, o=zb[:],
                               i0=h_slab[:, (g * 4) * TC:(g * 4 + 4) * TC],
                               i1=crep[:, (g * 4) * TC:(g * 4 + 4) * TC]:
                               e.tensor_tensor(out=o, in0=i0, in1=i1, op=ALU.mult),
                               waits=(([ccp[g // 2]] if k == 0 else []) +
                                      lu(("z4", zi))),
                               inc="s_dve")
                    last_use[("h", g)] = [("s_dve", cz)]
                    for j in range(4):
                        nn = g * 4 + j
                        w0 = [("s_dve", cz)] if j == 0 else []
                        if nn == 0:
                            w0 += lu(ykey)
                        cpe_g = p.add("tensor", lambda e, o=ybank[:, 0:TC], idn=ident_b[:],
                                      r=zb[:, j * TC:(j + 1) * TC],
                                      s=(nn == 0), st=(nn == N - 1):
                                      e.matmul(o, idn, r, start=s, stop=st),
                                      waits=w0, inc="s_pe" if j == 3 else None)
                    last_use[("z4", zi)] = [("s_pe", cpe_g)]
                    rem = (KD - k) * 4 - g - 1
                    if bg and rem > 0:
                        nbg = (len(bg) + rem - 1) // rem
                        for _ in range(min(nbg, len(bg))):
                            bg.pop(0)()
                cst = p.add("vector", lambda e, o=state[:, k * N:(k + 1) * N],
                            i=h_slab[:].rearrange("p (n t) -> p n t", n=N)[:, :, TC - 1]:
                            e.tensor_copy(o, i), inc="s_dve")
                last_use[("dt_t", kb)] = [("s_act", caa)]
                last_use[("dtx", kb)] = [("s_gp", cbb_last)]
                pend[0] = (k, kb, cpe_g, csz)
            emit_yevac(pend[0])
            pend[0] = None
            c_yg[sc] = p.c("s_dve")
            last_use["brep"] = [("s_gp", p.c("s_gp"))]
            last_use["crep"] = [("s_dve", p.c("s_dve"))]

        for u in stage1_units(0):
            u()
        p.mark("stage1_0_done")
        bg = []
        s1u = []
        for sc in range(nsc):
            cp = sc // 2
            if sc > 0:
                bg += post_units(sc - 1)
            if sc % 2 == 0 and cp + 1 < nmc:
                s1u = stage1_units(cp + 1)
            if s1u:
                take = (len(s1u) + 1) // 2 if sc % 2 == 0 else len(s1u)
                bg += s1u[:take]
                s1u = s1u[take:]
            scan_chunk(sc, bg)
            p.mark(f"scan_{sc}_done")
        for u in bg:
            u()
        for u in post_units(nsc - 1):
            u()
        p.mark("tail_done")

        if TRUNCATE is None:
            p.add("sync", lambda e: e.nop(), waits=[("d_out", p.c("d_out"))])

        with nc.Block() as block:
            p.emit(block, sems)

    return nc


_NC_CACHE = None


def kernel(L0_attn_x, L0_attn_W_in, L0_attn_A_log, L0_attn_W_B, L0_attn_W_C,
           L0_attn_W_dt, L0_attn_b_dt, L0_attn_D, L0_attn_output_gate,
           L0_attn_W_out, L0_ffn_cond, L0_ffn_mask, L0_ffn_W_shared,
           L0_ffn_W_expert, L0_ffn_W_gate, L0_ffn_bias):
    global _NC_CACHE
    if _NC_CACHE is None:
        _NC_CACHE = build_nc()
    nc = _NC_CACHE

    f32 = np.float32
    cont = np.ascontiguousarray
    winT = cont(L0_attn_W_in.T.astype(f32, copy=False))
    wdtT = cont(L0_attn_W_dt.T.astype(f32, copy=False))
    wbcT = cont(np.concatenate([L0_attn_W_B, L0_attn_W_C], axis=0).T.astype(f32, copy=False))
    woutT = cont(L0_attn_W_out.T.astype(f32, copy=False))
    wsT = cont(L0_ffn_W_shared.T.astype(f32, copy=False))
    weT = cont(L0_ffn_W_expert.T.astype(f32, copy=False))

    common = dict(winT=winT, wdtT=wdtT, wbcT=wbcT, woutT=woutT, wsT=wsT, weT=weT,
                  alog=cont(L0_attn_A_log.astype(f32, copy=False)),
                  bdt=cont(L0_attn_b_dt.astype(f32, copy=False)),
                  dvec=cont(L0_attn_D.astype(f32, copy=False)),
                  bias=cont(L0_ffn_bias.astype(f32, copy=False)),
                  gate=cont(L0_attn_output_gate.astype(f32, copy=False)).reshape(1, 1),
                  consts=np.tile(np.array([1e-5, -0.5, 1.0 / D, -1.0, 1.0, -1.0 / D, 0.0, 0.0],
                                          dtype=f32), (128, 1)),
                  identf=np.eye(128, dtype=f32),
                  identb=np.eye(128, dtype=ml_dtypes.bfloat16))
    in_maps = []
    for i in range(8):
        m = dict(common)
        m["x"] = cont(L0_attn_x[i].astype(f32, copy=False))
        in_maps.append(m)

    res = run_bass_kernel_spmd(nc, in_maps, core_ids=list(range(8)))
    return np.stack([res.results[i]["out"] for i in range(8)], axis=0)
